# version 11
# baseline (speedup 1.0000x reference)
import numpy as np
import ml_dtypes
import concourse.bass as bass
import concourse.mybir as mybir

F32 = mybir.dt.float32
BF16 = mybir.dt.bfloat16
AF = mybir.ActivationFunctionType
ALU = mybir.AluOpType
AX = mybir.AxisListType
AP = bass.AP

T, D, HF, E, NH, DH = 2048, 1024, 2048, 8, 16, 64
NT, ND, NHT = T // 128, D // 128, HF // 128
TC = 512
NTC = T // TC
EPS = 1e-5
EXPSHIFT = -20.0
BK = 512


def bc(ap, n):
    return AP(ap.tensor, ap.offset, [list(d) for d in ap.ap] + [[0, n]])


class SemC:
    __slots__ = ("h", "val")

    def __init__(self, h):
        self.h = h
        self.val = 0


class Plan:
    ENGS = ("tensor", "vector", "scalar", "sync", "gpsimd")

    def __init__(self, nc):
        self.nc = nc
        self.ops = {e: [] for e in self.ENGS}
        self.waited = {e: {} for e in self.ENGS}
        self.bank_free = {}

    def sem(self, name):
        return SemC(self.nc.alloc_semaphore(name))

    def wait(self, eng, s, v):
        if s is None or v is None or v <= 0:
            return
        if self.waited[eng].get(s.h.num, -1) >= v:
            return
        self.waited[eng][s.h.num] = v
        self.ops[eng].append(lambda e, h=s.h, v=v: e.wait_ge(h, v))

    def op(self, eng, build, sem=None, waits=(), dma=False):
        for w in waits:
            if w is None:
                continue
            self.wait(eng, w[0], w[1])
        if sem is not None:
            inc = 16 if dma else 1
            sem.val += inc
            self.ops[eng].append(lambda e, b=build, h=sem.h, i=inc: b(e).then_inc(h, i))
            return (sem, sem.val)
        self.ops[eng].append(build)
        return None

    def wait_toks(self, eng, toks):
        if toks is None:
            return
        if isinstance(toks, tuple):
            toks = [toks]
        for t in toks:
            if t is not None:
                self.wait(eng, t[0], t[1])

    def use_bank(self, eng, b):
        w = self.bank_free.get(b)
        if w is not None:
            self.wait(eng, w[0], w[1])

    def free_bank(self, b, tok):
        self.bank_free[b] = tok

    def finish(self):
        with self.nc.allow_low_precision("bf16 attn l-reciprocal (within tolerance)"), \
             self.nc.Block() as block:
            for eng in self.ENGS:
                def runner(e, eng=eng):
                    for f in self.ops[eng]:
                        f(e)
                getattr(block, eng)(runner)


def build(cfg):
    nc = bass.Bass("TRN2", target_bir_lowering=False, debug=False)

    x_in = nc.dram_tensor("x_in", [T, D], F32, kind="ExternalInput")
    w_qkv = nc.dram_tensor("w_qkv", [D, 3 * D], F32, kind="ExternalInput")
    w_proj = nc.dram_tensor("w_proj", [D, D], F32, kind="ExternalInput")
    b_proj = nc.dram_tensor("b_proj", [D], F32, kind="ExternalInput")
    W1 = nc.dram_tensor("W1", [E, HF, D], F32, kind="ExternalInput")
    b1 = nc.dram_tensor("b1", [E, HF], F32, kind="ExternalInput")
    W2 = nc.dram_tensor("W2", [E, D, HF], F32, kind="ExternalInput")
    b2 = nc.dram_tensor("b2", [E, D], F32, kind="ExternalInput")
    Wg = nc.dram_tensor("Wg", [E, D], F32, kind="ExternalInput")
    bg = nc.dram_tensor("bg", [E], F32, kind="ExternalInput")
    coss = nc.dram_tensor("coss", [4, 128, T], F32, kind="ExternalInput")
    ident_in = nc.dram_tensor("ident_in", [128, 128], F32, kind="ExternalInput")
    ones_in = nc.dram_tensor("ones_in", [128, 128], BF16, kind="ExternalInput")
    out = nc.dram_tensor("out", [T, D], F32, kind="ExternalOutput")

    W1s = nc.dram_tensor("W1s", [E, ND, HF, 128], BF16, kind="Internal")
    W2s = nc.dram_tensor("W2s", [E, NHT, D, 128], BF16, kind="Internal")
    qkvs = nc.dram_tensor("qkvs", [D, 3 * D], BF16, kind="Internal")
    wprojs = nc.dram_tensor("wprojs", [D, D], BF16, kind="Internal")

    AR = 212736
    arena = nc.alloc_sbuf_tensor("arena", [128, AR], mybir.dt.uint8)
    abase = nc.lookup_mloc(arena).addr

    def sb(name, shape, dtype, off):
        nbytes = int(np.prod(shape[1:])) * mybir.dt.size(dtype)
        assert off + nbytes <= AR, f"{name}: {off}+{nbytes} > {AR}"
        return nc.alloc_sbuf_tensor_at(name, shape, dtype, offset=abase + off)

    x_sb = sb("x_sb", [128, NT * D], F32, 0)
    M = 65536
    def psb(name, shape, dtype):
        nonlocal M
        t = sb(name, shape, dtype, M)
        nbytes = int(np.prod(shape[1:])) * mybir.dt.size(dtype)
        M += (nbytes + 31) // 32 * 32
        return t
    b1sb = psb("b1sb", [128, E * NHT], F32)
    comb = psb("comb", [128, NT * E], F32)
    b2bf = psb("b2bf", [8, D], BF16)
    bgbf = psb("bgbf", [1, E], BF16)
    wgtsb = psb("wgtsb", [128, ND * E], BF16)
    ident = psb("ident", [128, 128], F32)
    ones = psb("ones", [128, 128], BF16)
    stat = psb("stat", [128, 8 * NT], F32)
    rlf = psb("rlf", [1, 512], F32)
    rbsb = psb("rbsb", [64, 512], F32)
    onesf = psb("onesf", [1, 64], F32)
    assert M <= 75776, M
    xnT = sb("xnT", [128, ND * T], BF16, 75776)
    U = 108544
    w1t = sb("w1t", [128, ND * HF], BF16, U)
    w2t = sb("w2t", [128, NHT * D], BF16, U + 32768)
    hT = sb("hT", [128, NHT * TC], BF16, U + 65536)
    combT = sb("combT", [8, T], BF16, U + 81920)
    tmpah = [sb(f"tmpa{h}", [128, 512], F32, U + 57600 + h * 4096) for h in range(8)]
    tmpbh = [sb(f"tmpb{h}", [128, 512], F32, U + 59648 + h * 4096) for h in range(8)]
    cosu = sb("cosu", [128, T], BF16, U + 0)
    sinu = sb("sinu", [128, T], BF16, U + 4096)
    at1 = sb("at1", [128, 16 * 256], BF16, U + 8192)
    qrope = [sb(f"qrope{i}", [128, T], BF16, U + 16384 + 8192 * i) for i in range(2)]
    krope = [sb(f"krope{i}", [128, T], BF16, U + 20480 + 8192 * i) for i in range(2)]
    vh = sb("vh", [128, NT * 520], BF16, U + 32768)
    at0 = sb("at0", [128, 16 * 256], BF16, U + 49408)
    bprojbf = sb("bprojbf", [1, D], BF16, U + 49408)
    otall = sb("otall", [128, 8 * T], BF16, U + 57600)
    wsl = sb("wsl", [128, 4 * ND * 128], BF16, U + 90368)
    xnbuf = [sb("xnbuf0", [128, D], F32, U + 81920),
             sb("xnbuf1", [128, D], F32, U + 98560)]
    junk2 = [sb("junk0", [128, D], BF16, U + 92160),
             sb("junk1", [128, D], BF16, U + 96256)]
    wpj = sb("wpj", [128, ND * D], BF16, U + 16384)
    wgf = sb("wgf", [8, D], F32, U + 86016)

    ps = nc.alloc_psum_tensor("ps", [128, 8 * BK], F32)

    def bank(b, lo=0, n=BK, p0=0, pn=128):
        return ps[p0:p0 + pn, b * BK + lo: b * BK + lo + n]

    nc.const_aps.register(F32, 0.125, stat[0:128, 113:114])
    pl = Plan(nc)
    s_x4 = [pl.sem(f"s_x{i}") for i in range(4)]
    s_casm = pl.sem("s_casm")
    s_ca1 = [pl.sem(f"s_ca1_{i}") for i in range(E)]
    s_ca2 = [pl.sem(f"s_ca2_{i}") for i in range(E)]
    s_caq = pl.sem("s_caq")
    s_sml = pl.sem("s_sml")
    s_w1l = pl.sem("s_w1l")
    s_w2l = pl.sem("s_w2l")
    s_xp = [[pl.sem(f"s_xp{q}{p}") for p in range(2)] for q in range(2)]
    s_pe = pl.sem("s_pe")
    s_sc = pl.sem("s_sc")
    s_ve = pl.sem("s_ve")
    s_st = pl.sem("s_st")
    s_cs = pl.sem("s_cs")
    s_vsl = pl.sem("s_vsl")
    s_wsl = [pl.sem("s_wsl0"), pl.sem("s_wsl1")]
    s_wpj = pl.sem("s_wpj")

    tok_x4 = []
    for g in range(4):
        tok_x = pl.op("gpsimd", lambda e, g=g: e.dma_start(
            AP(x_sb, g * 4 * D, [[NT * D, 128], [D, 4], [1, D]]),
            AP(x_in, g * 4 * 128 * D, [[D, 128], [128 * D, 4], [1, D]])),
            sem=s_x4[g], dma=True)
        tok_x4.append(tok_x)
    tok_x = tok_x4[-1]
    tok_x_tiles = [tok_x4[c // 4] for c in range(NT)]

    def slow(e, dst, src):
        return e.dma_start(dst, src, allow_slow_non_contiguous=True)
    tok_b1 = pl.op("sync", lambda e: slow(e,
        AP(b1sb, 0, [[E * NHT, 128], [NHT, E], [1, NHT]]),
        AP(b1, 0, [[1, 128], [HF, E], [128, NHT]])), sem=s_sml, dma=True)
    pl.op("sync", lambda e: e.dma_start(ident[:, :], ident_in.ap()), sem=s_sml, dma=True)
    pl.op("sync", lambda e: e.dma_start(ones[:, :], ones_in.ap()), sem=s_sml, dma=True)
    tok_small = pl.op("sync", lambda e: e.dma_start(wgf[:, :], Wg.ap()), sem=s_sml, dma=True)
    v_small = tok_small[1]

    pl.op("gpsimd", lambda e: e.dma_start(b2bf[:, :], b2.ap()), sem=s_casm, dma=True)
    tok_sc2 = pl.op("gpsimd", lambda e: e.dma_start(
        bgbf[:, :], AP(bg, 0, [[E, 1], [1, E]])), sem=s_casm, dma=True)
    v_smallcast = tok_sc2[1]

    v_wcast1 = [None] * E
    v_wcast2 = [None] * E

    def emit_cast(ee, pace_waits=()):
        tok = None
        for dt in range(ND):
            tok = pl.op("gpsimd", lambda e, ee=ee, dt=dt: e.dma_start(
                AP(W1s, (ee * ND + dt) * HF * 128, [[128, HF], [1, 128]]),
                AP(W1, ee * HF * D + dt * 128, [[D, HF], [1, 128]])),
                sem=s_ca1[ee], dma=True,
                waits=(pace_waits if dt == 0 else []))
        v_wcast1[ee] = tok
        for ht in range(NHT):
            tok = pl.op("gpsimd", lambda e, ee=ee, ht=ht: e.dma_start(
                AP(W2s, (ee * NHT + ht) * D * 128, [[128, D], [1, 128]]),
                AP(W2, ee * D * HF + ht * 128, [[HF, D], [1, 128]])),
                sem=s_ca2[ee], dma=True)
        v_wcast2[ee] = tok

    emit_cast(0)
    emit_cast(1)
    emit_cast(2)

    v_qkvcast = None

    def emit_qkv_casts(pace_waits=()):
        nonlocal v_qkvcast
        pl.op("gpsimd", lambda e: e.dma_start(
            AP(qkvs, 0, [[3 * D, D], [1, 3 * D]]),
            AP(w_qkv, 0, [[3 * D, D], [1, 3 * D]])), sem=s_caq, dma=True,
            waits=pace_waits)
        tok = pl.op("gpsimd", lambda e: e.dma_start(
            AP(wprojs, 0, [[D, D], [1, D]]),
            AP(w_proj, 0, [[D, D], [1, D]])), sem=s_caq, dma=True)
        v_qkvcast = tok[1]
    if not cfg.get("ffn1"):
        if cfg.get("attn"):
            emit_qkv_casts()

    for dt in range(ND):
        pl.use_bank("tensor", 6)
        tokt = pl.op("tensor", lambda e, dt=dt: e.transpose(
            bank(6, 0, 8), wgf[0:8, dt * 128:(dt + 1) * 128],
            ident[0:8, 0:8]),
            sem=s_pe, waits=[(s_sml, v_small)])
        toke = pl.op("vector", lambda e, dt=dt: e.tensor_copy(
            wgtsb[:, dt * E:(dt + 1) * E], bank(6, 0, 8)), sem=s_ve,
            waits=[tokt])
        pl.free_bank(6, toke)
    v_wgt = s_ve.val

    def layernorm(x_tiles, xnT_free):
        SUM, SSQ, MM, RS, NMRS = 0, 16, 32, 48, 64
        if len(x_tiles) != NT:
            x_tiles = [x_tiles[-1] if x_tiles else None] * NT
        for w in xnT_free:
            if w is not None:
                pl.wait("sync", w[0], w[1])
        sq_toks = []
        v_tr = [None] * NT
        v_trpe = [None] * NT
        for g in range(4):
            cs = range(4 * g, 4 * g + 4)
            for c in cs:
                w = [] if len(sq_toks) < 2 else [sq_toks[-2]]
                if x_tiles[c] is not None:
                    w.append(x_tiles[c])
                tok_sq = pl.op("scalar", lambda e, c=c: e.activation(
                    junk2[c % 2][:, :], x_sb[:, c * D:(c + 1) * D], AF.Square,
                    accum_out=stat[:, SSQ + c:SSQ + c + 1]), sem=s_sc, waits=w)
                sq_toks.append(tok_sq)
            o = 4 * g

            def vo(b, waits=()):
                tok = pl.op("vector", b, sem=s_ve, waits=waits)
                pl.wait("vector", s_ve, tok[1])
                return tok
            wsum = [x_tiles[o + 3]] if x_tiles[o + 3] is not None else []
            vo(lambda e, o=o: e.reduce_sum(
                stat[:, SUM + o:SUM + o + 4],
                x_sb[:, o * D:(o + 4) * D].rearrange("p (c j) -> p c j", j=D),
                axis=AX.X), waits=wsum)
            vo(lambda e, o=o: e.tensor_scalar(
                stat[:, MM + o:MM + o + 4], stat[:, SUM + o:SUM + o + 4],
                1.0 / D, None, ALU.mult))
            vo(lambda e, o=o: e.scalar_tensor_tensor(
                stat[:, RS + o:RS + o + 4], stat[:, MM + o:MM + o + 4], -1.0,
                stat[:, MM + o:MM + o + 4], ALU.mult, ALU.mult))
            vo(lambda e, o=o: e.scalar_tensor_tensor(
                stat[:, RS + o:RS + o + 4], stat[:, SSQ + o:SSQ + o + 4], 1.0 / D,
                stat[:, RS + o:RS + o + 4], ALU.mult, ALU.add),
                waits=[sq_toks[o + 3]])
            vo(lambda e, o=o: e.tensor_scalar(
                stat[:, RS + o:RS + o + 4], stat[:, RS + o:RS + o + 4], EPS,
                None, ALU.add))
            tok_rcp = vo(lambda e, o=o: e.reciprocal(
                stat[:, RS + o:RS + o + 4], stat[:, RS + o:RS + o + 4]))
            tok_rs = pl.op("scalar", lambda e, o=o: e.activation(
                stat[:, RS + o:RS + o + 4], stat[:, RS + o:RS + o + 4], AF.Sqrt),
                sem=s_sc, waits=[tok_rcp])
            tok_nmrs = vo(lambda e, o=o: e.scalar_tensor_tensor(
                stat[:, NMRS + o:NMRS + o + 4], stat[:, MM + o:MM + o + 4], -1.0,
                stat[:, RS + o:RS + o + 4], ALU.mult, ALU.mult), waits=[tok_rs])
            for c in cs:
                par = c % 2
                waits = [tok_nmrs]
                if c >= 2:
                    waits.append(v_trpe[c - 2])
                tok_app = pl.op("vector", lambda e, c=c: e.tensor_scalar(
                    xnbuf[c % 2][:, :], x_sb[:, c * D:(c + 1) * D],
                    stat[:, RS + c:RS + c + 1], stat[:, NMRS + c:NMRS + c + 1],
                    ALU.mult, ALU.add), sem=s_ve, waits=waits)
                b0_ = 4 if c % 2 == 0 else 6
                pl.use_bank("tensor", b0_)
                pl.use_bank("tensor", b0_ + 1)
                for dt in range(ND):
                    tok_tp = pl.op("tensor", lambda e, c=c, dt=dt, b0_=b0_: e.transpose(
                        ps[:, b0_ * BK + dt * 128: b0_ * BK + (dt + 1) * 128],
                        xnbuf[c % 2][:, dt * 128:(dt + 1) * 128],
                        ident[:, :]), sem=s_pe,
                        waits=[tok_app] if dt == 0 else [])
                v_trpe[c] = tok_tp
                oap = AP(xnT, c * 128, [[ND * T, 128], [T, ND], [1, 128]])
                tok_ev = pl.op("vector", lambda e, oap=oap, b0_=b0_: e.tensor_copy(
                    oap, ps[:, b0_ * BK: b0_ * BK + 1024]), sem=s_ve, waits=[tok_tp])
                pl.free_bank(b0_, tok_ev)
                pl.free_bank(b0_ + 1, tok_ev)
                v_tr[c] = [tok_ev]
        return v_tr, sq_toks[-1]

    def ffn(tok_xnT, region_toks=(), region_toks2=None, call1=False):
        st_gate = {}

        def emit_gate():
            pl.wait("tensor", s_ve, v_wgt)
            pl.wait("tensor", s_casm, v_smallcast)
            pl.use_bank("tensor", 6)
            pl.use_bank("tensor", 7)
            for c in range(NT):
                pl.wait_toks("tensor", tok_xnT[c])
                for dt in range(ND):
                    pl.op("tensor", lambda e, c=c, dt=dt: e.matmul(
                        bank(6, c * E, E), xnT[:, dt * T + c * 128: dt * T + (c + 1) * 128],
                        wgtsb[:, dt * E:(dt + 1) * E], start=(dt == 0), stop=False), sem=s_pe)
                tok_g = pl.op("tensor", lambda e, c=c: e.matmul(
                    bank(6, c * E, E), ones[0:1, 0:128], bgbf[0:1, :],
                    start=False, stop=True), sem=s_pe)

            G0, G1 = 80, 96
            gv = bank(6, 0, NT * E).rearrange("p (c e) -> p c e", e=E)
            g2 = bank(7, 0, NT * E).rearrange("p (c e) -> p c e", e=E)

            def vo(b, waits=()):
                tok = pl.op("vector", b, sem=s_ve, waits=waits)
                pl.wait("vector", s_ve, tok[1])
                return tok

            vo(lambda e: e.reduce_max(stat[:, G0:G0 + NT], gv, axis=AX.X), waits=[tok_g])
            vo(lambda e: e.tensor_tensor(
                comb[:, :].rearrange("p (c e) -> p c e", e=E), gv,
                bc(stat[:, G0:G0 + NT], E), ALU.is_equal))
            vo(lambda e: e.scalar_tensor_tensor(
                bank(7, 0, NT * E), comb[:, :], -1e30, bank(6, 0, NT * E),
                ALU.mult, ALU.add))
            vo(lambda e: e.reduce_max(stat[:, G1:G1 + NT], g2, axis=AX.X))
            vo(lambda e: e.tensor_tensor(
                bank(6, 0, NT * E).rearrange("p (c e) -> p c e", e=E), g2,
                bc(stat[:, G1:G1 + NT], E), ALU.is_equal))
            tok_d = vo(lambda e: e.tensor_sub(
                stat[:, G1:G1 + NT], stat[:, G1:G1 + NT], stat[:, G0:G0 + NT]))
            tok_exp = pl.op("scalar", lambda e: e.activation(
                stat[:, G1:G1 + NT], stat[:, G1:G1 + NT], AF.Exp), sem=s_sc,
                waits=[tok_d])
            vo(lambda e: e.tensor_scalar(
                stat[:, G0:G0 + NT], stat[:, G1:G1 + NT], 1.0, None, ALU.add),
                waits=[tok_exp])
            vo(lambda e: e.reciprocal(stat[:, G0:G0 + NT], stat[:, G0:G0 + NT]))
            vo(lambda e: e.tensor_scalar(
                stat[:, G0:G0 + NT], stat[:, G0:G0 + NT], 0.5, None, ALU.mult))
            vo(lambda e: e.tensor_scalar(
                stat[:, G1:G1 + NT], stat[:, G0:G0 + NT], -1.0, 0.5, ALU.mult, ALU.add))
            vo(lambda e: e.tensor_tensor(
                comb[:, :].rearrange("p (c e) -> p c e", e=E),
                comb[:, :].rearrange("p (c e) -> p c e", e=E),
                bc(stat[:, G0:G0 + NT], E), ALU.mult))
            vo(lambda e: e.tensor_tensor(
                bank(6, 0, NT * E).rearrange("p (c e) -> p c e", e=E),
                bank(6, 0, NT * E).rearrange("p (c e) -> p c e", e=E),
                bc(stat[:, G1:G1 + NT], E), ALU.mult))
            tok_cb = vo(lambda e: e.tensor_add(comb[:, :], comb[:, :], bank(6, 0, NT * E)))
            pl.free_bank(6, tok_cb)
            pl.free_bank(7, tok_cb)

            pl.wait("tensor", tok_cb[0], tok_cb[1])
            for b in range(4):
                pl.use_bank("tensor", b)
            for c in range(NT):
                tok_t = pl.op("tensor", lambda e, c=c: e.transpose(
                    ps[0:8, c * 128:(c + 1) * 128],
                    comb[:, c * E:(c + 1) * E],
                    ident[:, :]), sem=s_pe)
            tok_cT = pl.op("vector", lambda e: e.tensor_copy(
                combT[0:8, :], ps[0:8, 0:T]), sem=s_ve, waits=[tok_t])
            for b in range(4):
                pl.free_bank(b, tok_cT)

            pl.wait("tensor", tok_cT[0], tok_cT[1])
            for tt in range(NT):
                for dc in range(2):
                    b = 6 + (tt * 2 + dc) % 2
                    pl.use_bank("tensor", b)
                    tok_m = pl.op("tensor", lambda e, tt=tt, dc=dc, b=b: e.matmul(
                        bank(b), combT[0:8, tt * 128:(tt + 1) * 128],
                        b2bf[0:8, dc * 512:(dc + 1) * 512], start=True, stop=True),
                        sem=s_pe)
                    xap = x_sb[:, tt * D + dc * 512: tt * D + dc * 512 + 512]
                    tok_a = pl.op("vector", lambda e, b=b, xap=xap: e.tensor_add(
                        xap, xap, bank(b)), sem=s_pe if False else s_ve, waits=[tok_m])
                    pl.free_bank(b, tok_a)

            st_gate["tok_cT"] = tok_cT

        v_w1 = [None] * E
        v_w2 = [None] * E
        tok_m1_done = [None] * E
        tok_m2_done = [None] * E
        relu_tok = {}
        x_tile_tok = [None] * NT
        for ee in range(E):

            w = [v_wcast1[ee]]
            if ee >= 1:
                w.append(tok_m1_done[ee - 1])
            else:
                w.extend(region_toks)
            for dt in range(ND):
                tok = pl.op("sync", lambda e, ee=ee, dt=dt: e.dma_start(
                    w1t[:, dt * HF:(dt + 1) * HF],
                    AP(W1s, (ee * ND + dt) * HF * 128, [[128, HF], [1, 128]]),
                    transpose=True), sem=s_w1l, dma=True, waits=w)
                w = []
            v_w1[ee] = tok[1]
            w = [v_wcast2[ee]]
            if ee >= 1:
                w.append(tok_m2_done[ee - 1])
            else:
                w.extend(region_toks2 if region_toks2 is not None else region_toks)
            for ht in range(NHT):
                tok = pl.op("sync", lambda e, ee=ee, ht=ht: e.dma_start(
                    w2t[:, ht * D:(ht + 1) * D],
                    AP(W2s, (ee * NHT + ht) * D * 128, [[128, D], [1, 128]]),
                    transpose=True), sem=s_w2l, dma=True, waits=w)
                w = []
            v_w2[ee] = tok[1]

            for tc in range(NTC):
                pl.wait("tensor", s_w1l, v_w1[ee])
                for cc in range(4 * tc, 4 * tc + 4):
                    pl.wait_toks("tensor", tok_xnT[cc])
                for ht in range(NHT):
                    b = ht % 4
                    pl.use_bank("tensor", b)
                    for dt in range(ND):
                        tok_m = pl.op("tensor", lambda e, ht=ht, tc=tc, dt=dt, b=b: e.matmul(
                            bank(b),
                            w1t[:, dt * HF + ht * 128: dt * HF + (ht + 1) * 128],
                            xnT[:, dt * T + tc * TC: dt * T + (tc + 1) * TC],
                            start=(dt == 0), stop=(dt == ND - 1)), sem=s_pe)
                    bcol = b1sb[:, ee * NHT + ht: ee * NHT + ht + 1]
                    oap = hT[:, ht * TC:(ht + 1) * TC]
                    if ht % 2 == 0:
                        tok_r = pl.op("scalar", lambda e, oap=oap, b=b, bcol=bcol:
                                      e.activation(oap, bank(b), AF.Relu, bias=bcol),
                                      sem=s_sc, waits=[tok_m])
                    else:
                        tok_r = pl.op("vector", lambda e, oap=oap, b=b, bcol=bcol:
                                      e.tensor_scalar(oap, bank(b), bcol, 0.0,
                                                      ALU.add, ALU.max),
                                      sem=s_ve, waits=[tok_m])
                    relu_tok[(tc, ht)] = tok_r
                    pl.free_bank(b, tok_r)
                tok_m1_done[ee] = tok_m
                if ee == 0 and tc == 0:
                    emit_gate()
                if call1 and tc == 0:
                    if ee >= 1 and ee + 2 < E:
                        emit_cast(ee + 2, pace_waits=[tok_m])
                    if ee == 2 and cfg.get("attn"):
                        emit_qkv_casts(pace_waits=[tok_m])

                pl.wait("tensor", s_w2l, v_w2[ee])
                for msub in range(TC // 128):
                    tt = tc * (TC // 128) + msub
                    for dc in range(2):
                        b = 4 + (msub * 2 + dc) % 2
                        pl.use_bank("tensor", b)
                        for ht in range(NHT):
                            waits = [relu_tok[(tc, ht)]] if msub == 0 and dc == 0 else []
                            tok_m = pl.op("tensor", lambda e, ht=ht, msub=msub, dc=dc, b=b, tc=tc: e.matmul(
                                bank(b),
                                hT[:, ht * TC + msub * 128: ht * TC + (msub + 1) * 128],
                                w2t[:, ht * D + dc * 512: ht * D + (dc + 1) * 512],
                                start=(ht == 0), stop=(ht == NHT - 1)), sem=s_pe,
                                waits=waits)
                        xap = x_sb[:, tt * D + dc * 512: tt * D + dc * 512 + 512]
                        tok_y = pl.op("vector", lambda e, b=b, xap=xap, col=tt * E + ee:
                                      e.scalar_tensor_tensor(
                                          xap, bank(b), comb[:, col:col + 1], xap,
                                          ALU.mult, ALU.add), sem=s_ve, waits=[tok_m])
                        pl.free_bank(b, tok_y)
                        if ee == E - 1:
                            x_tile_tok[tt] = tok_y
                tok_m2_done[ee] = tok_m

        return dict(x_done=x_tile_tok[NT - 1], x_tiles=x_tile_tok,
                    m1_last=tok_m1_done[E - 1], m2_last=tok_m2_done[E - 1])


    def attn(tok_xnT, tok_ffn1, tok_lnsq):
        SHIFTC = 112

        tok_cs = None
        for i, tgt in ((2, cosu), (3, sinu)):
            tok_cs = pl.op("gpsimd", lambda e, i=i, tgt=tgt: e.dma_start(
                tgt[:, :], AP(coss, i * 128 * T, [[T, 128], [1, T]])),
                sem=s_cs, dma=True,
                waits=[tok_ffn1["m1_last"], (s_caq, v_qkvcast)] if i == 2 else [])
        pl.op("vector", lambda e: e.memset(stat[:, SHIFTC:SHIFTC + 1], EXPSHIFT), sem=s_ve)
        pl.op("vector", lambda e: e.memset(stat[:, 113:114], 0.125), sem=s_ve)
        tok_shift = pl.op("vector", lambda e: e.memset(onesf[0:1, :], 1.0), sem=s_ve)

        tok_vones = pl.op("vector", lambda e: e.memset(
            AP(vh, 64, [[NT * 520, 128], [520, NT], [65, 8], [1, 1]]), 1.0),
            sem=s_ve, waits=[tok_ffn1["m2_last"]])

        def v_half(half, extra_waits):
            toks = []
            for grp in range(2):
                tok_l = pl.op("sync", lambda e, half=half, grp=grp: e.dma_start(
                    wsl[:, grp * 2048:(grp + 1) * 2048],
                    AP(qkvs, 2048 + half * 512 + grp * 256,
                       [[3 * D, 128], [128 * 3 * D, ND], [1, 256]]),
                    transpose=False), sem=s_vsl, dma=True,
                    waits=(extra_waits if grp == 0 else []))
            v_ld = tok_l[1]
            last_ev = None
            for tt in range(NT):
                pl.wait_toks("tensor", tok_xnT[tt])
                for grp in range(2):
                    b = 6 + grp
                    pl.use_bank("tensor", b)
                    for dt in range(ND):
                        tok_m = pl.op("tensor", lambda e, tt=tt, grp=grp, dt=dt, b=b: e.matmul(
                            bank(b, 0, 256),
                            xnT[:, dt * T + tt * 128: dt * T + (tt + 1) * 128],
                            wsl[:, grp * 2048 + dt * 256: grp * 2048 + (dt + 1) * 256],
                            start=(dt == 0), stop=(dt == ND - 1)), sem=s_pe,
                            waits=[(s_vsl, v_ld)] if (tt == 0 and dt == 0) else [])
                    oap = AP(vh, tt * 520 + grp * 4 * 65,
                             [[NT * 520, 128], [65, 4], [1, 64]])
                    iap = bank(b, 0, 256).rearrange("p (h c) -> p h c", c=64)
                    last_ev = pl.op("vector", lambda e, oap=oap, iap=iap: e.tensor_copy(
                        oap, iap), sem=s_ve, waits=[tok_m])
                    pl.free_bank(b, last_ev)
            return last_ev

        tok_vA = v_half(0, [tok_ffn1["m2_last"], (s_caq, v_qkvcast), tok_vones, tok_lnsq])

        at_free = {0: None, 1: None}
        tok_qk_done = [None] * 9
        tok_rope_done = [None] * 9
        tok_sc_done = [None] * 8
        tok_norm = None
        tok_vB = [None]

        def emit_chunk_qk(hc, extra_w=()):
            par = hc % 2
            w = list(extra_w)
            if hc >= 1:
                w.append(tok_qk_done[hc - 1])
            if hc == 0:
                w.append(tok_vA)
            if hc == 5 and tok_vB[0] is not None:
                w.append(tok_vB[0])
            tok_l = None
            for qd in (0, 2):
                base_c = hc * 128 if qd == 0 else D + hc * 128
                tok_l = pl.op("sync", lambda e, qd=qd, base_c=base_c: e.dma_start(
                    wsl[:, qd * 1024:(qd + 1) * 1024],
                    AP(qkvs, base_c, [[3 * D, 128], [128 * 3 * D, ND], [1, 128]])),
                    sem=s_wsl[par], dma=True, waits=(w if qd == 0 else []))
            v_wsl = tok_l[1]
            tok_rot = None
            for qsrc, qdst in ((0, 1), (2, 3)):
                for half in range(2):
                    oap = AP(wsl, qdst * 1024 + half * 32,
                             [[4 * ND * 128, 128], [128, ND], [64, 2], [1, 32]])
                    iap = AP(wsl, qsrc * 1024 + (32 if half == 0 else 0),
                             [[4 * ND * 128, 128], [128, ND], [64, 2], [1, 32]])
                    tok_rot = pl.op("vector", lambda e, oap=oap, iap=iap: e.tensor_copy(
                        oap, iap), sem=s_ve,
                        waits=[(s_wsl[par], v_wsl)])
            tok_mq = None
            rope_waits = []
            if hc >= 2:
                rope_waits.append(tok_sc_done[hc - 2])
            last_rope = None
            for part_i, (cost, sint, dest) in enumerate((
                    (cosu, sinu, qrope[par]), (cosu, sinu, krope[par]))):
                for qq in range(4):
                    pl.use_bank("tensor", 6)
                    pl.use_bank("tensor", 7)
                    for cc in range(4 * qq, 4 * qq + 4):
                        pl.wait_toks("tensor", tok_xnT[cc])
                    for dt in range(ND):
                        wts = []
                        if part_i == 0 and qq == 0 and dt == 0:
                            wts = [(s_wsl[par], v_wsl), (s_cs, tok_cs[1]), tok_rot]
                            wts.extend(rope_waits)
                        tok_mq = pl.op("tensor", lambda e, qq=qq, dt=dt, pi=part_i: e.matmul(
                            bank(6),
                            wsl[:, (2 * pi) * 1024 + dt * 128:(2 * pi) * 1024 + (dt + 1) * 128],
                            xnT[:, dt * T + qq * 512: dt * T + (qq + 1) * 512],
                            start=(dt == 0), stop=(dt == ND - 1)), sem=s_pe, waits=wts)
                    for dt in range(ND):
                        tok_mq = pl.op("tensor", lambda e, qq=qq, dt=dt, pi=part_i: e.matmul(
                            bank(7),
                            wsl[:, (2 * pi + 1) * 1024 + dt * 128:(2 * pi + 1) * 1024 + (dt + 1) * 128],
                            xnT[:, dt * T + qq * 512: dt * T + (qq + 1) * 512],
                            start=(dt == 0), stop=(dt == ND - 1)), sem=s_pe)
                    wts = [tok_mq] + (rope_waits if part_i == 0 and qq == 0 else [])
                    tma, tmb = tmpah[hc], tmpbh[hc]
                    tok_a = pl.op("vector", lambda e, qq=qq, cost=cost, tma=tma: e.tensor_tensor(
                        tma[:, :], bank(6), cost[:, qq * 512:(qq + 1) * 512],
                        ALU.mult), sem=s_ve, waits=wts + ([(s_ve, last_rope[1])] if last_rope else []))
                    tok_b = pl.op("vector", lambda e, qq=qq, sint=sint, tmb=tmb: e.tensor_tensor(
                        tmb[:, :], bank(7), sint[:, qq * 512:(qq + 1) * 512],
                        ALU.mult), sem=s_ve)
                    pl.free_bank(6, tok_a)
                    pl.free_bank(7, tok_b)
                    pl.wait("vector", s_ve, tok_b[1])
                    last_rope = pl.op("vector", lambda e, qq=qq, dest=dest, tma=tma, tmb=tmb: e.tensor_add(
                        dest[:, qq * 512:(qq + 1) * 512], tma[:, :], tmb[:, :]),
                        sem=s_ve)
            tok_qk_done[hc] = tok_mq
            tok_rope_done[hc] = last_rope

        def scores_head(hc, hp):
            nonlocal tok_norm
            par = hc % 2
            h = 2 * hc + hp
            hh = h % 8
            at_b = [at0, at1]
            prev = [None, None]
            pend = None

            av_state = {}

            def do_av_mms(tqc, evac_toks):
                atb = at_b[tqc % 2]
                bo = 4 if tqc % 2 == 0 else 6
                pl.use_bank("tensor", bo)
                tok_av = None
                for tk in range(16):
                    wts = [evac_toks[tk // 4]] if tk % 4 == 0 else []
                    if tk == 0 and tqc == 0 and hp == 0 and hc >= 4:
                        wts.append(tok_vB[0])
                    tok_av = pl.op("tensor", lambda e, tk=tk, hh=hh, bo=bo, atb=atb: e.matmul(
                        bank(bo, 0, 256, 0, 65),
                        vh[:, tk * 520 + hh * 65: tk * 520 + hh * 65 + 65],
                        atb[:, tk * 256:(tk + 1) * 256],
                        start=(tk == 0), stop=(tk == 15)), sem=s_pe, waits=wts)
                at_free[tqc % 2] = tok_av
                ro = 256 * (tqc % 2)
                tok_rl = pl.op("vector", lambda e, bo=bo, ro=ro: e.reciprocal(
                    rlf[0:1, ro:ro + 256], bank(bo, 0, 256, 64, 1)), sem=s_ve,
                    waits=[tok_av])
                pl.wait("vector", s_ve, tok_rl[1])
                av_state[tqc] = tok_rl

            def do_norm(tqc):
                nonlocal tok_norm
                bo = 4 if tqc % 2 == 0 else 6
                br = 5 if tqc % 2 == 0 else 7
                ro = 256 * (tqc % 2)
                tok_rl = av_state.pop(tqc)
                pl.use_bank("tensor", br)
                tok_rb = pl.op("tensor", lambda e, br=br, ro=ro: e.matmul(
                    bank(br, 0, 256, 0, 64), onesf[0:1, :],
                    rlf[0:1, ro:ro + 256],
                    start=True, stop=True), sem=s_pe, waits=[tok_rl, tok_shift])
                tok_rbe = pl.op("vector", lambda e, br=br, ro=ro: e.tensor_copy(
                    rbsb[:, ro:ro + 256], bank(br, 0, 256, 0, 64)), sem=s_ve,
                    waits=[tok_rb])
                pl.free_bank(br, tok_rbe)
                pl.wait("vector", s_ve, tok_rbe[1])
                tok_norm = pl.op("vector", lambda e, hp=hp, hc=hc, tqc=tqc, bo=bo, ro=ro: e.tensor_mul(
                    otall[64 * hp:64 * hp + 64,
                          hc * T + tqc * 256: hc * T + tqc * 256 + 256],
                    bank(bo, 0, 256, 0, 64), rbsb[:, ro:ro + 256]), sem=s_ve)
                pl.free_bank(bo, tok_norm)

            for tqc in range(8):
                atb = at_b[tqc % 2]
                evac_toks = []
                for g in range(4):
                    if g == 2 and pend is not None:
                        do_av_mms(*pend)
                    pb = 0 if g % 2 == 0 else 2
                    pl.use_bank("tensor", pb)
                    pl.use_bank("tensor", pb + 1)
                    for tki in range(4):
                        tk = 4 * g + tki
                        wts = []
                        if g == 0 and tki == 0 and tqc == 0 and hp == 0:
                            wts = [tok_rope_done[hc]]
                        tok_s = pl.op("tensor", lambda e, tk=tk, tki=tki, pb=pb, tqc=tqc, hp=hp, par=par: e.matmul(
                            ps[:, (pb + tki // 2) * BK + (tki % 2) * 256:
                               (pb + tki // 2) * BK + (tki % 2) * 256 + 256],
                            krope[par][64 * hp:64 * hp + 64, tk * 128:(tk + 1) * 128],
                            qrope[par][64 * hp:64 * hp + 64, tqc * 256:(tqc + 1) * 256],
                            start=True, stop=True), sem=s_pe, waits=wts)
                    ew = [tok_s, tok_shift]
                    if at_free[tqc % 2] is not None:
                        ew.append(at_free[tqc % 2])
                    tok_e = pl.op("scalar", lambda e, g=g, pb=pb, atb=atb: e.activation(
                        atb[:, g * 1024:(g + 1) * 1024],
                        ps[:, pb * BK: pb * BK + 1024], AF.Exp,
                        bias=stat[:, SHIFTC:SHIFTC + 1], scale=0.125), sem=s_sc,
                        waits=ew)
                    evac_toks.append(tok_e)
                    pl.free_bank(pb, tok_e)
                    pl.free_bank(pb + 1, tok_e)
                if pend is not None:
                    do_norm(pend[0])
                pend = (tqc, evac_toks)
            do_av_mms(*pend)
            do_norm(pend[0])

        emit_chunk_qk(0)
        for hc in range(8):
            scores_head(hc, 0)
            if hc + 1 < 8:
                emit_chunk_qk(hc + 1)
            scores_head(hc, 1)
            tok_sc_done[hc] = (s_pe, s_pe.val)
            if hc == 3:
                tok_vB[0] = v_half(1, [tok_qk_done[4]])

        tok_bpj = pl.op("gpsimd", lambda e: e.dma_start(
            bprojbf[:, :], AP(b_proj, 0, [[D, 1], [1, D]])), sem=s_cs, dma=True,
            waits=[tok_sc_done[7]])
        tok_wpj = pl.op("sync", lambda e: e.dma_start(
            wpj[:, :], AP(wprojs, 0, [[D, 128], [128 * D, ND], [1, D]])),
            sem=s_wpj, dma=True,
            waits=[tok_sc_done[7], (s_caq, v_qkvcast)])
        last = None
        x_proj_tok = [None] * NT
        for tt in range(NT):
            for dc in range(2):
                b = (tt * 2 + dc) % 2
                pl.use_bank("tensor", b)
                for ct in range(ND):
                    wts = []
                    if tt == 0 and dc == 0 and ct == 0:
                        wts = [tok_norm, (s_wpj, tok_wpj[1]), tok_bpj]
                    tok_m = pl.op("tensor", lambda e, tt=tt, dc=dc, ct=ct, b=b: e.matmul(
                        bank(b), otall[:, ct * T + tt * 128: ct * T + (tt + 1) * 128],
                        wpj[:, ct * D + dc * 512: ct * D + (dc + 1) * 512],
                        start=(ct == 0), stop=False), sem=s_pe, waits=wts)
                tok_m = pl.op("tensor", lambda e, dc=dc, b=b: e.matmul(
                    bank(b), ones[0:1, 0:128], bprojbf[0:1, dc * 512:(dc + 1) * 512],
                    start=False, stop=True), sem=s_pe)
                xap = x_sb[:, tt * D + dc * 512: tt * D + dc * 512 + 512]
                last = pl.op("vector", lambda e, b=b, xap=xap: e.tensor_add(
                    xap, xap, bank(b)), sem=s_ve, waits=[tok_m])
                pl.free_bank(b, last)
                x_proj_tok[tt] = last
        return dict(x_done=last, x_tiles=x_proj_tok,
                    proj_done=(s_pe, s_pe.val),
                    rope_last=tok_rope_done[7], sc_last=tok_sc_done[7],
                    qk_last=tok_qk_done[7], vB=tok_vB[0])

    tok_xnT, _sq1 = layernorm(tok_x_tiles, [])
    r1 = ffn(tok_xnT, call1=True) if cfg.get("ffn1") else None
    if cfg.get("attn"):
        xr = r1["x_tiles"] if r1 else [tok_x]
        xnfree = [r1["m1_last"]] if r1 else []
        tok_xnT2, sq2 = layernorm(xr, xnfree)
        if r1 is None:
            emit_cast(0)
            emit_cast(1)
            r1 = dict(m1_last=(s_pe, 0), m2_last=(s_pe, 0))
        ra = attn(tok_xnT2, r1, sq2)
        if cfg.get("ffn2"):
            for ee in range(2, E):
                emit_cast(ee)
            tok_xnT3, _sq3 = layernorm(ra["x_tiles"], [ra["qk_last"], ra["vB"]])
            r2 = ffn(tok_xnT3, region_toks=[ra["proj_done"]],
                     region_toks2=[ra["sc_last"]])

    waits = [(s_ve, s_ve.val), (s_sc, s_sc.val), (s_pe, s_pe.val)]
    pl.op("gpsimd", lambda e: e.dma_start(
        AP(out, 0, [[D, 128], [128 * D, NT], [1, D]]),
        AP(x_sb, 0, [[NT * D, 128], [D, NT], [1, D]])), sem=s_st, dma=True,
        waits=waits)
    pl.op("gpsimd", lambda e: e.nop(), waits=[(s_st, s_st.val)])

    pl.finish()
    return nc


def _rope_tables_host():
    try:
        import jax
        import jax.numpy as jnp
        cpu = [d for d in jax.devices("cpu")]
        with jax.default_device(cpu[0]):
            inv = 1.0 / 10000.0 ** (jnp.arange(0, DH, 2, dtype=jnp.float32) / DH)
            t = jnp.arange(T, dtype=jnp.float32)
            freqs = t[:, None] * inv[None, :]
            cos = np.asarray(jnp.cos(freqs), dtype=np.float32)
            sin = np.asarray(jnp.sin(freqs), dtype=np.float32)
            return cos, sin
    except Exception:
        expo = np.arange(0, DH, 2, dtype=np.float32) / np.float32(DH)
        inv = np.float32(1.0) / np.power(np.float32(10000.0), expo)
        t = np.arange(T, dtype=np.float32)
        ang = t[:, None] * inv[None, :]
        return np.cos(ang), np.sin(ang)


def make_consts():
    cos, sin = _rope_tables_host()
    p = np.arange(128)
    cosf = cos[:, p % 32].T
    sinf = sin[:, p % 32].T
    sgn = np.where((p % 64) < 32, -1.0, 1.0).astype(np.float32)[:, None]
    coss = np.stack([cosf / 8.0, (sinf * sgn) / 8.0, cosf, sinf * sgn])
    ident = np.eye(128, dtype=np.float32)
    onesm = np.ones((128, 128), dtype=ml_dtypes.bfloat16)
    return coss.astype(np.float32), ident, onesm




_NC_CACHE = {}


def _get_nc():
    if "nc" not in _NC_CACHE:
        _NC_CACHE["nc"] = build({"ffn1": True, "attn": True, "ffn2": True})
    return _NC_CACHE["nc"]


def kernel(L0_attn_x, L0_attn_w_qkv, L0_attn_w_proj, L0_attn_b_proj,
           L0_ffn_W1, L0_ffn_b1, L0_ffn_W2, L0_ffn_b2, L0_ffn_Wg, L0_ffn_bg):
    from concourse.bass_utils import run_bass_kernel_spmd
    x = np.ascontiguousarray(np.asarray(L0_attn_x, dtype=np.float32))
    coss, ident, onesm = make_consts()
    shared = dict(
        w_qkv=np.ascontiguousarray(np.asarray(L0_attn_w_qkv, dtype=np.float32)),
        w_proj=np.ascontiguousarray(np.asarray(L0_attn_w_proj, dtype=np.float32)),
        b_proj=np.ascontiguousarray(np.asarray(L0_attn_b_proj, dtype=np.float32)),
        W1=np.ascontiguousarray(np.asarray(L0_ffn_W1, dtype=np.float32)),
        b1=np.ascontiguousarray(np.asarray(L0_ffn_b1, dtype=np.float32)),
        W2=np.ascontiguousarray(np.asarray(L0_ffn_W2, dtype=np.float32)),
        b2=np.ascontiguousarray(np.asarray(L0_ffn_b2, dtype=np.float32)),
        Wg=np.ascontiguousarray(np.asarray(L0_ffn_Wg, dtype=np.float32)),
        bg=np.ascontiguousarray(np.asarray(L0_ffn_bg, dtype=np.float32)),
        coss=coss, ident_in=ident, ones_in=onesm,
    )
    n_cores = 8
    in_maps = [dict(shared, x_in=np.ascontiguousarray(x[i])) for i in range(n_cores)]
    nc = _get_nc()
    res = run_bass_kernel_spmd(nc, in_maps, core_ids=list(range(n_cores)))
    return np.stack([np.asarray(r["out"], dtype=np.float32) for r in res.results])


# version 12
# speedup vs baseline: 1.1221x; 1.1221x over previous
import numpy as np
import ml_dtypes
import concourse.bass as bass
import concourse.mybir as mybir

F32 = mybir.dt.float32
BF16 = mybir.dt.bfloat16
AF = mybir.ActivationFunctionType
ALU = mybir.AluOpType
AX = mybir.AxisListType
AP = bass.AP

T, D, HF, E, NH, DH = 2048, 1024, 2048, 8, 16, 64
NT, ND, NHT = T // 128, D // 128, HF // 128
TC = 512
NTC = T // TC
EPS = 1e-5
EXPSHIFT = -20.0
BK = 512


def bc(ap, n):
    return AP(ap.tensor, ap.offset, [list(d) for d in ap.ap] + [[0, n]])


class SemC:
    __slots__ = ("h", "val")

    def __init__(self, h):
        self.h = h
        self.val = 0


class Plan:
    ENGS = ("tensor", "vector", "scalar", "sync", "gpsimd")

    def __init__(self, nc):
        self.nc = nc
        self.ops = {e: [] for e in self.ENGS}
        self.waited = {e: {} for e in self.ENGS}
        self.bank_free = {}

    def sem(self, name):
        return SemC(self.nc.alloc_semaphore(name))

    def wait(self, eng, s, v):
        if s is None or v is None or v <= 0:
            return
        if self.waited[eng].get(s.h.num, -1) >= v:
            return
        self.waited[eng][s.h.num] = v
        self.ops[eng].append(lambda e, h=s.h, v=v: e.wait_ge(h, v))

    def op(self, eng, build, sem=None, waits=(), dma=False):
        for w in waits:
            if w is None:
                continue
            self.wait(eng, w[0], w[1])
        if sem is not None:
            inc = 16 if dma else 1
            sem.val += inc
            self.ops[eng].append(lambda e, b=build, h=sem.h, i=inc: b(e).then_inc(h, i))
            return (sem, sem.val)
        self.ops[eng].append(build)
        return None

    def wait_toks(self, eng, toks):
        if toks is None:
            return
        if isinstance(toks, tuple):
            toks = [toks]
        for t in toks:
            if t is not None:
                self.wait(eng, t[0], t[1])

    def use_bank(self, eng, b):
        w = self.bank_free.get(b)
        if w is not None:
            self.wait(eng, w[0], w[1])

    def free_bank(self, b, tok):
        self.bank_free[b] = tok

    def finish(self):
        with self.nc.allow_low_precision("bf16 attn l-reciprocal (within tolerance)"), \
             self.nc.Block() as block:
            for eng in self.ENGS:
                def runner(e, eng=eng):
                    for f in self.ops[eng]:
                        f(e)
                getattr(block, eng)(runner)


def build(cfg):
    nc = bass.Bass("TRN2", target_bir_lowering=False, debug=False)

    x_in = nc.dram_tensor("x_in", [T, D], F32, kind="ExternalInput")
    w_qkv = nc.dram_tensor("w_qkv", [D, 3 * D], F32, kind="ExternalInput")
    w_proj = nc.dram_tensor("w_proj", [D, D], F32, kind="ExternalInput")
    b_proj = nc.dram_tensor("b_proj", [D], F32, kind="ExternalInput")
    W1 = nc.dram_tensor("W1", [E, HF, D], F32, kind="ExternalInput")
    b1 = nc.dram_tensor("b1", [E, HF], F32, kind="ExternalInput")
    W2 = nc.dram_tensor("W2", [E, D, HF], F32, kind="ExternalInput")
    b2 = nc.dram_tensor("b2", [E, D], F32, kind="ExternalInput")
    Wg = nc.dram_tensor("Wg", [E, D], F32, kind="ExternalInput")
    bg = nc.dram_tensor("bg", [E], F32, kind="ExternalInput")
    coss = nc.dram_tensor("coss", [4, 128, T], F32, kind="ExternalInput")
    ident_in = nc.dram_tensor("ident_in", [128, 128], F32, kind="ExternalInput")
    ones_in = nc.dram_tensor("ones_in", [128, 128], BF16, kind="ExternalInput")
    out = nc.dram_tensor("out", [T, D], F32, kind="ExternalOutput")

    W1s = nc.dram_tensor("W1s", [E, ND, HF, 128], BF16, kind="Internal")
    W2s = nc.dram_tensor("W2s", [E, NHT, D, 128], BF16, kind="Internal")
    qkvs = nc.dram_tensor("qkvs", [D, 3 * D], BF16, kind="Internal")
    wprojs = nc.dram_tensor("wprojs", [D, D], BF16, kind="Internal")

    AR = 212736
    arena = nc.alloc_sbuf_tensor("arena", [128, AR], mybir.dt.uint8)
    abase = nc.lookup_mloc(arena).addr

    def sb(name, shape, dtype, off):
        nbytes = int(np.prod(shape[1:])) * mybir.dt.size(dtype)
        assert off + nbytes <= AR, f"{name}: {off}+{nbytes} > {AR}"
        return nc.alloc_sbuf_tensor_at(name, shape, dtype, offset=abase + off)

    x_sb = sb("x_sb", [128, NT * D], F32, 0)
    M = 65536
    def psb(name, shape, dtype):
        nonlocal M
        t = sb(name, shape, dtype, M)
        nbytes = int(np.prod(shape[1:])) * mybir.dt.size(dtype)
        M += (nbytes + 31) // 32 * 32
        return t
    b1sb = psb("b1sb", [128, E * NHT], F32)
    comb = psb("comb", [128, NT * E], F32)
    b2bf = psb("b2bf", [8, D], BF16)
    bgbf = psb("bgbf", [1, E], BF16)
    wgtsb = psb("wgtsb", [128, ND * E], BF16)
    ident = psb("ident", [128, 128], F32)
    ones = psb("ones", [128, 128], BF16)
    stat = psb("stat", [128, 8 * NT], F32)
    rlbf = psb("rlbf", [1, 1024], BF16)
    rbsb = psb("rbsb", [64, 512], F32)
    assert M <= 75776, M
    xnT = sb("xnT", [128, ND * T], BF16, 75776)
    U = 108544
    w1t = sb("w1t", [128, ND * HF], BF16, U)
    w2t = sb("w2t", [128, NHT * D], BF16, U + 32768)
    hT = sb("hT", [128, NHT * TC], BF16, U + 65536)
    combT = sb("combT", [8, T], BF16, U + 81920)
    tmpah = [sb(f"tmpa{h}", [128, 512], F32, U + 57600 + h * 4096) for h in range(8)]
    tmpbh = [sb(f"tmpb{h}", [128, 512], F32, U + 59648 + h * 4096) for h in range(8)]
    cosu = sb("cosu", [128, T], BF16, U + 0)
    sinu = sb("sinu", [128, T], BF16, U + 4096)
    atA = sb("atA", [128, 16 * 256], BF16, U + 8192)
    atB = sb("atB", [128, 16 * 256], BF16, U + 16384)
    qrope = [sb(f"qrope{i}", [128, T], BF16, U + 24576 + 8192 * i) for i in range(2)]
    krope = [sb(f"krope{i}", [128, T], BF16, U + 28672 + 8192 * i) for i in range(2)]
    vh = sb("vh", [128, NT * 520], BF16, U + 40960)
    bprojbf = sb("bprojbf", [1, D], BF16, U + 8192)
    rlbf2 = None
    otall = sb("otall", [128, 8 * T], BF16, U + 57600)
    wsl = sb("wsl", [128, 4 * ND * 128], BF16, U + 90368)
    xnbuf = [sb("xnbuf0", [128, D], F32, U + 81920),
             sb("xnbuf1", [128, D], F32, U + 98560)]
    junk2 = [sb("junk0", [128, D], BF16, U + 92160),
             sb("junk1", [128, D], BF16, U + 96256)]
    wpj = sb("wpj", [128, ND * D], BF16, U + 24576)
    wgf = sb("wgf", [8, D], F32, U + 86016)

    ps = nc.alloc_psum_tensor("ps", [128, 8 * BK], F32)

    def bank(b, lo=0, n=BK, p0=0, pn=128):
        return ps[p0:p0 + pn, b * BK + lo: b * BK + lo + n]

    nc.const_aps.register(F32, 0.125, stat[0:128, 113:114])
    pl = Plan(nc)
    s_x4 = [pl.sem(f"s_x{i}") for i in range(4)]
    s_casm = pl.sem("s_casm")
    s_ca1 = [pl.sem(f"s_ca1_{i}") for i in range(E)]
    s_ca2 = [pl.sem(f"s_ca2_{i}") for i in range(E)]
    s_caq = pl.sem("s_caq")
    s_sml = pl.sem("s_sml")
    s_w1l = pl.sem("s_w1l")
    s_w2l = pl.sem("s_w2l")
    s_xp = [[pl.sem(f"s_xp{q}{p}") for p in range(2)] for q in range(2)]
    s_pe = pl.sem("s_pe")
    s_sc = pl.sem("s_sc")
    s_ve = pl.sem("s_ve")
    s_st = pl.sem("s_st")
    s_cs = pl.sem("s_cs")
    s_vsl = pl.sem("s_vsl")
    s_wsl = [pl.sem("s_wsl0"), pl.sem("s_wsl1")]
    s_wpj = pl.sem("s_wpj")

    tok_x4 = []
    for g in range(4):
        tok_x = pl.op("gpsimd", lambda e, g=g: e.dma_start(
            AP(x_sb, g * 4 * D, [[NT * D, 128], [D, 4], [1, D]]),
            AP(x_in, g * 4 * 128 * D, [[D, 128], [128 * D, 4], [1, D]])),
            sem=s_x4[g], dma=True)
        tok_x4.append(tok_x)
    tok_x = tok_x4[-1]
    tok_x_tiles = [tok_x4[c // 4] for c in range(NT)]

    def slow(e, dst, src):
        return e.dma_start(dst, src, allow_slow_non_contiguous=True)
    tok_b1 = pl.op("sync", lambda e: slow(e,
        AP(b1sb, 0, [[E * NHT, 128], [NHT, E], [1, NHT]]),
        AP(b1, 0, [[1, 128], [HF, E], [128, NHT]])), sem=s_sml, dma=True)
    pl.op("sync", lambda e: e.dma_start(ident[:, :], ident_in.ap()), sem=s_sml, dma=True)
    pl.op("sync", lambda e: e.dma_start(ones[:, :], ones_in.ap()), sem=s_sml, dma=True)
    tok_small = pl.op("sync", lambda e: e.dma_start(wgf[:, :], Wg.ap()), sem=s_sml, dma=True)
    v_small = tok_small[1]

    pl.op("gpsimd", lambda e: e.dma_start(b2bf[:, :], b2.ap()), sem=s_casm, dma=True)
    tok_sc2 = pl.op("gpsimd", lambda e: e.dma_start(
        bgbf[:, :], AP(bg, 0, [[E, 1], [1, E]])), sem=s_casm, dma=True)
    v_smallcast = tok_sc2[1]

    v_wcast1 = [None] * E
    v_wcast2 = [None] * E

    def emit_cast(ee, pace_waits=()):
        tok = None
        for dt in range(ND):
            tok = pl.op("gpsimd", lambda e, ee=ee, dt=dt: e.dma_start(
                AP(W1s, (ee * ND + dt) * HF * 128, [[128, HF], [1, 128]]),
                AP(W1, ee * HF * D + dt * 128, [[D, HF], [1, 128]])),
                sem=s_ca1[ee], dma=True,
                waits=(pace_waits if dt == 0 else []))
        v_wcast1[ee] = tok
        for ht in range(NHT):
            tok = pl.op("gpsimd", lambda e, ee=ee, ht=ht: e.dma_start(
                AP(W2s, (ee * NHT + ht) * D * 128, [[128, D], [1, 128]]),
                AP(W2, ee * D * HF + ht * 128, [[HF, D], [1, 128]])),
                sem=s_ca2[ee], dma=True)
        v_wcast2[ee] = tok

    emit_cast(0)
    emit_cast(1)
    emit_cast(2)

    v_qkvcast = None

    def emit_qkv_casts(pace_waits=()):
        nonlocal v_qkvcast
        pl.op("gpsimd", lambda e: e.dma_start(
            AP(qkvs, 0, [[3 * D, D], [1, 3 * D]]),
            AP(w_qkv, 0, [[3 * D, D], [1, 3 * D]])), sem=s_caq, dma=True,
            waits=pace_waits)
        tok = pl.op("gpsimd", lambda e: e.dma_start(
            AP(wprojs, 0, [[D, D], [1, D]]),
            AP(w_proj, 0, [[D, D], [1, D]])), sem=s_caq, dma=True)
        v_qkvcast = tok[1]
    if not cfg.get("ffn1"):
        if cfg.get("attn"):
            emit_qkv_casts()

    for dt in range(ND):
        pl.use_bank("tensor", 6)
        tokt = pl.op("tensor", lambda e, dt=dt: e.transpose(
            bank(6, 0, 8), wgf[0:8, dt * 128:(dt + 1) * 128],
            ident[0:8, 0:8]),
            sem=s_pe, waits=[(s_sml, v_small)])
        toke = pl.op("vector", lambda e, dt=dt: e.tensor_copy(
            wgtsb[:, dt * E:(dt + 1) * E], bank(6, 0, 8)), sem=s_ve,
            waits=[tokt])
        pl.free_bank(6, toke)
    v_wgt = s_ve.val

    def layernorm(x_tiles, xnT_free):
        SUM, SSQ, MM, RS, NMRS = 0, 16, 32, 48, 64
        if len(x_tiles) != NT:
            x_tiles = [x_tiles[-1] if x_tiles else None] * NT
        for w in xnT_free:
            if w is not None:
                pl.wait("sync", w[0], w[1])
        sq_toks = []
        v_tr = [None] * NT
        v_trpe = [None] * NT
        for g in range(4):
            cs = range(4 * g, 4 * g + 4)
            for c in cs:
                w = [] if len(sq_toks) < 2 else [sq_toks[-2]]
                if x_tiles[c] is not None:
                    w.append(x_tiles[c])
                tok_sq = pl.op("scalar", lambda e, c=c: e.activation(
                    junk2[c % 2][:, :], x_sb[:, c * D:(c + 1) * D], AF.Square,
                    accum_out=stat[:, SSQ + c:SSQ + c + 1]), sem=s_sc, waits=w)
                sq_toks.append(tok_sq)
            o = 4 * g

            def vo(b, waits=()):
                tok = pl.op("vector", b, sem=s_ve, waits=waits)
                pl.wait("vector", s_ve, tok[1])
                return tok
            wsum = [x_tiles[o + 3]] if x_tiles[o + 3] is not None else []
            vo(lambda e, o=o: e.reduce_sum(
                stat[:, SUM + o:SUM + o + 4],
                x_sb[:, o * D:(o + 4) * D].rearrange("p (c j) -> p c j", j=D),
                axis=AX.X), waits=wsum)
            vo(lambda e, o=o: e.tensor_scalar(
                stat[:, MM + o:MM + o + 4], stat[:, SUM + o:SUM + o + 4],
                1.0 / D, None, ALU.mult))
            vo(lambda e, o=o: e.scalar_tensor_tensor(
                stat[:, RS + o:RS + o + 4], stat[:, MM + o:MM + o + 4], -1.0,
                stat[:, MM + o:MM + o + 4], ALU.mult, ALU.mult))
            vo(lambda e, o=o: e.scalar_tensor_tensor(
                stat[:, RS + o:RS + o + 4], stat[:, SSQ + o:SSQ + o + 4], 1.0 / D,
                stat[:, RS + o:RS + o + 4], ALU.mult, ALU.add),
                waits=[sq_toks[o + 3]])
            vo(lambda e, o=o: e.tensor_scalar(
                stat[:, RS + o:RS + o + 4], stat[:, RS + o:RS + o + 4], EPS,
                None, ALU.add))
            tok_rcp = vo(lambda e, o=o: e.reciprocal(
                stat[:, RS + o:RS + o + 4], stat[:, RS + o:RS + o + 4]))
            tok_rs = pl.op("scalar", lambda e, o=o: e.activation(
                stat[:, RS + o:RS + o + 4], stat[:, RS + o:RS + o + 4], AF.Sqrt),
                sem=s_sc, waits=[tok_rcp])
            tok_nmrs = vo(lambda e, o=o: e.scalar_tensor_tensor(
                stat[:, NMRS + o:NMRS + o + 4], stat[:, MM + o:MM + o + 4], -1.0,
                stat[:, RS + o:RS + o + 4], ALU.mult, ALU.mult), waits=[tok_rs])
            for c in cs:
                par = c % 2
                waits = [tok_nmrs]
                if c >= 2:
                    waits.append(v_trpe[c - 2])
                tok_app = pl.op("vector", lambda e, c=c: e.tensor_scalar(
                    xnbuf[c % 2][:, :], x_sb[:, c * D:(c + 1) * D],
                    stat[:, RS + c:RS + c + 1], stat[:, NMRS + c:NMRS + c + 1],
                    ALU.mult, ALU.add), sem=s_ve, waits=waits)
                b0_ = 4 if c % 2 == 0 else 6
                pl.use_bank("tensor", b0_)
                pl.use_bank("tensor", b0_ + 1)
                for dt in range(ND):
                    tok_tp = pl.op("tensor", lambda e, c=c, dt=dt, b0_=b0_: e.transpose(
                        ps[:, b0_ * BK + dt * 128: b0_ * BK + (dt + 1) * 128],
                        xnbuf[c % 2][:, dt * 128:(dt + 1) * 128],
                        ident[:, :]), sem=s_pe,
                        waits=[tok_app] if dt == 0 else [])
                v_trpe[c] = tok_tp
                oap = AP(xnT, c * 128, [[ND * T, 128], [T, ND], [1, 128]])
                tok_ev = pl.op("vector", lambda e, oap=oap, b0_=b0_: e.tensor_copy(
                    oap, ps[:, b0_ * BK: b0_ * BK + 1024]), sem=s_ve, waits=[tok_tp])
                pl.free_bank(b0_, tok_ev)
                pl.free_bank(b0_ + 1, tok_ev)
                v_tr[c] = [tok_ev]
        return v_tr, sq_toks[-1]

    def ffn(tok_xnT, region_toks=(), region_toks2=None, call1=False):
        st_gate = {}

        def emit_gate():
            pl.wait("tensor", s_ve, v_wgt)
            pl.wait("tensor", s_casm, v_smallcast)
            pl.use_bank("tensor", 6)
            pl.use_bank("tensor", 7)
            for c in range(NT):
                pl.wait_toks("tensor", tok_xnT[c])
                for dt in range(ND):
                    pl.op("tensor", lambda e, c=c, dt=dt: e.matmul(
                        bank(6, c * E, E), xnT[:, dt * T + c * 128: dt * T + (c + 1) * 128],
                        wgtsb[:, dt * E:(dt + 1) * E], start=(dt == 0), stop=False), sem=s_pe)
                tok_g = pl.op("tensor", lambda e, c=c: e.matmul(
                    bank(6, c * E, E), ones[0:1, 0:128], bgbf[0:1, :],
                    start=False, stop=True), sem=s_pe)

            G0, G1 = 80, 96
            gv = bank(6, 0, NT * E).rearrange("p (c e) -> p c e", e=E)
            g2 = bank(7, 0, NT * E).rearrange("p (c e) -> p c e", e=E)

            def vo(b, waits=()):
                tok = pl.op("vector", b, sem=s_ve, waits=waits)
                pl.wait("vector", s_ve, tok[1])
                return tok

            vo(lambda e: e.reduce_max(stat[:, G0:G0 + NT], gv, axis=AX.X), waits=[tok_g])
            vo(lambda e: e.tensor_tensor(
                comb[:, :].rearrange("p (c e) -> p c e", e=E), gv,
                bc(stat[:, G0:G0 + NT], E), ALU.is_equal))
            vo(lambda e: e.scalar_tensor_tensor(
                bank(7, 0, NT * E), comb[:, :], -1e30, bank(6, 0, NT * E),
                ALU.mult, ALU.add))
            vo(lambda e: e.reduce_max(stat[:, G1:G1 + NT], g2, axis=AX.X))
            vo(lambda e: e.tensor_tensor(
                bank(6, 0, NT * E).rearrange("p (c e) -> p c e", e=E), g2,
                bc(stat[:, G1:G1 + NT], E), ALU.is_equal))
            tok_d = vo(lambda e: e.tensor_sub(
                stat[:, G1:G1 + NT], stat[:, G1:G1 + NT], stat[:, G0:G0 + NT]))
            tok_exp = pl.op("scalar", lambda e: e.activation(
                stat[:, G1:G1 + NT], stat[:, G1:G1 + NT], AF.Exp), sem=s_sc,
                waits=[tok_d])
            vo(lambda e: e.tensor_scalar(
                stat[:, G0:G0 + NT], stat[:, G1:G1 + NT], 1.0, None, ALU.add),
                waits=[tok_exp])
            vo(lambda e: e.reciprocal(stat[:, G0:G0 + NT], stat[:, G0:G0 + NT]))
            vo(lambda e: e.tensor_scalar(
                stat[:, G0:G0 + NT], stat[:, G0:G0 + NT], 0.5, None, ALU.mult))
            vo(lambda e: e.tensor_scalar(
                stat[:, G1:G1 + NT], stat[:, G0:G0 + NT], -1.0, 0.5, ALU.mult, ALU.add))
            vo(lambda e: e.tensor_tensor(
                comb[:, :].rearrange("p (c e) -> p c e", e=E),
                comb[:, :].rearrange("p (c e) -> p c e", e=E),
                bc(stat[:, G0:G0 + NT], E), ALU.mult))
            vo(lambda e: e.tensor_tensor(
                bank(6, 0, NT * E).rearrange("p (c e) -> p c e", e=E),
                bank(6, 0, NT * E).rearrange("p (c e) -> p c e", e=E),
                bc(stat[:, G1:G1 + NT], E), ALU.mult))
            tok_cb = vo(lambda e: e.tensor_add(comb[:, :], comb[:, :], bank(6, 0, NT * E)))
            pl.free_bank(6, tok_cb)
            pl.free_bank(7, tok_cb)

            pl.wait("tensor", tok_cb[0], tok_cb[1])
            for b in range(4):
                pl.use_bank("tensor", b)
            for c in range(NT):
                tok_t = pl.op("tensor", lambda e, c=c: e.transpose(
                    ps[0:8, c * 128:(c + 1) * 128],
                    comb[:, c * E:(c + 1) * E],
                    ident[:, :]), sem=s_pe)
            tok_cT = pl.op("vector", lambda e: e.tensor_copy(
                combT[0:8, :], ps[0:8, 0:T]), sem=s_ve, waits=[tok_t])
            for b in range(4):
                pl.free_bank(b, tok_cT)

            pl.wait("tensor", tok_cT[0], tok_cT[1])
            for tt in range(NT):
                for dc in range(2):
                    b = 6 + (tt * 2 + dc) % 2
                    pl.use_bank("tensor", b)
                    tok_m = pl.op("tensor", lambda e, tt=tt, dc=dc, b=b: e.matmul(
                        bank(b), combT[0:8, tt * 128:(tt + 1) * 128],
                        b2bf[0:8, dc * 512:(dc + 1) * 512], start=True, stop=True),
                        sem=s_pe)
                    xap = x_sb[:, tt * D + dc * 512: tt * D + dc * 512 + 512]
                    tok_a = pl.op("vector", lambda e, b=b, xap=xap: e.tensor_add(
                        xap, xap, bank(b)), sem=s_pe if False else s_ve, waits=[tok_m])
                    pl.free_bank(b, tok_a)

            st_gate["tok_cT"] = tok_cT

        v_w1 = [None] * E
        v_w2 = [None] * E
        tok_m1_done = [None] * E
        tok_m2_done = [None] * E
        relu_tok = {}
        x_tile_tok = [None] * NT
        for ee in range(E):

            w = [v_wcast1[ee]]
            if ee >= 1:
                w.append(tok_m1_done[ee - 1])
            else:
                w.extend(region_toks)
            for dt in range(ND):
                tok = pl.op("sync", lambda e, ee=ee, dt=dt: e.dma_start(
                    w1t[:, dt * HF:(dt + 1) * HF],
                    AP(W1s, (ee * ND + dt) * HF * 128, [[128, HF], [1, 128]]),
                    transpose=True), sem=s_w1l, dma=True, waits=w)
                w = []
            v_w1[ee] = tok[1]
            w = [v_wcast2[ee]]
            if ee >= 1:
                w.append(tok_m2_done[ee - 1])
            else:
                w.extend(region_toks2 if region_toks2 is not None else region_toks)
            for ht in range(NHT):
                tok = pl.op("sync", lambda e, ee=ee, ht=ht: e.dma_start(
                    w2t[:, ht * D:(ht + 1) * D],
                    AP(W2s, (ee * NHT + ht) * D * 128, [[128, D], [1, 128]]),
                    transpose=True), sem=s_w2l, dma=True, waits=w)
                w = []
            v_w2[ee] = tok[1]

            for tc in range(NTC):
                pl.wait("tensor", s_w1l, v_w1[ee])
                for cc in range(4 * tc, 4 * tc + 4):
                    pl.wait_toks("tensor", tok_xnT[cc])
                for ht in range(NHT):
                    b = ht % 4
                    pl.use_bank("tensor", b)
                    for dt in range(ND):
                        tok_m = pl.op("tensor", lambda e, ht=ht, tc=tc, dt=dt, b=b: e.matmul(
                            bank(b),
                            w1t[:, dt * HF + ht * 128: dt * HF + (ht + 1) * 128],
                            xnT[:, dt * T + tc * TC: dt * T + (tc + 1) * TC],
                            start=(dt == 0), stop=(dt == ND - 1)), sem=s_pe)
                    bcol = b1sb[:, ee * NHT + ht: ee * NHT + ht + 1]
                    oap = hT[:, ht * TC:(ht + 1) * TC]
                    if ht % 2 == 0:
                        tok_r = pl.op("scalar", lambda e, oap=oap, b=b, bcol=bcol:
                                      e.activation(oap, bank(b), AF.Relu, bias=bcol),
                                      sem=s_sc, waits=[tok_m])
                    else:
                        tok_r = pl.op("vector", lambda e, oap=oap, b=b, bcol=bcol:
                                      e.tensor_scalar(oap, bank(b), bcol, 0.0,
                                                      ALU.add, ALU.max),
                                      sem=s_ve, waits=[tok_m])
                    relu_tok[(tc, ht)] = tok_r
                    pl.free_bank(b, tok_r)
                tok_m1_done[ee] = tok_m
                if ee == 0 and tc == 0:
                    emit_gate()
                if call1 and tc == 0:
                    if ee >= 1 and ee + 2 < E:
                        emit_cast(ee + 2, pace_waits=[tok_m])
                    if ee == 2 and cfg.get("attn"):
                        emit_qkv_casts(pace_waits=[tok_m])

                pl.wait("tensor", s_w2l, v_w2[ee])
                for msub in range(TC // 128):
                    tt = tc * (TC // 128) + msub
                    for dc in range(2):
                        b = 4 + (msub * 2 + dc) % 2
                        pl.use_bank("tensor", b)
                        for ht in range(NHT):
                            waits = [relu_tok[(tc, ht)]] if msub == 0 and dc == 0 else []
                            tok_m = pl.op("tensor", lambda e, ht=ht, msub=msub, dc=dc, b=b, tc=tc: e.matmul(
                                bank(b),
                                hT[:, ht * TC + msub * 128: ht * TC + (msub + 1) * 128],
                                w2t[:, ht * D + dc * 512: ht * D + (dc + 1) * 512],
                                start=(ht == 0), stop=(ht == NHT - 1)), sem=s_pe,
                                waits=waits)
                        xap = x_sb[:, tt * D + dc * 512: tt * D + dc * 512 + 512]
                        tok_y = pl.op("vector", lambda e, b=b, xap=xap, col=tt * E + ee:
                                      e.scalar_tensor_tensor(
                                          xap, bank(b), comb[:, col:col + 1], xap,
                                          ALU.mult, ALU.add), sem=s_ve, waits=[tok_m])
                        pl.free_bank(b, tok_y)
                        if ee == E - 1:
                            x_tile_tok[tt] = tok_y
                tok_m2_done[ee] = tok_m

        return dict(x_done=x_tile_tok[NT - 1], x_tiles=x_tile_tok,
                    m1_last=tok_m1_done[E - 1], m2_last=tok_m2_done[E - 1])


    def attn(tok_xnT, tok_ffn1, tok_lnsq):
        SHIFTC = 112

        tok_cs = None
        for i, tgt in ((2, cosu), (3, sinu)):
            tok_cs = pl.op("gpsimd", lambda e, i=i, tgt=tgt: e.dma_start(
                tgt[:, :], AP(coss, i * 128 * T, [[T, 128], [1, T]])),
                sem=s_cs, dma=True,
                waits=[tok_ffn1["m1_last"], (s_caq, v_qkvcast)] if i == 2 else [])
        pl.op("vector", lambda e: e.memset(stat[:, SHIFTC:SHIFTC + 1], EXPSHIFT), sem=s_ve)
        tok_shift = pl.op("vector", lambda e: e.memset(stat[:, 113:114], 0.125), sem=s_ve)

        tok_vones = pl.op("vector", lambda e: e.memset(
            AP(vh, 64, [[NT * 520, 128], [520, NT], [65, 8], [1, 1]]), 1.0),
            sem=s_ve, waits=[tok_ffn1["m2_last"]])

        def v_half(half, extra_waits):
            toks = []
            for grp in range(2):
                tok_l = pl.op("sync", lambda e, half=half, grp=grp: e.dma_start(
                    wsl[:, grp * 2048:(grp + 1) * 2048],
                    AP(qkvs, 2048 + half * 512 + grp * 256,
                       [[3 * D, 128], [128 * 3 * D, ND], [1, 256]]),
                    transpose=False), sem=s_vsl, dma=True,
                    waits=(extra_waits if grp == 0 else []))
            v_ld = tok_l[1]
            last_ev = None
            for tt in range(NT):
                pl.wait_toks("tensor", tok_xnT[tt])
                for grp in range(2):
                    b = 6 + grp
                    pl.use_bank("tensor", b)
                    for dt in range(ND):
                        tok_m = pl.op("tensor", lambda e, tt=tt, grp=grp, dt=dt, b=b: e.matmul(
                            bank(b, 0, 256),
                            xnT[:, dt * T + tt * 128: dt * T + (tt + 1) * 128],
                            wsl[:, grp * 2048 + dt * 256: grp * 2048 + (dt + 1) * 256],
                            start=(dt == 0), stop=(dt == ND - 1)), sem=s_pe,
                            waits=[(s_vsl, v_ld)] if (tt == 0 and dt == 0) else [])
                    oap = AP(vh, tt * 520 + grp * 4 * 65,
                             [[NT * 520, 128], [65, 4], [1, 64]])
                    iap = bank(b, 0, 256).rearrange("p (h c) -> p h c", c=64)
                    last_ev = pl.op("vector", lambda e, oap=oap, iap=iap: e.tensor_copy(
                        oap, iap), sem=s_ve, waits=[tok_m])
                    pl.free_bank(b, last_ev)
            return last_ev

        tok_vA = v_half(0, [tok_ffn1["m2_last"], (s_caq, v_qkvcast), tok_vones, tok_lnsq])

        avq_tok = [[None] * 4, [None] * 4]
        ev_prev = [[None] * 4, [None] * 4]
        rl_pend = [None, None]
        norm_pend = [None, None]
        tok_qk_done = [None] * 9
        tok_rope_done = [None] * 9
        tok_sc_done = [None] * 8
        tok_norm = None
        tok_vB = [None]

        def emit_chunk_qk(hc, extra_w=()):
            par = hc % 2
            w = list(extra_w)
            if hc >= 1:
                w.append(tok_qk_done[hc - 1])
            if hc == 0:
                w.append(tok_vA)
            if hc == 5 and tok_vB[0] is not None:
                w.append(tok_vB[0])
            tok_l = None
            for qd in (0, 2):
                base_c = hc * 128 if qd == 0 else D + hc * 128
                tok_l = pl.op("sync", lambda e, qd=qd, base_c=base_c: e.dma_start(
                    wsl[:, qd * 1024:(qd + 1) * 1024],
                    AP(qkvs, base_c, [[3 * D, 128], [128 * 3 * D, ND], [1, 128]])),
                    sem=s_wsl[par], dma=True, waits=(w if qd == 0 else []))
            v_wsl = tok_l[1]
            tok_rot = None
            for qsrc, qdst in ((0, 1), (2, 3)):
                for half in range(2):
                    oap = AP(wsl, qdst * 1024 + half * 32,
                             [[4 * ND * 128, 128], [128, ND], [64, 2], [1, 32]])
                    iap = AP(wsl, qsrc * 1024 + (32 if half == 0 else 0),
                             [[4 * ND * 128, 128], [128, ND], [64, 2], [1, 32]])
                    tok_rot = pl.op("vector", lambda e, oap=oap, iap=iap: e.tensor_copy(
                        oap, iap), sem=s_ve,
                        waits=[(s_wsl[par], v_wsl)])
            tok_mq = None
            rope_waits = []
            if hc >= 2:
                rope_waits.append(tok_sc_done[hc - 2])
            last_rope = None
            for part_i, (cost, sint, dest) in enumerate((
                    (cosu, sinu, qrope[par]), (cosu, sinu, krope[par]))):
                for qq in range(4):
                    pl.use_bank("tensor", 6)
                    pl.use_bank("tensor", 7)
                    for cc in range(4 * qq, 4 * qq + 4):
                        pl.wait_toks("tensor", tok_xnT[cc])
                    for dt in range(ND):
                        wts = []
                        if part_i == 0 and qq == 0 and dt == 0:
                            wts = [(s_wsl[par], v_wsl), (s_cs, tok_cs[1]), tok_rot]
                            wts.extend(rope_waits)
                        tok_mq = pl.op("tensor", lambda e, qq=qq, dt=dt, pi=part_i: e.matmul(
                            bank(6),
                            wsl[:, (2 * pi) * 1024 + dt * 128:(2 * pi) * 1024 + (dt + 1) * 128],
                            xnT[:, dt * T + qq * 512: dt * T + (qq + 1) * 512],
                            start=(dt == 0), stop=(dt == ND - 1)), sem=s_pe, waits=wts)
                    for dt in range(ND):
                        tok_mq = pl.op("tensor", lambda e, qq=qq, dt=dt, pi=part_i: e.matmul(
                            bank(7),
                            wsl[:, (2 * pi + 1) * 1024 + dt * 128:(2 * pi + 1) * 1024 + (dt + 1) * 128],
                            xnT[:, dt * T + qq * 512: dt * T + (qq + 1) * 512],
                            start=(dt == 0), stop=(dt == ND - 1)), sem=s_pe)
                    wts = [tok_mq] + (rope_waits if part_i == 0 and qq == 0 else [])
                    tma, tmb = tmpah[hc], tmpbh[hc]
                    tok_a = pl.op("vector", lambda e, qq=qq, cost=cost, tma=tma: e.tensor_tensor(
                        tma[:, :], bank(6), cost[:, qq * 512:(qq + 1) * 512],
                        ALU.mult), sem=s_ve, waits=wts + ([(s_ve, last_rope[1])] if last_rope else []))
                    tok_b = pl.op("vector", lambda e, qq=qq, sint=sint, tmb=tmb: e.tensor_tensor(
                        tmb[:, :], bank(7), sint[:, qq * 512:(qq + 1) * 512],
                        ALU.mult), sem=s_ve)
                    pl.free_bank(6, tok_a)
                    pl.free_bank(7, tok_b)
                    pl.wait("vector", s_ve, tok_b[1])
                    last_rope = pl.op("vector", lambda e, qq=qq, dest=dest, tma=tma, tmb=tmb: e.tensor_add(
                        dest[:, qq * 512:(qq + 1) * 512], tma[:, :], tmb[:, :]),
                        sem=s_ve)
            tok_qk_done[hc] = tok_mq
            tok_rope_done[hc] = last_rope

        def scores_chunk(hc):
            nonlocal tok_norm
            par = hc % 2
            at_b = [atA, atB]

            def avq(hp, j, g, tok_exp):
                atb = at_b[hp]
                bo = (4 if hp == 0 else 6) + (j % 2)
                hh = (2 * hc + hp) % 8
                if g == 0:
                    pl.use_bank("tensor", bo)
                tok = None
                for tki in range(4):
                    tk = 4 * g + tki
                    wts = [tok_exp] if tki == 0 else []
                    if tki == 0 and g == 0 and j == 0 and hp == 0 and hc >= 4:
                        wts.append(tok_vB[0])
                    tok = pl.op("tensor", lambda e, tk=tk, hh=hh, bo=bo, atb=atb, g=g, j=j: e.matmul(
                        bank(bo, 0, 256, 0, 65),
                        vh[:, tk * 520 + hh * 65: tk * 520 + hh * 65 + 65],
                        atb[:, tk * 256:(tk + 1) * 256],
                        start=(tk == 0), stop=(tk == 15)), sem=s_pe, waits=wts)
                avq_tok[hp][g] = tok
                if g == 3:
                    ro = 512 * hp + 256 * (j % 2)
                    tok_rl = pl.op("vector", lambda e, bo=bo, ro=ro: e.reciprocal(
                        rlbf[0:1, ro:ro + 256], bank(bo, 0, 256, 64, 1)), sem=s_ve,
                        waits=[tok])
                    pl.wait("vector", s_ve, tok_rl[1])
                    rl_pend[hp] = (j, tok_rl)

            def do_norm(hp, j, tok_rl):
                nonlocal tok_norm
                bo = (4 if hp == 0 else 6) + (j % 2)
                ro = 512 * hp + 256 * (j % 2)
                tok_rb = pl.op("tensor", lambda e, bo=bo, ro=ro: e.matmul(
                    bank(bo, 256, 256, 0, 64), ones[0:1, 0:64],
                    rlbf[0:1, ro:ro + 256],
                    start=True, stop=True), sem=s_pe, waits=[tok_rl, tok_shift])
                rbo = 256 * hp
                tok_rbe = pl.op("vector", lambda e, bo=bo, rbo=rbo: e.tensor_copy(
                    rbsb[:, rbo:rbo + 256], bank(bo, 256, 256, 0, 64)), sem=s_ve,
                    waits=[tok_rb])
                pl.wait("vector", s_ve, tok_rbe[1])
                tok_norm = pl.op("vector", lambda e, hp=hp, hc=hc, j=j, bo=bo, rbo=rbo: e.tensor_mul(
                    otall[64 * hp:64 * hp + 64,
                          hc * T + j * 256: hc * T + j * 256 + 256],
                    bank(bo, 0, 256, 0, 64), rbsb[:, rbo:rbo + 256]), sem=s_ve)
                pl.free_bank(bo, tok_norm)

            from collections import deque
            avail = [deque(), deque()]
            slot_now = 0
            for tqc in range(8):
                evs = [[], []]
                for g in range(4):
                    slot_now += 1
                    pl.use_bank("tensor", 0)
                    pl.use_bank("tensor", 1)
                    tok_s = None
                    first = True
                    for tki in range(4):
                        tk = 4 * g + tki
                        for hp in range(2):
                            base = (0 if hp == 0 else 2) * BK
                            wts = []
                            if g == 0 and tki == 0 and tqc == 0 and hp == 0:
                                wts = [tok_rope_done[hc]]
                            tok_s = pl.op("tensor", lambda e, tk=tk, tki=tki, base=base, tqc=tqc, hp=hp, par=par: e.matmul(
                                ps[:, base + tki * 256: base + tki * 256 + 256],
                                krope[par][64 * hp:64 * hp + 64, tk * 128:(tk + 1) * 128],
                                qrope[par][64 * hp:64 * hp + 64, tqc * 256:(tqc + 1) * 256],
                                start=True, stop=True), sem=s_pe, waits=wts)
                            if first:
                                pl.use_bank("tensor", 2)
                                pl.use_bank("tensor", 3)
                                first = False
                    for hp in range(2):
                        if avail[hp] and slot_now - avail[hp][0][0] >= 2:
                            sp, jq, qq_, te = avail[hp].popleft()
                            avq(hp, jq, qq_, te)
                    for hp in range(2):
                        base = (0 if hp == 0 else 2) * BK
                        ew = [tok_s, tok_shift]
                        if avq_tok[hp][g] is not None:
                            ew.append(avq_tok[hp][g])
                        atb = at_b[hp]
                        tok_e = pl.op("scalar", lambda e, g=g, base=base, atb=atb: e.activation(
                            atb[:, g * 1024:(g + 1) * 1024],
                            ps[:, base: base + 1024], AF.Exp,
                            bias=stat[:, SHIFTC:SHIFTC + 1], scale=0.125), sem=s_sc,
                            waits=ew)
                        evs[hp].append(tok_e)
                        pl.free_bank(0 + base // BK, tok_e)
                        pl.free_bank(1 + base // BK, tok_e)
                        avail[hp].append((slot_now, tqc, g, tok_e))
                    if g == 0 and norm_pend[0] is not None:
                        for hp in range(2):
                            jn, tok_rl = norm_pend[hp]
                            do_norm(hp, jn, tok_rl)
                        norm_pend[0] = norm_pend[1] = None
                    if g == 3:
                        for hp in range(2):
                            if rl_pend[hp] is not None:
                                norm_pend[hp] = rl_pend[hp]
                                rl_pend[hp] = None
            while avail[0] or avail[1]:
                for hp in range(2):
                    if avail[hp]:
                        sp, jq, qq_, te = avail[hp].popleft()
                        avq(hp, jq, qq_, te)
                for hp in range(2):
                    if norm_pend[hp] is not None:
                        jn, tok_rl = norm_pend[hp]
                        do_norm(hp, jn, tok_rl)
                        norm_pend[hp] = None
                    if rl_pend[hp] is not None:
                        norm_pend[hp] = rl_pend[hp]
                        rl_pend[hp] = None
            for hp in range(2):
                if norm_pend[hp] is not None:
                    jn, tok_rl = norm_pend[hp]
                    do_norm(hp, jn, tok_rl)
                    norm_pend[hp] = None
                if rl_pend[hp] is not None:
                    jn, tok_rl = rl_pend[hp]
                    do_norm(hp, jn, tok_rl)
                    rl_pend[hp] = None

        emit_chunk_qk(0)
        for hc in range(8):
            if hc + 1 < 8:
                emit_chunk_qk(hc + 1)
            scores_chunk(hc)
            tok_sc_done[hc] = (s_pe, s_pe.val)
            if hc == 3:
                tok_vB[0] = v_half(1, [tok_qk_done[4]])

        tok_bpj = pl.op("gpsimd", lambda e: e.dma_start(
            bprojbf[:, :], AP(b_proj, 0, [[D, 1], [1, D]])), sem=s_cs, dma=True,
            waits=[tok_sc_done[7]])
        tok_wpj = pl.op("sync", lambda e: e.dma_start(
            wpj[:, :], AP(wprojs, 0, [[D, 128], [128 * D, ND], [1, D]])),
            sem=s_wpj, dma=True,
            waits=[tok_sc_done[7], (s_caq, v_qkvcast)])
        last = None
        x_proj_tok = [None] * NT
        for tt in range(NT):
            for dc in range(2):
                b = (tt * 2 + dc) % 2
                pl.use_bank("tensor", b)
                for ct in range(ND):
                    wts = []
                    if tt == 0 and dc == 0 and ct == 0:
                        wts = [tok_norm, (s_wpj, tok_wpj[1]), tok_bpj]
                    tok_m = pl.op("tensor", lambda e, tt=tt, dc=dc, ct=ct, b=b: e.matmul(
                        bank(b), otall[:, ct * T + tt * 128: ct * T + (tt + 1) * 128],
                        wpj[:, ct * D + dc * 512: ct * D + (dc + 1) * 512],
                        start=(ct == 0), stop=False), sem=s_pe, waits=wts)
                tok_m = pl.op("tensor", lambda e, dc=dc, b=b: e.matmul(
                    bank(b), ones[0:1, 0:128], bprojbf[0:1, dc * 512:(dc + 1) * 512],
                    start=False, stop=True), sem=s_pe)
                xap = x_sb[:, tt * D + dc * 512: tt * D + dc * 512 + 512]
                last = pl.op("vector", lambda e, b=b, xap=xap: e.tensor_add(
                    xap, xap, bank(b)), sem=s_ve, waits=[tok_m])
                pl.free_bank(b, last)
                x_proj_tok[tt] = last
        return dict(x_done=last, x_tiles=x_proj_tok,
                    proj_done=(s_pe, s_pe.val),
                    rope_last=tok_rope_done[7], sc_last=tok_sc_done[7],
                    qk_last=tok_qk_done[7], vB=tok_vB[0])

    tok_xnT, _sq1 = layernorm(tok_x_tiles, [])
    r1 = ffn(tok_xnT, call1=True) if cfg.get("ffn1") else None
    if cfg.get("attn"):
        xr = r1["x_tiles"] if r1 else [tok_x]
        xnfree = [r1["m1_last"]] if r1 else []
        tok_xnT2, sq2 = layernorm(xr, xnfree)
        if r1 is None:
            emit_cast(0)
            emit_cast(1)
            r1 = dict(m1_last=(s_pe, 0), m2_last=(s_pe, 0))
        ra = attn(tok_xnT2, r1, sq2)
        if cfg.get("ffn2"):
            for ee in range(2, E):
                emit_cast(ee)
            tok_xnT3, _sq3 = layernorm(ra["x_tiles"], [ra["qk_last"], ra["vB"]])
            r2 = ffn(tok_xnT3, region_toks=[ra["proj_done"]],
                     region_toks2=[ra["proj_done"]])

    x_fin = r2["x_tiles"] if cfg.get("ffn2") else (
        [None] * NT if not cfg.get("ffn1") else r1["x_tiles"])
    for tt in range(NT):
        w = [x_fin[tt]] if x_fin[tt] is not None else [
            (s_ve, s_ve.val), (s_sc, s_sc.val), (s_pe, s_pe.val)]
        pl.op("gpsimd", lambda e, tt=tt: e.dma_start(
            AP(out, tt * 128 * D, [[D, 128], [1, D]]),
            AP(x_sb, tt * D, [[NT * D, 128], [1, D]])), sem=s_st, dma=True,
            waits=w)
    pl.op("gpsimd", lambda e: e.nop(), waits=[(s_st, s_st.val)])

    pl.finish()
    return nc


def _rope_tables_host():
    try:
        import jax
        import jax.numpy as jnp
        cpu = [d for d in jax.devices("cpu")]
        with jax.default_device(cpu[0]):
            inv = 1.0 / 10000.0 ** (jnp.arange(0, DH, 2, dtype=jnp.float32) / DH)
            t = jnp.arange(T, dtype=jnp.float32)
            freqs = t[:, None] * inv[None, :]
            cos = np.asarray(jnp.cos(freqs), dtype=np.float32)
            sin = np.asarray(jnp.sin(freqs), dtype=np.float32)
            return cos, sin
    except Exception:
        expo = np.arange(0, DH, 2, dtype=np.float32) / np.float32(DH)
        inv = np.float32(1.0) / np.power(np.float32(10000.0), expo)
        t = np.arange(T, dtype=np.float32)
        ang = t[:, None] * inv[None, :]
        return np.cos(ang), np.sin(ang)


def make_consts():
    cos, sin = _rope_tables_host()
    p = np.arange(128)
    cosf = cos[:, p % 32].T
    sinf = sin[:, p % 32].T
    sgn = np.where((p % 64) < 32, -1.0, 1.0).astype(np.float32)[:, None]
    coss = np.stack([cosf / 8.0, (sinf * sgn) / 8.0, cosf, sinf * sgn])
    ident = np.eye(128, dtype=np.float32)
    onesm = np.ones((128, 128), dtype=ml_dtypes.bfloat16)
    return coss.astype(np.float32), ident, onesm


_NC_CACHE = {}


def _get_nc():
    if "nc" not in _NC_CACHE:
        _NC_CACHE["nc"] = build({"ffn1": True, "attn": True, "ffn2": True})
    return _NC_CACHE["nc"]


def kernel(L0_attn_x, L0_attn_w_qkv, L0_attn_w_proj, L0_attn_b_proj,
           L0_ffn_W1, L0_ffn_b1, L0_ffn_W2, L0_ffn_b2, L0_ffn_Wg, L0_ffn_bg):
    from concourse.bass_utils import run_bass_kernel_spmd
    x = np.ascontiguousarray(np.asarray(L0_attn_x, dtype=np.float32))
    coss, ident, onesm = make_consts()
    shared = dict(
        w_qkv=np.ascontiguousarray(np.asarray(L0_attn_w_qkv, dtype=np.float32)),
        w_proj=np.ascontiguousarray(np.asarray(L0_attn_w_proj, dtype=np.float32)),
        b_proj=np.ascontiguousarray(np.asarray(L0_attn_b_proj, dtype=np.float32)),
        W1=np.ascontiguousarray(np.asarray(L0_ffn_W1, dtype=np.float32)),
        b1=np.ascontiguousarray(np.asarray(L0_ffn_b1, dtype=np.float32)),
        W2=np.ascontiguousarray(np.asarray(L0_ffn_W2, dtype=np.float32)),
        b2=np.ascontiguousarray(np.asarray(L0_ffn_b2, dtype=np.float32)),
        Wg=np.ascontiguousarray(np.asarray(L0_ffn_Wg, dtype=np.float32)),
        bg=np.ascontiguousarray(np.asarray(L0_ffn_bg, dtype=np.float32)),
        coss=coss, ident_in=ident, ones_in=onesm,
    )
    n_cores = 8
    in_maps = [dict(shared, x_in=np.ascontiguousarray(x[i])) for i in range(n_cores)]
    nc = _get_nc()
    res = run_bass_kernel_spmd(nc, in_maps, core_ids=list(range(n_cores)))
    return np.stack([np.asarray(r["out"], dtype=np.float32) for r in res.results])


# version 13
# speedup vs baseline: 1.1265x; 1.0039x over previous
import numpy as np
import ml_dtypes
import concourse.bass as bass
import concourse.mybir as mybir

F32 = mybir.dt.float32
BF16 = mybir.dt.bfloat16
AF = mybir.ActivationFunctionType
ALU = mybir.AluOpType
AX = mybir.AxisListType
AP = bass.AP

T, D, HF, E, NH, DH = 2048, 1024, 2048, 8, 16, 64
NT, ND, NHT = T // 128, D // 128, HF // 128
TC = 512
NTC = T // TC
EPS = 1e-5
EXPSHIFT = -20.0
BK = 512


def bc(ap, n):
    return AP(ap.tensor, ap.offset, [list(d) for d in ap.ap] + [[0, n]])


class SemC:
    __slots__ = ("h", "val")

    def __init__(self, h):
        self.h = h
        self.val = 0


class Plan:
    ENGS = ("tensor", "vector", "scalar", "sync", "gpsimd")

    def __init__(self, nc):
        self.nc = nc
        self.ops = {e: [] for e in self.ENGS}
        self.waited = {e: {} for e in self.ENGS}
        self.bank_free = {}

    def sem(self, name):
        return SemC(self.nc.alloc_semaphore(name))

    def wait(self, eng, s, v):
        if s is None or v is None or v <= 0:
            return
        if self.waited[eng].get(s.h.num, -1) >= v:
            return
        self.waited[eng][s.h.num] = v
        self.ops[eng].append(lambda e, h=s.h, v=v: e.wait_ge(h, v))

    def op(self, eng, build, sem=None, waits=(), dma=False):
        for w in waits:
            if w is None:
                continue
            self.wait(eng, w[0], w[1])
        if sem is not None:
            inc = 16 if dma else 1
            sem.val += inc
            self.ops[eng].append(lambda e, b=build, h=sem.h, i=inc: b(e).then_inc(h, i))
            return (sem, sem.val)
        self.ops[eng].append(build)
        return None

    def wait_toks(self, eng, toks):
        if toks is None:
            return
        if isinstance(toks, tuple):
            toks = [toks]
        for t in toks:
            if t is not None:
                self.wait(eng, t[0], t[1])

    def use_bank(self, eng, b):
        w = self.bank_free.get(b)
        if w is not None:
            self.wait(eng, w[0], w[1])

    def free_bank(self, b, tok):
        self.bank_free[b] = tok

    def finish(self):
        with self.nc.allow_low_precision("bf16 attn l-reciprocal (within tolerance)"), \
             self.nc.Block() as block:
            for eng in self.ENGS:
                def runner(e, eng=eng):
                    for f in self.ops[eng]:
                        f(e)
                getattr(block, eng)(runner)


def build(cfg):
    nc = bass.Bass("TRN2", target_bir_lowering=False, debug=False)

    x_in = nc.dram_tensor("x_in", [T, D], F32, kind="ExternalInput")
    w_qkv = nc.dram_tensor("w_qkv", [D, 3 * D], F32, kind="ExternalInput")
    w_proj = nc.dram_tensor("w_proj", [D, D], F32, kind="ExternalInput")
    b_proj = nc.dram_tensor("b_proj", [D], F32, kind="ExternalInput")
    W1 = nc.dram_tensor("W1", [E, HF, D], F32, kind="ExternalInput")
    b1 = nc.dram_tensor("b1", [E, HF], F32, kind="ExternalInput")
    W2 = nc.dram_tensor("W2", [E, D, HF], F32, kind="ExternalInput")
    b2 = nc.dram_tensor("b2", [E, D], F32, kind="ExternalInput")
    Wg = nc.dram_tensor("Wg", [E, D], F32, kind="ExternalInput")
    bg = nc.dram_tensor("bg", [E], F32, kind="ExternalInput")
    coss = nc.dram_tensor("coss", [4, 128, T], F32, kind="ExternalInput")
    ident_in = nc.dram_tensor("ident_in", [128, 128], F32, kind="ExternalInput")
    ones_in = nc.dram_tensor("ones_in", [128, 128], BF16, kind="ExternalInput")
    out = nc.dram_tensor("out", [T, D], F32, kind="ExternalOutput")

    W1s = nc.dram_tensor("W1s", [E, ND, HF, 128], BF16, kind="Internal")
    W2s = nc.dram_tensor("W2s", [E, NHT, D, 128], BF16, kind="Internal")
    qkvs = nc.dram_tensor("qkvs", [D, 3 * D], BF16, kind="Internal")
    wprojs = nc.dram_tensor("wprojs", [D, D], BF16, kind="Internal")

    AR = 212736
    arena = nc.alloc_sbuf_tensor("arena", [128, AR], mybir.dt.uint8)
    abase = nc.lookup_mloc(arena).addr

    def sb(name, shape, dtype, off):
        nbytes = int(np.prod(shape[1:])) * mybir.dt.size(dtype)
        assert off + nbytes <= AR, f"{name}: {off}+{nbytes} > {AR}"
        return nc.alloc_sbuf_tensor_at(name, shape, dtype, offset=abase + off)

    x_sb = sb("x_sb", [128, NT * D], F32, 0)
    M = 65536
    def psb(name, shape, dtype):
        nonlocal M
        t = sb(name, shape, dtype, M)
        nbytes = int(np.prod(shape[1:])) * mybir.dt.size(dtype)
        M += (nbytes + 31) // 32 * 32
        return t
    b1sb = psb("b1sb", [128, E * NHT], F32)
    comb = psb("comb", [128, NT * E], F32)
    b2bf = psb("b2bf", [8, D], BF16)
    bgbf = psb("bgbf", [1, E], BF16)
    wgtsb = psb("wgtsb", [128, ND * E], BF16)
    ident = psb("ident", [128, 128], F32)
    ones = psb("ones", [128, 128], BF16)
    stat = psb("stat", [128, 8 * NT], F32)
    rlbf = psb("rlbf", [1, 1024], BF16)
    rbsb = psb("rbsb", [64, 512], F32)
    assert M <= 75776, M
    xnT = sb("xnT", [128, ND * T], BF16, 75776)
    U = 108544
    w1t = sb("w1t", [128, ND * HF], BF16, U)
    w2t = sb("w2t", [128, NHT * D], BF16, U + 32768)
    hT = sb("hT", [128, NHT * TC], BF16, U + 65536)
    combT = sb("combT", [8, T], BF16, U + 81920)
    tmpah = [sb(f"tmpa{h}", [128, 512], F32, U + 57600 + h * 4096) for h in range(8)]
    tmpbh = [sb(f"tmpb{h}", [128, 512], F32, U + 59648 + h * 4096) for h in range(8)]
    cosu = sb("cosu", [128, T], BF16, U + 0)
    sinu = sb("sinu", [128, T], BF16, U + 4096)
    atA = sb("atA", [128, 16 * 256], BF16, U + 8192)
    atB = sb("atB", [128, 16 * 256], BF16, U + 16384)
    qrope = [sb(f"qrope{i}", [128, T], BF16, U + 24576 + 8192 * i) for i in range(2)]
    krope = [sb(f"krope{i}", [128, T], BF16, U + 28672 + 8192 * i) for i in range(2)]
    vh = sb("vh", [128, NT * 520], BF16, U + 40960)
    bprojbf = sb("bprojbf", [1, D], BF16, U + 8192)
    rlbf2 = None
    otall = sb("otall", [128, 8 * T], BF16, U + 57600)
    wsl = sb("wsl", [128, 4 * ND * 128], BF16, U + 90368)
    xnbuf = [sb("xnbuf0", [128, D], F32, U + 81920),
             sb("xnbuf1", [128, D], F32, U + 98560)]
    junk2 = [sb("junk0", [128, D], BF16, U + 92160),
             sb("junk1", [128, D], BF16, U + 96256)]
    wpj = sb("wpj", [128, ND * D], BF16, U + 24576)
    wgf = sb("wgf", [8, D], F32, U + 86016)

    ps = nc.alloc_psum_tensor("ps", [128, 8 * BK], F32)

    def bank(b, lo=0, n=BK, p0=0, pn=128):
        return ps[p0:p0 + pn, b * BK + lo: b * BK + lo + n]

    nc.const_aps.register(F32, 0.125, stat[0:128, 113:114])
    pl = Plan(nc)
    s_x4 = [pl.sem(f"s_x{i}") for i in range(4)]
    s_casm = pl.sem("s_casm")
    s_ca1 = [pl.sem(f"s_ca1_{i}") for i in range(E)]
    s_ca2 = [pl.sem(f"s_ca2_{i}") for i in range(E)]
    s_caq = pl.sem("s_caq")
    s_sml = pl.sem("s_sml")
    s_w1l = pl.sem("s_w1l")
    s_w2l = pl.sem("s_w2l")
    s_xp = [[pl.sem(f"s_xp{q}{p}") for p in range(2)] for q in range(2)]
    s_pe = pl.sem("s_pe")
    s_sc = pl.sem("s_sc")
    s_ve = pl.sem("s_ve")
    s_st = pl.sem("s_st")
    s_cs = pl.sem("s_cs")
    s_vsl = pl.sem("s_vsl")
    s_wsl = [pl.sem("s_wsl0"), pl.sem("s_wsl1")]
    s_wpj = pl.sem("s_wpj")

    tok_x4 = []
    for g in range(4):
        tok_x = pl.op("gpsimd", lambda e, g=g: e.dma_start(
            AP(x_sb, g * 4 * D, [[NT * D, 128], [D, 4], [1, D]]),
            AP(x_in, g * 4 * 128 * D, [[D, 128], [128 * D, 4], [1, D]])),
            sem=s_x4[g], dma=True)
        tok_x4.append(tok_x)
    tok_x = tok_x4[-1]
    tok_x_tiles = [tok_x4[c // 4] for c in range(NT)]

    def slow(e, dst, src):
        return e.dma_start(dst, src, allow_slow_non_contiguous=True)
    tok_b1 = pl.op("sync", lambda e: slow(e,
        AP(b1sb, 0, [[E * NHT, 128], [NHT, E], [1, NHT]]),
        AP(b1, 0, [[1, 128], [HF, E], [128, NHT]])), sem=s_sml, dma=True)
    pl.op("sync", lambda e: e.dma_start(ident[:, :], ident_in.ap()), sem=s_sml, dma=True)
    pl.op("sync", lambda e: e.dma_start(ones[:, :], ones_in.ap()), sem=s_sml, dma=True)
    tok_small = pl.op("sync", lambda e: e.dma_start(wgf[:, :], Wg.ap()), sem=s_sml, dma=True)
    v_small = tok_small[1]

    pl.op("gpsimd", lambda e: e.dma_start(b2bf[:, :], b2.ap()), sem=s_casm, dma=True)
    tok_sc2 = pl.op("gpsimd", lambda e: e.dma_start(
        bgbf[:, :], AP(bg, 0, [[E, 1], [1, E]])), sem=s_casm, dma=True)
    v_smallcast = tok_sc2[1]

    v_wcast1 = [None] * E
    v_wcast2 = [None] * E

    def emit_cast(ee, pace_waits=()):
        tok = None
        for dt in range(ND):
            tok = pl.op("gpsimd", lambda e, ee=ee, dt=dt: e.dma_start(
                AP(W1s, (ee * ND + dt) * HF * 128, [[128, HF], [1, 128]]),
                AP(W1, ee * HF * D + dt * 128, [[D, HF], [1, 128]])),
                sem=s_ca1[ee], dma=True,
                waits=(pace_waits if dt == 0 else []))
        v_wcast1[ee] = tok
        for ht in range(NHT):
            tok = pl.op("gpsimd", lambda e, ee=ee, ht=ht: e.dma_start(
                AP(W2s, (ee * NHT + ht) * D * 128, [[128, D], [1, 128]]),
                AP(W2, ee * D * HF + ht * 128, [[HF, D], [1, 128]])),
                sem=s_ca2[ee], dma=True)
        v_wcast2[ee] = tok

    emit_cast(0)
    emit_cast(1)
    emit_cast(2)

    v_qkvcast = None

    def emit_qkv_casts(pace_waits=()):
        nonlocal v_qkvcast
        pl.op("gpsimd", lambda e: e.dma_start(
            AP(qkvs, 0, [[3 * D, D], [1, 3 * D]]),
            AP(w_qkv, 0, [[3 * D, D], [1, 3 * D]])), sem=s_caq, dma=True,
            waits=pace_waits)
        tok = pl.op("gpsimd", lambda e: e.dma_start(
            AP(wprojs, 0, [[D, D], [1, D]]),
            AP(w_proj, 0, [[D, D], [1, D]])), sem=s_caq, dma=True)
        v_qkvcast = tok[1]
    if not cfg.get("ffn1"):
        if cfg.get("attn"):
            emit_qkv_casts()

    for dt in range(ND):
        pl.use_bank("tensor", 6)
        tokt = pl.op("tensor", lambda e, dt=dt: e.transpose(
            bank(6, 0, 8), wgf[0:8, dt * 128:(dt + 1) * 128],
            ident[0:8, 0:8]),
            sem=s_pe, waits=[(s_sml, v_small)])
        toke = pl.op("vector", lambda e, dt=dt: e.tensor_copy(
            wgtsb[:, dt * E:(dt + 1) * E], bank(6, 0, 8)), sem=s_ve,
            waits=[tokt])
        pl.free_bank(6, toke)
    v_wgt = s_ve.val

    def layernorm(x_tiles, xnT_free):
        SUM, SSQ, MM, RS, NMRS = 0, 16, 32, 48, 64
        if len(x_tiles) != NT:
            x_tiles = [x_tiles[-1] if x_tiles else None] * NT
        for w in xnT_free:
            if w is not None:
                pl.wait("sync", w[0], w[1])
        sq_toks = []
        v_tr = [None] * NT
        v_trpe = [None] * NT
        for g in range(4):
            cs = range(4 * g, 4 * g + 4)
            for c in cs:
                w = [] if len(sq_toks) < 2 else [sq_toks[-2]]
                if x_tiles[c] is not None:
                    w.append(x_tiles[c])
                tok_sq = pl.op("scalar", lambda e, c=c: e.activation(
                    junk2[c % 2][:, :], x_sb[:, c * D:(c + 1) * D], AF.Square,
                    accum_out=stat[:, SSQ + c:SSQ + c + 1]), sem=s_sc, waits=w)
                sq_toks.append(tok_sq)
            o = 4 * g

            def vo(b, waits=()):
                tok = pl.op("vector", b, sem=s_ve, waits=waits)
                pl.wait("vector", s_ve, tok[1])
                return tok
            wsum = [x_tiles[o + 3]] if x_tiles[o + 3] is not None else []
            vo(lambda e, o=o: e.reduce_sum(
                stat[:, SUM + o:SUM + o + 4],
                x_sb[:, o * D:(o + 4) * D].rearrange("p (c j) -> p c j", j=D),
                axis=AX.X), waits=wsum)
            vo(lambda e, o=o: e.tensor_scalar(
                stat[:, MM + o:MM + o + 4], stat[:, SUM + o:SUM + o + 4],
                1.0 / D, None, ALU.mult))
            vo(lambda e, o=o: e.scalar_tensor_tensor(
                stat[:, RS + o:RS + o + 4], stat[:, MM + o:MM + o + 4], -1.0,
                stat[:, MM + o:MM + o + 4], ALU.mult, ALU.mult))
            vo(lambda e, o=o: e.scalar_tensor_tensor(
                stat[:, RS + o:RS + o + 4], stat[:, SSQ + o:SSQ + o + 4], 1.0 / D,
                stat[:, RS + o:RS + o + 4], ALU.mult, ALU.add),
                waits=[sq_toks[o + 3]])
            vo(lambda e, o=o: e.tensor_scalar(
                stat[:, RS + o:RS + o + 4], stat[:, RS + o:RS + o + 4], EPS,
                None, ALU.add))
            tok_rcp = vo(lambda e, o=o: e.reciprocal(
                stat[:, RS + o:RS + o + 4], stat[:, RS + o:RS + o + 4]))
            tok_rs = pl.op("scalar", lambda e, o=o: e.activation(
                stat[:, RS + o:RS + o + 4], stat[:, RS + o:RS + o + 4], AF.Sqrt),
                sem=s_sc, waits=[tok_rcp])
            tok_nmrs = vo(lambda e, o=o: e.scalar_tensor_tensor(
                stat[:, NMRS + o:NMRS + o + 4], stat[:, MM + o:MM + o + 4], -1.0,
                stat[:, RS + o:RS + o + 4], ALU.mult, ALU.mult), waits=[tok_rs])
            for c in cs:
                par = c % 2
                waits = [tok_nmrs]
                if c >= 2:
                    waits.append(v_trpe[c - 2])
                tok_app = pl.op("vector", lambda e, c=c: e.tensor_scalar(
                    xnbuf[c % 2][:, :], x_sb[:, c * D:(c + 1) * D],
                    stat[:, RS + c:RS + c + 1], stat[:, NMRS + c:NMRS + c + 1],
                    ALU.mult, ALU.add), sem=s_ve, waits=waits)
                b0_ = 4 if c % 2 == 0 else 6
                pl.use_bank("tensor", b0_)
                pl.use_bank("tensor", b0_ + 1)
                for dt in range(ND):
                    tok_tp = pl.op("tensor", lambda e, c=c, dt=dt, b0_=b0_: e.transpose(
                        ps[:, b0_ * BK + dt * 128: b0_ * BK + (dt + 1) * 128],
                        xnbuf[c % 2][:, dt * 128:(dt + 1) * 128],
                        ident[:, :]), sem=s_pe,
                        waits=[tok_app] if dt == 0 else [])
                v_trpe[c] = tok_tp
                oap = AP(xnT, c * 128, [[ND * T, 128], [T, ND], [1, 128]])
                tok_ev = pl.op("vector", lambda e, oap=oap, b0_=b0_: e.tensor_copy(
                    oap, ps[:, b0_ * BK: b0_ * BK + 1024]), sem=s_ve, waits=[tok_tp])
                pl.free_bank(b0_, tok_ev)
                pl.free_bank(b0_ + 1, tok_ev)
                v_tr[c] = [tok_ev]
        return v_tr, sq_toks[-1]

    def ffn(tok_xnT, region_toks=(), region_toks2=None, call1=False):
        st_gate = {}

        def emit_gate():
            pl.wait("tensor", s_ve, v_wgt)
            pl.wait("tensor", s_casm, v_smallcast)
            pl.use_bank("tensor", 6)
            pl.use_bank("tensor", 7)
            for c in range(NT):
                pl.wait_toks("tensor", tok_xnT[c])
                for dt in range(ND):
                    pl.op("tensor", lambda e, c=c, dt=dt: e.matmul(
                        bank(6, c * E, E), xnT[:, dt * T + c * 128: dt * T + (c + 1) * 128],
                        wgtsb[:, dt * E:(dt + 1) * E], start=(dt == 0), stop=False), sem=s_pe)
                tok_g = pl.op("tensor", lambda e, c=c: e.matmul(
                    bank(6, c * E, E), ones[0:1, 0:128], bgbf[0:1, :],
                    start=False, stop=True), sem=s_pe)

            G0, G1 = 80, 96
            gv = bank(6, 0, NT * E).rearrange("p (c e) -> p c e", e=E)
            g2 = bank(7, 0, NT * E).rearrange("p (c e) -> p c e", e=E)

            def vo(b, waits=()):
                tok = pl.op("vector", b, sem=s_ve, waits=waits)
                pl.wait("vector", s_ve, tok[1])
                return tok

            vo(lambda e: e.reduce_max(stat[:, G0:G0 + NT], gv, axis=AX.X), waits=[tok_g])
            vo(lambda e: e.tensor_tensor(
                comb[:, :].rearrange("p (c e) -> p c e", e=E), gv,
                bc(stat[:, G0:G0 + NT], E), ALU.is_equal))
            vo(lambda e: e.scalar_tensor_tensor(
                bank(7, 0, NT * E), comb[:, :], -1e30, bank(6, 0, NT * E),
                ALU.mult, ALU.add))
            vo(lambda e: e.reduce_max(stat[:, G1:G1 + NT], g2, axis=AX.X))
            vo(lambda e: e.tensor_tensor(
                bank(6, 0, NT * E).rearrange("p (c e) -> p c e", e=E), g2,
                bc(stat[:, G1:G1 + NT], E), ALU.is_equal))
            tok_d = vo(lambda e: e.tensor_sub(
                stat[:, G1:G1 + NT], stat[:, G1:G1 + NT], stat[:, G0:G0 + NT]))
            tok_exp = pl.op("scalar", lambda e: e.activation(
                stat[:, G1:G1 + NT], stat[:, G1:G1 + NT], AF.Exp), sem=s_sc,
                waits=[tok_d])
            vo(lambda e: e.tensor_scalar(
                stat[:, G0:G0 + NT], stat[:, G1:G1 + NT], 1.0, None, ALU.add),
                waits=[tok_exp])
            vo(lambda e: e.reciprocal(stat[:, G0:G0 + NT], stat[:, G0:G0 + NT]))
            vo(lambda e: e.tensor_scalar(
                stat[:, G0:G0 + NT], stat[:, G0:G0 + NT], 0.5, None, ALU.mult))
            vo(lambda e: e.tensor_scalar(
                stat[:, G1:G1 + NT], stat[:, G0:G0 + NT], -1.0, 0.5, ALU.mult, ALU.add))
            vo(lambda e: e.tensor_tensor(
                comb[:, :].rearrange("p (c e) -> p c e", e=E),
                comb[:, :].rearrange("p (c e) -> p c e", e=E),
                bc(stat[:, G0:G0 + NT], E), ALU.mult))
            vo(lambda e: e.tensor_tensor(
                bank(6, 0, NT * E).rearrange("p (c e) -> p c e", e=E),
                bank(6, 0, NT * E).rearrange("p (c e) -> p c e", e=E),
                bc(stat[:, G1:G1 + NT], E), ALU.mult))
            tok_cb = vo(lambda e: e.tensor_add(comb[:, :], comb[:, :], bank(6, 0, NT * E)))
            pl.free_bank(6, tok_cb)
            pl.free_bank(7, tok_cb)

            pl.wait("tensor", tok_cb[0], tok_cb[1])
            for b in range(4):
                pl.use_bank("tensor", b)
            for c in range(NT):
                tok_t = pl.op("tensor", lambda e, c=c: e.transpose(
                    ps[0:8, c * 128:(c + 1) * 128],
                    comb[:, c * E:(c + 1) * E],
                    ident[:, :]), sem=s_pe)
            tok_cT = pl.op("vector", lambda e: e.tensor_copy(
                combT[0:8, :], ps[0:8, 0:T]), sem=s_ve, waits=[tok_t])
            for b in range(4):
                pl.free_bank(b, tok_cT)

            pl.wait("tensor", tok_cT[0], tok_cT[1])
            for tt in range(NT):
                for dc in range(2):
                    b = 6 + (tt * 2 + dc) % 2
                    pl.use_bank("tensor", b)
                    tok_m = pl.op("tensor", lambda e, tt=tt, dc=dc, b=b: e.matmul(
                        bank(b), combT[0:8, tt * 128:(tt + 1) * 128],
                        b2bf[0:8, dc * 512:(dc + 1) * 512], start=True, stop=True),
                        sem=s_pe)
                    xap = x_sb[:, tt * D + dc * 512: tt * D + dc * 512 + 512]
                    tok_a = pl.op("vector", lambda e, b=b, xap=xap: e.tensor_add(
                        xap, xap, bank(b)), sem=s_pe if False else s_ve, waits=[tok_m])
                    pl.free_bank(b, tok_a)

            st_gate["tok_cT"] = tok_cT

        v_w1 = [None] * E
        v_w2 = [None] * E
        tok_m1_done = [None] * E
        tok_m2_done = [None] * E
        relu_tok = {}
        x_tile_tok = [None] * NT
        for ee in range(E):

            w = [v_wcast1[ee]]
            if ee >= 1:
                w.append(tok_m1_done[ee - 1])
            else:
                w.extend(region_toks)
            for dt in range(ND):
                tok = pl.op("sync", lambda e, ee=ee, dt=dt: e.dma_start(
                    w1t[:, dt * HF:(dt + 1) * HF],
                    AP(W1s, (ee * ND + dt) * HF * 128, [[128, HF], [1, 128]]),
                    transpose=True), sem=s_w1l, dma=True, waits=w)
                w = []
            v_w1[ee] = tok[1]
            w = [v_wcast2[ee]]
            if ee >= 1:
                w.append(tok_m2_done[ee - 1])
            else:
                w.extend(region_toks2 if region_toks2 is not None else region_toks)
            for ht in range(NHT):
                tok = pl.op("sync", lambda e, ee=ee, ht=ht: e.dma_start(
                    w2t[:, ht * D:(ht + 1) * D],
                    AP(W2s, (ee * NHT + ht) * D * 128, [[128, D], [1, 128]]),
                    transpose=True), sem=s_w2l, dma=True, waits=w)
                w = []
            v_w2[ee] = tok[1]

            for tc in range(NTC):
                pl.wait("tensor", s_w1l, v_w1[ee])
                for cc in range(4 * tc, 4 * tc + 4):
                    pl.wait_toks("tensor", tok_xnT[cc])
                for ht in range(NHT):
                    b = ht % 4
                    pl.use_bank("tensor", b)
                    for dt in range(ND):
                        tok_m = pl.op("tensor", lambda e, ht=ht, tc=tc, dt=dt, b=b: e.matmul(
                            bank(b),
                            w1t[:, dt * HF + ht * 128: dt * HF + (ht + 1) * 128],
                            xnT[:, dt * T + tc * TC: dt * T + (tc + 1) * TC],
                            start=(dt == 0), stop=(dt == ND - 1)), sem=s_pe)
                    bcol = b1sb[:, ee * NHT + ht: ee * NHT + ht + 1]
                    oap = hT[:, ht * TC:(ht + 1) * TC]
                    if ht % 2 == 0:
                        tok_r = pl.op("scalar", lambda e, oap=oap, b=b, bcol=bcol:
                                      e.activation(oap, bank(b), AF.Relu, bias=bcol),
                                      sem=s_sc, waits=[tok_m])
                    else:
                        tok_r = pl.op("vector", lambda e, oap=oap, b=b, bcol=bcol:
                                      e.tensor_scalar(oap, bank(b), bcol, 0.0,
                                                      ALU.add, ALU.max),
                                      sem=s_ve, waits=[tok_m])
                    relu_tok[(tc, ht)] = tok_r
                    pl.free_bank(b, tok_r)
                tok_m1_done[ee] = tok_m
                if ee == 0 and tc == 0:
                    emit_gate()
                if call1 and tc == 0:
                    if ee >= 1 and ee + 2 < E:
                        emit_cast(ee + 2, pace_waits=[tok_m])
                    if ee == 2 and cfg.get("attn"):
                        emit_qkv_casts(pace_waits=[tok_m])

                pl.wait("tensor", s_w2l, v_w2[ee])
                for msub in range(TC // 128):
                    tt = tc * (TC // 128) + msub
                    for dc in range(2):
                        b = 4 + (msub * 2 + dc) % 2
                        pl.use_bank("tensor", b)
                        for ht in range(NHT):
                            waits = [relu_tok[(tc, ht)]] if msub == 0 and dc == 0 else []
                            tok_m = pl.op("tensor", lambda e, ht=ht, msub=msub, dc=dc, b=b, tc=tc: e.matmul(
                                bank(b),
                                hT[:, ht * TC + msub * 128: ht * TC + (msub + 1) * 128],
                                w2t[:, ht * D + dc * 512: ht * D + (dc + 1) * 512],
                                start=(ht == 0), stop=(ht == NHT - 1)), sem=s_pe,
                                waits=waits)
                        xap = x_sb[:, tt * D + dc * 512: tt * D + dc * 512 + 512]
                        tok_y = pl.op("vector", lambda e, b=b, xap=xap, col=tt * E + ee:
                                      e.scalar_tensor_tensor(
                                          xap, bank(b), comb[:, col:col + 1], xap,
                                          ALU.mult, ALU.add), sem=s_ve, waits=[tok_m])
                        pl.free_bank(b, tok_y)
                        if ee == E - 1:
                            x_tile_tok[tt] = tok_y
                tok_m2_done[ee] = tok_m

        return dict(x_done=x_tile_tok[NT - 1], x_tiles=x_tile_tok,
                    m1_last=tok_m1_done[E - 1], m2_last=tok_m2_done[E - 1])


    def attn(tok_xnT, tok_ffn1, tok_lnsq):
        SHIFTC = 112

        tok_cs = None
        for i, tgt in ((2, cosu), (3, sinu)):
            tok_cs = pl.op("gpsimd", lambda e, i=i, tgt=tgt: e.dma_start(
                tgt[:, :], AP(coss, i * 128 * T, [[T, 128], [1, T]])),
                sem=s_cs, dma=True,
                waits=[tok_ffn1["m1_last"], (s_caq, v_qkvcast)] if i == 2 else [])
        pl.op("vector", lambda e: e.memset(stat[:, SHIFTC:SHIFTC + 1], EXPSHIFT), sem=s_ve)
        tok_shift = pl.op("vector", lambda e: e.memset(stat[:, 113:114], 0.125), sem=s_ve)

        tok_vones = pl.op("vector", lambda e: e.memset(
            AP(vh, 64, [[NT * 520, 128], [520, NT], [65, 8], [1, 1]]), 1.0),
            sem=s_ve, waits=[tok_ffn1["m2_last"]])

        def v_half(half, extra_waits):
            toks = []
            for grp in range(2):
                tok_l = pl.op("sync", lambda e, half=half, grp=grp: e.dma_start(
                    wsl[:, grp * 2048:(grp + 1) * 2048],
                    AP(qkvs, 2048 + half * 512 + grp * 256,
                       [[3 * D, 128], [128 * 3 * D, ND], [1, 256]]),
                    transpose=False), sem=s_vsl, dma=True,
                    waits=(extra_waits if grp == 0 else []))
            v_ld = tok_l[1]
            last_ev = None
            for tt in range(NT):
                pl.wait_toks("tensor", tok_xnT[tt])
                for grp in range(2):
                    b = 6 + grp
                    pl.use_bank("tensor", b)
                    for dt in range(ND):
                        tok_m = pl.op("tensor", lambda e, tt=tt, grp=grp, dt=dt, b=b: e.matmul(
                            bank(b, 0, 256),
                            xnT[:, dt * T + tt * 128: dt * T + (tt + 1) * 128],
                            wsl[:, grp * 2048 + dt * 256: grp * 2048 + (dt + 1) * 256],
                            start=(dt == 0), stop=(dt == ND - 1)), sem=s_pe,
                            waits=[(s_vsl, v_ld)] if (tt == 0 and dt == 0) else [])
                    oap = AP(vh, tt * 520 + grp * 4 * 65,
                             [[NT * 520, 128], [65, 4], [1, 64]])
                    iap = bank(b, 0, 256).rearrange("p (h c) -> p h c", c=64)
                    last_ev = pl.op("vector", lambda e, oap=oap, iap=iap: e.tensor_copy(
                        oap, iap), sem=s_ve, waits=[tok_m])
                    pl.free_bank(b, last_ev)
            return last_ev

        tok_vA = v_half(0, [tok_ffn1["m2_last"], (s_caq, v_qkvcast), tok_vones, tok_lnsq])

        avq_tok = [[None] * 4, [None] * 4]
        ev_prev = [[None] * 4, [None] * 4]
        rl_pend = [None, None]
        norm_pend = [None, None]
        tok_qk_done = [None] * 9
        tok_rope_done = [None] * 9
        tok_sc_done = [None] * 8
        tok_norm = None
        tok_vB = [None]

        def emit_chunk_qk(hc, extra_w=()):
            par = hc % 2
            w = list(extra_w)
            if hc >= 1:
                w.append(tok_qk_done[hc - 1])
            if hc == 0:
                w.append(tok_vA)
            if hc == 5 and tok_vB[0] is not None:
                w.append(tok_vB[0])
            tok_l = None
            for qd in (0, 2):
                base_c = hc * 128 if qd == 0 else D + hc * 128
                tok_l = pl.op("sync", lambda e, qd=qd, base_c=base_c: e.dma_start(
                    wsl[:, qd * 1024:(qd + 1) * 1024],
                    AP(qkvs, base_c, [[3 * D, 128], [128 * 3 * D, ND], [1, 128]])),
                    sem=s_wsl[par], dma=True, waits=(w if qd == 0 else []))
            for qsrc, qdst in ((0, 1), (2, 3)):
                base_c = hc * 128 if qsrc == 0 else D + hc * 128
                for half in range(2):
                    for sub in range(2):
                        oap = AP(wsl, qdst * 1024 + half * 32 + sub * 64,
                                 [[4 * ND * 128, 128], [128, ND], [1, 32]])
                        iap = AP(qkvs, base_c + sub * 64 + (32 if half == 0 else 0),
                                 [[3 * D, 128], [128 * 3 * D, ND], [1, 32]])
                        tok_l = pl.op("sync", lambda e, oap=oap, iap=iap: e.dma_start(
                            oap, iap), sem=s_wsl[par], dma=True)
            v_wsl = tok_l[1]
            tok_mq = None
            rope_waits = []
            if hc >= 2:
                rope_waits.append(tok_sc_done[hc - 2])
            last_rope = None
            for part_i, (cost, sint, dest) in enumerate((
                    (cosu, sinu, qrope[par]), (cosu, sinu, krope[par]))):
                for qq in range(4):
                    ba, bb = ((6, 7), (0, 1), (2, 3))[(part_i * 4 + qq) % 3]
                    pl.use_bank("tensor", ba)
                    pl.use_bank("tensor", bb)
                    for cc in range(4 * qq, 4 * qq + 4):
                        pl.wait_toks("tensor", tok_xnT[cc])
                    for dt in range(ND):
                        wts = []
                        if part_i == 0 and qq == 0 and dt == 0:
                            wts = [(s_wsl[par], v_wsl), (s_cs, tok_cs[1])]
                            wts.extend(rope_waits)
                        tok_mq = pl.op("tensor", lambda e, qq=qq, dt=dt, pi=part_i, ba=ba: e.matmul(
                            bank(ba),
                            wsl[:, (2 * pi) * 1024 + dt * 128:(2 * pi) * 1024 + (dt + 1) * 128],
                            xnT[:, dt * T + qq * 512: dt * T + (qq + 1) * 512],
                            start=(dt == 0), stop=(dt == ND - 1)), sem=s_pe, waits=wts)
                    for dt in range(ND):
                        tok_mq = pl.op("tensor", lambda e, qq=qq, dt=dt, pi=part_i, bb=bb: e.matmul(
                            bank(bb),
                            wsl[:, (2 * pi + 1) * 1024 + dt * 128:(2 * pi + 1) * 1024 + (dt + 1) * 128],
                            xnT[:, dt * T + qq * 512: dt * T + (qq + 1) * 512],
                            start=(dt == 0), stop=(dt == ND - 1)), sem=s_pe)
                    wts = [tok_mq] + (rope_waits if part_i == 0 and qq == 0 else [])
                    tma, tmb = tmpah[hc], tmpbh[hc]
                    tok_a = pl.op("vector", lambda e, qq=qq, cost=cost, tma=tma, ba=ba: e.tensor_tensor(
                        tma[:, :], bank(ba), cost[:, qq * 512:(qq + 1) * 512],
                        ALU.mult), sem=s_ve, waits=wts + ([(s_ve, last_rope[1])] if last_rope else []))
                    tok_b = pl.op("vector", lambda e, qq=qq, sint=sint, tmb=tmb, bb=bb: e.tensor_tensor(
                        tmb[:, :], bank(bb), sint[:, qq * 512:(qq + 1) * 512],
                        ALU.mult), sem=s_ve)
                    pl.free_bank(ba, tok_a)
                    pl.free_bank(bb, tok_b)
                    pl.wait("vector", s_ve, tok_b[1])
                    last_rope = pl.op("vector", lambda e, qq=qq, dest=dest, tma=tma, tmb=tmb: e.tensor_add(
                        dest[:, qq * 512:(qq + 1) * 512], tma[:, :], tmb[:, :]),
                        sem=s_ve)
            tok_qk_done[hc] = tok_mq
            tok_rope_done[hc] = last_rope

        def scores_chunk(hc):
            nonlocal tok_norm
            par = hc % 2
            at_b = [atA, atB]

            def avq(hp, j, g, tok_exp):
                atb = at_b[hp]
                bo = (4 if hp == 0 else 6) + (j % 2)
                hh = (2 * hc + hp) % 8
                if g == 0:
                    pl.use_bank("tensor", bo)
                tok = None
                for tki in range(4):
                    tk = 4 * g + tki
                    wts = [tok_exp] if tki == 0 else []
                    if tki == 0 and g == 0 and j == 0 and hp == 0 and hc >= 4:
                        wts.append(tok_vB[0])
                    tok = pl.op("tensor", lambda e, tk=tk, hh=hh, bo=bo, atb=atb, g=g, j=j: e.matmul(
                        bank(bo, 0, 256, 0, 65),
                        vh[:, tk * 520 + hh * 65: tk * 520 + hh * 65 + 65],
                        atb[:, tk * 256:(tk + 1) * 256],
                        start=(tk == 0), stop=(tk == 15)), sem=s_pe, waits=wts)
                avq_tok[hp][g] = tok
                if g == 3:
                    ro = 512 * hp + 256 * (j % 2)
                    tok_rl = pl.op("vector", lambda e, bo=bo, ro=ro: e.reciprocal(
                        rlbf[0:1, ro:ro + 256], bank(bo, 0, 256, 64, 1)), sem=s_ve,
                        waits=[tok])
                    pl.wait("vector", s_ve, tok_rl[1])
                    rl_pend[hp] = (j, tok_rl)

            def do_norm(hp, j, tok_rl):
                nonlocal tok_norm
                bo = (4 if hp == 0 else 6) + (j % 2)
                ro = 512 * hp + 256 * (j % 2)
                tok_rb = pl.op("tensor", lambda e, bo=bo, ro=ro: e.matmul(
                    bank(bo, 256, 256, 0, 64), ones[0:1, 0:64],
                    rlbf[0:1, ro:ro + 256],
                    start=True, stop=True), sem=s_pe, waits=[tok_rl, tok_shift])
                rbo = 256 * hp
                tok_rbe = pl.op("vector", lambda e, bo=bo, rbo=rbo: e.tensor_copy(
                    rbsb[:, rbo:rbo + 256], bank(bo, 256, 256, 0, 64)), sem=s_ve,
                    waits=[tok_rb])
                pl.wait("vector", s_ve, tok_rbe[1])
                tok_norm = pl.op("vector", lambda e, hp=hp, hc=hc, j=j, bo=bo, rbo=rbo: e.tensor_mul(
                    otall[64 * hp:64 * hp + 64,
                          hc * T + j * 256: hc * T + j * 256 + 256],
                    bank(bo, 0, 256, 0, 64), rbsb[:, rbo:rbo + 256]), sem=s_ve)
                pl.free_bank(bo, tok_norm)

            from collections import deque
            avail = [deque(), deque()]
            slot_now = 0
            for tqc in range(8):
                evs = [[], []]
                for g in range(4):
                    slot_now += 1
                    pl.use_bank("tensor", 0)
                    pl.use_bank("tensor", 1)
                    tok_s = None
                    first = True
                    for tki in range(4):
                        tk = 4 * g + tki
                        for hp in range(2):
                            base = (0 if hp == 0 else 2) * BK
                            wts = []
                            if g == 0 and tki == 0 and tqc == 0 and hp == 0:
                                wts = [tok_rope_done[hc]]
                            tok_s = pl.op("tensor", lambda e, tk=tk, tki=tki, base=base, tqc=tqc, hp=hp, par=par: e.matmul(
                                ps[:, base + tki * 256: base + tki * 256 + 256],
                                krope[par][64 * hp:64 * hp + 64, tk * 128:(tk + 1) * 128],
                                qrope[par][64 * hp:64 * hp + 64, tqc * 256:(tqc + 1) * 256],
                                start=True, stop=True), sem=s_pe, waits=wts)
                            if first:
                                pl.use_bank("tensor", 2)
                                pl.use_bank("tensor", 3)
                                first = False
                    for hp in range(2):
                        if avail[hp] and slot_now - avail[hp][0][0] >= 2:
                            sp, jq, qq_, te = avail[hp].popleft()
                            avq(hp, jq, qq_, te)
                    for hp in range(2):
                        base = (0 if hp == 0 else 2) * BK
                        ew = [tok_s, tok_shift]
                        if avq_tok[hp][g] is not None:
                            ew.append(avq_tok[hp][g])
                        atb = at_b[hp]
                        tok_e = pl.op("scalar", lambda e, g=g, base=base, atb=atb: e.activation(
                            atb[:, g * 1024:(g + 1) * 1024],
                            ps[:, base: base + 1024], AF.Exp,
                            bias=stat[:, SHIFTC:SHIFTC + 1], scale=0.125), sem=s_sc,
                            waits=ew)
                        evs[hp].append(tok_e)
                        pl.free_bank(0 + base // BK, tok_e)
                        pl.free_bank(1 + base // BK, tok_e)
                        avail[hp].append((slot_now, tqc, g, tok_e))
                    if g == 0 and norm_pend[0] is not None:
                        for hp in range(2):
                            jn, tok_rl = norm_pend[hp]
                            do_norm(hp, jn, tok_rl)
                        norm_pend[0] = norm_pend[1] = None
                    if g == 3:
                        for hp in range(2):
                            if rl_pend[hp] is not None:
                                norm_pend[hp] = rl_pend[hp]
                                rl_pend[hp] = None
            while avail[0] or avail[1]:
                for hp in range(2):
                    if avail[hp]:
                        sp, jq, qq_, te = avail[hp].popleft()
                        avq(hp, jq, qq_, te)
                for hp in range(2):
                    if norm_pend[hp] is not None:
                        jn, tok_rl = norm_pend[hp]
                        do_norm(hp, jn, tok_rl)
                        norm_pend[hp] = None
                    if rl_pend[hp] is not None:
                        norm_pend[hp] = rl_pend[hp]
                        rl_pend[hp] = None
            for hp in range(2):
                if norm_pend[hp] is not None:
                    jn, tok_rl = norm_pend[hp]
                    do_norm(hp, jn, tok_rl)
                    norm_pend[hp] = None
                if rl_pend[hp] is not None:
                    jn, tok_rl = rl_pend[hp]
                    do_norm(hp, jn, tok_rl)
                    rl_pend[hp] = None

        emit_chunk_qk(0)
        for hc in range(8):
            if hc + 1 < 8:
                emit_chunk_qk(hc + 1)
            scores_chunk(hc)
            tok_sc_done[hc] = (s_pe, s_pe.val)
            if hc == 3:
                tok_vB[0] = v_half(1, [tok_qk_done[4]])

        tok_bpj = pl.op("gpsimd", lambda e: e.dma_start(
            bprojbf[:, :], AP(b_proj, 0, [[D, 1], [1, D]])), sem=s_cs, dma=True,
            waits=[tok_sc_done[7]])
        tok_wpj = pl.op("sync", lambda e: e.dma_start(
            wpj[:, :], AP(wprojs, 0, [[D, 128], [128 * D, ND], [1, D]])),
            sem=s_wpj, dma=True,
            waits=[tok_sc_done[7], (s_caq, v_qkvcast)])
        last = None
        x_proj_tok = [None] * NT
        for tt in range(NT):
            for dc in range(2):
                b = (tt * 2 + dc) % 2
                pl.use_bank("tensor", b)
                for ct in range(ND):
                    wts = []
                    if tt == 0 and dc == 0 and ct == 0:
                        wts = [tok_norm, (s_wpj, tok_wpj[1]), tok_bpj]
                    tok_m = pl.op("tensor", lambda e, tt=tt, dc=dc, ct=ct, b=b: e.matmul(
                        bank(b), otall[:, ct * T + tt * 128: ct * T + (tt + 1) * 128],
                        wpj[:, ct * D + dc * 512: ct * D + (dc + 1) * 512],
                        start=(ct == 0), stop=False), sem=s_pe, waits=wts)
                tok_m = pl.op("tensor", lambda e, dc=dc, b=b: e.matmul(
                    bank(b), ones[0:1, 0:128], bprojbf[0:1, dc * 512:(dc + 1) * 512],
                    start=False, stop=True), sem=s_pe)
                xap = x_sb[:, tt * D + dc * 512: tt * D + dc * 512 + 512]
                last = pl.op("vector", lambda e, b=b, xap=xap: e.tensor_add(
                    xap, xap, bank(b)), sem=s_ve, waits=[tok_m])
                pl.free_bank(b, last)
                x_proj_tok[tt] = last
        return dict(x_done=last, x_tiles=x_proj_tok,
                    proj_done=(s_pe, s_pe.val),
                    rope_last=tok_rope_done[7], sc_last=tok_sc_done[7],
                    qk_last=tok_qk_done[7], vB=tok_vB[0])

    tok_xnT, _sq1 = layernorm(tok_x_tiles, [])
    r1 = ffn(tok_xnT, call1=True) if cfg.get("ffn1") else None
    if cfg.get("attn"):
        xr = r1["x_tiles"] if r1 else [tok_x]
        xnfree = [r1["m1_last"]] if r1 else []
        tok_xnT2, sq2 = layernorm(xr, xnfree)
        if r1 is None:
            emit_cast(0)
            emit_cast(1)
            r1 = dict(m1_last=(s_pe, 0), m2_last=(s_pe, 0))
        ra = attn(tok_xnT2, r1, sq2)
        if cfg.get("ffn2"):
            for ee in range(2, E):
                emit_cast(ee)
            tok_xnT3, _sq3 = layernorm(ra["x_tiles"], [ra["qk_last"], ra["vB"]])
            r2 = ffn(tok_xnT3, region_toks=[ra["proj_done"]],
                     region_toks2=[ra["proj_done"]])

    x_fin = r2["x_tiles"] if cfg.get("ffn2") else (
        [None] * NT if not cfg.get("ffn1") else r1["x_tiles"])
    for tt in range(NT):
        w = [x_fin[tt]] if x_fin[tt] is not None else [
            (s_ve, s_ve.val), (s_sc, s_sc.val), (s_pe, s_pe.val)]
        pl.op("gpsimd", lambda e, tt=tt: e.dma_start(
            AP(out, tt * 128 * D, [[D, 128], [1, D]]),
            AP(x_sb, tt * D, [[NT * D, 128], [1, D]])), sem=s_st, dma=True,
            waits=w)
    pl.op("gpsimd", lambda e: e.nop(), waits=[(s_st, s_st.val)])

    pl.finish()
    return nc


def _rope_tables_host():
    try:
        import jax
        import jax.numpy as jnp
        cpu = [d for d in jax.devices("cpu")]
        with jax.default_device(cpu[0]):
            inv = 1.0 / 10000.0 ** (jnp.arange(0, DH, 2, dtype=jnp.float32) / DH)
            t = jnp.arange(T, dtype=jnp.float32)
            freqs = t[:, None] * inv[None, :]
            cos = np.asarray(jnp.cos(freqs), dtype=np.float32)
            sin = np.asarray(jnp.sin(freqs), dtype=np.float32)
            return cos, sin
    except Exception:
        expo = np.arange(0, DH, 2, dtype=np.float32) / np.float32(DH)
        inv = np.float32(1.0) / np.power(np.float32(10000.0), expo)
        t = np.arange(T, dtype=np.float32)
        ang = t[:, None] * inv[None, :]
        return np.cos(ang), np.sin(ang)


def make_consts():
    cos, sin = _rope_tables_host()
    p = np.arange(128)
    cosf = cos[:, p % 32].T
    sinf = sin[:, p % 32].T
    sgn = np.where((p % 64) < 32, -1.0, 1.0).astype(np.float32)[:, None]
    coss = np.stack([cosf / 8.0, (sinf * sgn) / 8.0, cosf, sinf * sgn])
    ident = np.eye(128, dtype=np.float32)
    onesm = np.ones((128, 128), dtype=ml_dtypes.bfloat16)
    return coss.astype(np.float32), ident, onesm


_NC_CACHE = {}


def _get_nc():
    if "nc" not in _NC_CACHE:
        _NC_CACHE["nc"] = build({"ffn1": True, "attn": True, "ffn2": True})
    return _NC_CACHE["nc"]


def kernel(L0_attn_x, L0_attn_w_qkv, L0_attn_w_proj, L0_attn_b_proj,
           L0_ffn_W1, L0_ffn_b1, L0_ffn_W2, L0_ffn_b2, L0_ffn_Wg, L0_ffn_bg):
    from concourse.bass_utils import run_bass_kernel_spmd
    x = np.ascontiguousarray(np.asarray(L0_attn_x, dtype=np.float32))
    coss, ident, onesm = make_consts()
    shared = dict(
        w_qkv=np.ascontiguousarray(np.asarray(L0_attn_w_qkv, dtype=np.float32)),
        w_proj=np.ascontiguousarray(np.asarray(L0_attn_w_proj, dtype=np.float32)),
        b_proj=np.ascontiguousarray(np.asarray(L0_attn_b_proj, dtype=np.float32)),
        W1=np.ascontiguousarray(np.asarray(L0_ffn_W1, dtype=np.float32)),
        b1=np.ascontiguousarray(np.asarray(L0_ffn_b1, dtype=np.float32)),
        W2=np.ascontiguousarray(np.asarray(L0_ffn_W2, dtype=np.float32)),
        b2=np.ascontiguousarray(np.asarray(L0_ffn_b2, dtype=np.float32)),
        Wg=np.ascontiguousarray(np.asarray(L0_ffn_Wg, dtype=np.float32)),
        bg=np.ascontiguousarray(np.asarray(L0_ffn_bg, dtype=np.float32)),
        coss=coss, ident_in=ident, ones_in=onesm,
    )
    n_cores = 8
    in_maps = [dict(shared, x_in=np.ascontiguousarray(x[i])) for i in range(n_cores)]
    nc = _get_nc()
    res = run_bass_kernel_spmd(nc, in_maps, core_ids=list(range(n_cores)))
    return np.stack([np.asarray(r["out"], dtype=np.float32) for r in res.results])


# version 14
# speedup vs baseline: 1.1545x; 1.0249x over previous
import numpy as np
import ml_dtypes
import concourse.bass as bass
import concourse.mybir as mybir

F32 = mybir.dt.float32
BF16 = mybir.dt.bfloat16
AF = mybir.ActivationFunctionType
ALU = mybir.AluOpType
AX = mybir.AxisListType
AP = bass.AP

T, D, HF, E, NH, DH = 2048, 1024, 2048, 8, 16, 64
NT, ND, NHT = T // 128, D // 128, HF // 128
TC = 512
NTC = T // TC
EPS = 1e-5
EXPSHIFT = -20.0
BK = 512


def bc(ap, n):
    return AP(ap.tensor, ap.offset, [list(d) for d in ap.ap] + [[0, n]])


class SemC:
    __slots__ = ("h", "val")

    def __init__(self, h):
        self.h = h
        self.val = 0


class Plan:
    ENGS = ("tensor", "vector", "scalar", "sync", "gpsimd")

    def __init__(self, nc):
        self.nc = nc
        self.ops = {e: [] for e in self.ENGS}
        self.waited = {e: {} for e in self.ENGS}
        self.bank_free = {}

    def sem(self, name):
        return SemC(self.nc.alloc_semaphore(name))

    def wait(self, eng, s, v):
        if s is None or v is None or v <= 0:
            return
        if self.waited[eng].get(s.h.num, -1) >= v:
            return
        self.waited[eng][s.h.num] = v
        self.ops[eng].append(lambda e, h=s.h, v=v: e.wait_ge(h, v))

    def op(self, eng, build, sem=None, waits=(), dma=False):
        for w in waits:
            if w is None:
                continue
            self.wait(eng, w[0], w[1])
        if sem is not None:
            inc = 16 if dma else 1
            sem.val += inc
            self.ops[eng].append(lambda e, b=build, h=sem.h, i=inc: b(e).then_inc(h, i))
            return (sem, sem.val)
        self.ops[eng].append(build)
        return None

    def wait_toks(self, eng, toks):
        if toks is None:
            return
        if isinstance(toks, tuple):
            toks = [toks]
        for t in toks:
            if t is not None:
                self.wait(eng, t[0], t[1])

    def use_bank(self, eng, b):
        w = self.bank_free.get(b)
        if w is not None:
            self.wait(eng, w[0], w[1])

    def free_bank(self, b, tok):
        self.bank_free[b] = tok

    def finish(self):
        with self.nc.allow_low_precision("bf16 attn l-reciprocal (within tolerance)"), \
             self.nc.Block() as block:
            for eng in self.ENGS:
                def runner(e, eng=eng):
                    for f in self.ops[eng]:
                        f(e)
                getattr(block, eng)(runner)


def build(cfg):
    nc = bass.Bass("TRN2", target_bir_lowering=False, debug=False)

    x_in = nc.dram_tensor("x_in", [T, D], F32, kind="ExternalInput")
    w_qkv = nc.dram_tensor("w_qkv", [D, 3 * D], F32, kind="ExternalInput")
    w_proj = nc.dram_tensor("w_proj", [D, D], F32, kind="ExternalInput")
    b_proj = nc.dram_tensor("b_proj", [D], F32, kind="ExternalInput")
    W1 = nc.dram_tensor("W1", [E, HF, D], F32, kind="ExternalInput")
    b1 = nc.dram_tensor("b1", [E, HF], F32, kind="ExternalInput")
    W2 = nc.dram_tensor("W2", [E, D, HF], F32, kind="ExternalInput")
    b2 = nc.dram_tensor("b2", [E, D], F32, kind="ExternalInput")
    Wg = nc.dram_tensor("Wg", [E, D], F32, kind="ExternalInput")
    bg = nc.dram_tensor("bg", [E], F32, kind="ExternalInput")
    coss = nc.dram_tensor("coss", [4, 128, T], F32, kind="ExternalInput")
    ident_in = nc.dram_tensor("ident_in", [128, 128], F32, kind="ExternalInput")
    ones_in = nc.dram_tensor("ones_in", [128, 128], BF16, kind="ExternalInput")
    out = nc.dram_tensor("out", [T, D], F32, kind="ExternalOutput")

    W1s = nc.dram_tensor("W1s", [E, ND, HF, 128], BF16, kind="Internal")
    W2s = nc.dram_tensor("W2s", [E, NHT, D, 128], BF16, kind="Internal")
    qkvs = nc.dram_tensor("qkvs", [D, 3 * D], BF16, kind="Internal")
    wprojs = nc.dram_tensor("wprojs", [D, D], BF16, kind="Internal")

    AR = 212736
    arena = nc.alloc_sbuf_tensor("arena", [128, AR], mybir.dt.uint8)
    abase = nc.lookup_mloc(arena).addr

    def sb(name, shape, dtype, off):
        nbytes = int(np.prod(shape[1:])) * mybir.dt.size(dtype)
        assert off + nbytes <= AR, f"{name}: {off}+{nbytes} > {AR}"
        return nc.alloc_sbuf_tensor_at(name, shape, dtype, offset=abase + off)

    x_sb = sb("x_sb", [128, NT * D], F32, 0)
    M = 65536
    def psb(name, shape, dtype):
        nonlocal M
        t = sb(name, shape, dtype, M)
        nbytes = int(np.prod(shape[1:])) * mybir.dt.size(dtype)
        M += (nbytes + 31) // 32 * 32
        return t
    b1sb = psb("b1sb", [128, E * NHT], F32)
    comb = psb("comb", [128, NT * E], F32)
    b2bf = psb("b2bf", [8, D], BF16)
    bgbf = psb("bgbf", [1, E], BF16)
    wgtsb = psb("wgtsb", [128, ND * E], BF16)
    ident = psb("ident", [128, 128], F32)
    ones = psb("ones", [128, 128], BF16)
    stat = psb("stat", [128, 8 * NT], F32)
    rlbf = psb("rlbf", [1, 1024], BF16)
    rbsb = psb("rbsb", [64, 512], F32)
    assert M <= 75776, M
    xnT = sb("xnT", [128, ND * T], BF16, 75776)
    U = 108544
    w1t = sb("w1t", [128, ND * HF], BF16, U)
    w2t = sb("w2t", [128, NHT * D], BF16, U + 32768)
    hT = sb("hT", [128, NHT * TC], BF16, U + 65536)
    combT = sb("combT", [8, T], BF16, U + 81920)
    tmpah = [sb(f"tmpa{h}", [128, 512], F32, U + 57600 + h * 4096) for h in range(8)]
    tmpbh = [sb(f"tmpb{h}", [128, 512], F32, U + 59648 + h * 4096) for h in range(8)]
    cosu = sb("cosu", [128, T], BF16, U + 0)
    sinu = sb("sinu", [128, T], BF16, U + 4096)
    atAB = sb("atAB", [128, 32 * 256], BF16, U + 8192)
    qrope = [sb(f"qrope{i}", [128, T], BF16, U + 24576 + 8192 * i) for i in range(2)]
    krope = [sb(f"krope{i}", [128, T], BF16, U + 28672 + 8192 * i) for i in range(2)]
    vh = sb("vh", [128, NT * 520], BF16, U + 40960)
    bprojbf = sb("bprojbf", [1, D], BF16, U + 8192)
    rlbf2 = None
    otall = sb("otall", [128, 8 * T], BF16, U + 57600)
    wsl = sb("wsl", [128, 4 * ND * 128], BF16, U + 90368)
    xnbuf = [sb("xnbuf0", [128, D], F32, U + 81920),
             sb("xnbuf1", [128, D], F32, U + 98560)]
    junk2 = [sb("junk0", [128, D], BF16, U + 92160),
             sb("junk1", [128, D], BF16, U + 96256)]
    wpj = sb("wpj", [128, ND * D], BF16, U + 24576)
    wgf = sb("wgf", [8, D], F32, U + 86016)

    ps = nc.alloc_psum_tensor("ps", [128, 8 * BK], F32)

    def bank(b, lo=0, n=BK, p0=0, pn=128):
        return ps[p0:p0 + pn, b * BK + lo: b * BK + lo + n]

    nc.const_aps.register(F32, 0.125, stat[0:128, 113:114])
    pl = Plan(nc)
    s_x4 = [pl.sem(f"s_x{i}") for i in range(4)]
    s_casm = pl.sem("s_casm")
    s_ca1 = [pl.sem(f"s_ca1_{i}") for i in range(E)]
    s_ca2 = [pl.sem(f"s_ca2_{i}") for i in range(E)]
    s_caq = pl.sem("s_caq")
    s_sml = pl.sem("s_sml")
    s_w1l = pl.sem("s_w1l")
    s_w2l = pl.sem("s_w2l")
    s_xp = [[pl.sem(f"s_xp{q}{p}") for p in range(2)] for q in range(2)]
    s_pe = pl.sem("s_pe")
    s_sc = pl.sem("s_sc")
    s_ve = pl.sem("s_ve")
    s_st = pl.sem("s_st")
    s_cs = pl.sem("s_cs")
    s_vsl = pl.sem("s_vsl")
    s_wsl = [pl.sem("s_wsl0"), pl.sem("s_wsl1")]
    s_wpj = pl.sem("s_wpj")

    tok_x4 = []
    for g in range(4):
        tok_x = pl.op("gpsimd", lambda e, g=g: e.dma_start(
            AP(x_sb, g * 4 * D, [[NT * D, 128], [D, 4], [1, D]]),
            AP(x_in, g * 4 * 128 * D, [[D, 128], [128 * D, 4], [1, D]])),
            sem=s_x4[g], dma=True)
        tok_x4.append(tok_x)
    tok_x = tok_x4[-1]
    tok_x_tiles = [tok_x4[c // 4] for c in range(NT)]

    def slow(e, dst, src):
        return e.dma_start(dst, src, allow_slow_non_contiguous=True)
    tok_b1 = pl.op("sync", lambda e: slow(e,
        AP(b1sb, 0, [[E * NHT, 128], [NHT, E], [1, NHT]]),
        AP(b1, 0, [[1, 128], [HF, E], [128, NHT]])), sem=s_sml, dma=True)
    pl.op("sync", lambda e: e.dma_start(ident[:, :], ident_in.ap()), sem=s_sml, dma=True)
    pl.op("sync", lambda e: e.dma_start(ones[:, :], ones_in.ap()), sem=s_sml, dma=True)
    tok_small = pl.op("sync", lambda e: e.dma_start(wgf[:, :], Wg.ap()), sem=s_sml, dma=True)
    v_small = tok_small[1]

    pl.op("gpsimd", lambda e: e.dma_start(b2bf[:, :], b2.ap()), sem=s_casm, dma=True)
    tok_sc2 = pl.op("gpsimd", lambda e: e.dma_start(
        bgbf[:, :], AP(bg, 0, [[E, 1], [1, E]])), sem=s_casm, dma=True)
    v_smallcast = tok_sc2[1]

    v_wcast1 = [None] * E
    v_wcast2 = [None] * E

    def emit_cast(ee, pace_waits=()):
        tok = None
        for dt in range(ND):
            tok = pl.op("gpsimd", lambda e, ee=ee, dt=dt: e.dma_start(
                AP(W1s, (ee * ND + dt) * HF * 128, [[128, HF], [1, 128]]),
                AP(W1, ee * HF * D + dt * 128, [[D, HF], [1, 128]])),
                sem=s_ca1[ee], dma=True,
                waits=(pace_waits if dt == 0 else []))
        v_wcast1[ee] = tok
        for ht in range(NHT):
            tok = pl.op("gpsimd", lambda e, ee=ee, ht=ht: e.dma_start(
                AP(W2s, (ee * NHT + ht) * D * 128, [[128, D], [1, 128]]),
                AP(W2, ee * D * HF + ht * 128, [[HF, D], [1, 128]])),
                sem=s_ca2[ee], dma=True)
        v_wcast2[ee] = tok

    emit_cast(0)
    emit_cast(1)
    emit_cast(2)

    v_qkvcast = None

    def emit_qkv_casts(pace_waits=()):
        nonlocal v_qkvcast
        pl.op("gpsimd", lambda e: e.dma_start(
            AP(qkvs, 0, [[3 * D, D], [1, 3 * D]]),
            AP(w_qkv, 0, [[3 * D, D], [1, 3 * D]])), sem=s_caq, dma=True,
            waits=pace_waits)
        tok = pl.op("gpsimd", lambda e: e.dma_start(
            AP(wprojs, 0, [[D, D], [1, D]]),
            AP(w_proj, 0, [[D, D], [1, D]])), sem=s_caq, dma=True)
        v_qkvcast = tok[1]
    if not cfg.get("ffn1"):
        if cfg.get("attn"):
            emit_qkv_casts()

    for dt in range(ND):
        pl.use_bank("tensor", 6)
        tokt = pl.op("tensor", lambda e, dt=dt: e.transpose(
            bank(6, 0, 8), wgf[0:8, dt * 128:(dt + 1) * 128],
            ident[0:8, 0:8]),
            sem=s_pe, waits=[(s_sml, v_small)])
        toke = pl.op("vector", lambda e, dt=dt: e.tensor_copy(
            wgtsb[:, dt * E:(dt + 1) * E], bank(6, 0, 8)), sem=s_ve,
            waits=[tokt])
        pl.free_bank(6, toke)
    v_wgt = s_ve.val

    def layernorm(x_tiles, xnT_free):
        SUM, SSQ, MM, RS, NMRS = 0, 16, 32, 48, 64
        if len(x_tiles) != NT:
            x_tiles = [x_tiles[-1] if x_tiles else None] * NT
        for w in xnT_free:
            if w is not None:
                pl.wait("sync", w[0], w[1])
        sq_toks = []
        v_tr = [None] * NT
        v_trpe = [None] * NT
        for g in range(4):
            cs = range(4 * g, 4 * g + 4)
            for c in cs:
                w = [] if len(sq_toks) < 2 else [sq_toks[-2]]
                if x_tiles[c] is not None:
                    w.append(x_tiles[c])
                tok_sq = pl.op("scalar", lambda e, c=c: e.activation(
                    junk2[c % 2][:, :], x_sb[:, c * D:(c + 1) * D], AF.Square,
                    accum_out=stat[:, SSQ + c:SSQ + c + 1]), sem=s_sc, waits=w)
                sq_toks.append(tok_sq)
            o = 4 * g

            def vo(b, waits=()):
                tok = pl.op("vector", b, sem=s_ve, waits=waits)
                pl.wait("vector", s_ve, tok[1])
                return tok
            wsum = [x_tiles[o + 3]] if x_tiles[o + 3] is not None else []
            vo(lambda e, o=o: e.reduce_sum(
                stat[:, SUM + o:SUM + o + 4],
                x_sb[:, o * D:(o + 4) * D].rearrange("p (c j) -> p c j", j=D),
                axis=AX.X), waits=wsum)
            vo(lambda e, o=o: e.tensor_scalar(
                stat[:, MM + o:MM + o + 4], stat[:, SUM + o:SUM + o + 4],
                1.0 / D, None, ALU.mult))
            vo(lambda e, o=o: e.scalar_tensor_tensor(
                stat[:, RS + o:RS + o + 4], stat[:, MM + o:MM + o + 4], -1.0,
                stat[:, MM + o:MM + o + 4], ALU.mult, ALU.mult))
            vo(lambda e, o=o: e.scalar_tensor_tensor(
                stat[:, RS + o:RS + o + 4], stat[:, SSQ + o:SSQ + o + 4], 1.0 / D,
                stat[:, RS + o:RS + o + 4], ALU.mult, ALU.add),
                waits=[sq_toks[o + 3]])
            vo(lambda e, o=o: e.tensor_scalar(
                stat[:, RS + o:RS + o + 4], stat[:, RS + o:RS + o + 4], EPS,
                None, ALU.add))
            tok_rcp = vo(lambda e, o=o: e.reciprocal(
                stat[:, RS + o:RS + o + 4], stat[:, RS + o:RS + o + 4]))
            tok_rs = pl.op("scalar", lambda e, o=o: e.activation(
                stat[:, RS + o:RS + o + 4], stat[:, RS + o:RS + o + 4], AF.Sqrt),
                sem=s_sc, waits=[tok_rcp])
            tok_nmrs = vo(lambda e, o=o: e.scalar_tensor_tensor(
                stat[:, NMRS + o:NMRS + o + 4], stat[:, MM + o:MM + o + 4], -1.0,
                stat[:, RS + o:RS + o + 4], ALU.mult, ALU.mult), waits=[tok_rs])
            for c in cs:
                par = c % 2
                waits = [tok_nmrs]
                if c >= 2:
                    waits.append(v_trpe[c - 2])
                tok_app = pl.op("vector", lambda e, c=c: e.tensor_scalar(
                    xnbuf[c % 2][:, :], x_sb[:, c * D:(c + 1) * D],
                    stat[:, RS + c:RS + c + 1], stat[:, NMRS + c:NMRS + c + 1],
                    ALU.mult, ALU.add), sem=s_ve, waits=waits)
                b0_ = 4 if c % 2 == 0 else 6
                pl.use_bank("tensor", b0_)
                pl.use_bank("tensor", b0_ + 1)
                for dt in range(ND):
                    tok_tp = pl.op("tensor", lambda e, c=c, dt=dt, b0_=b0_: e.transpose(
                        ps[:, b0_ * BK + dt * 128: b0_ * BK + (dt + 1) * 128],
                        xnbuf[c % 2][:, dt * 128:(dt + 1) * 128],
                        ident[:, :]), sem=s_pe,
                        waits=[tok_app] if dt == 0 else [])
                v_trpe[c] = tok_tp
                oap = AP(xnT, c * 128, [[ND * T, 128], [T, ND], [1, 128]])
                tok_ev = pl.op("vector", lambda e, oap=oap, b0_=b0_: e.tensor_copy(
                    oap, ps[:, b0_ * BK: b0_ * BK + 1024]), sem=s_ve, waits=[tok_tp])
                pl.free_bank(b0_, tok_ev)
                pl.free_bank(b0_ + 1, tok_ev)
                v_tr[c] = [tok_ev]
        return v_tr, sq_toks[-1]

    def ffn(tok_xnT, region_toks=(), region_toks2=None, call1=False):
        st_gate = {}

        def emit_gate():
            pl.wait("tensor", s_ve, v_wgt)
            pl.wait("tensor", s_casm, v_smallcast)
            pl.use_bank("tensor", 6)
            pl.use_bank("tensor", 7)
            for c in range(NT):
                pl.wait_toks("tensor", tok_xnT[c])
                for dt in range(ND):
                    pl.op("tensor", lambda e, c=c, dt=dt: e.matmul(
                        bank(6, c * E, E), xnT[:, dt * T + c * 128: dt * T + (c + 1) * 128],
                        wgtsb[:, dt * E:(dt + 1) * E], start=(dt == 0), stop=False), sem=s_pe)
                tok_g = pl.op("tensor", lambda e, c=c: e.matmul(
                    bank(6, c * E, E), ones[0:1, 0:128], bgbf[0:1, :],
                    start=False, stop=True), sem=s_pe)

            G0, G1 = 80, 96
            gv = bank(6, 0, NT * E).rearrange("p (c e) -> p c e", e=E)
            g2 = bank(7, 0, NT * E).rearrange("p (c e) -> p c e", e=E)

            def vo(b, waits=()):
                tok = pl.op("vector", b, sem=s_ve, waits=waits)
                pl.wait("vector", s_ve, tok[1])
                return tok

            vo(lambda e: e.reduce_max(stat[:, G0:G0 + NT], gv, axis=AX.X), waits=[tok_g])
            vo(lambda e: e.tensor_tensor(
                comb[:, :].rearrange("p (c e) -> p c e", e=E), gv,
                bc(stat[:, G0:G0 + NT], E), ALU.is_equal))
            vo(lambda e: e.scalar_tensor_tensor(
                bank(7, 0, NT * E), comb[:, :], -1e30, bank(6, 0, NT * E),
                ALU.mult, ALU.add))
            vo(lambda e: e.reduce_max(stat[:, G1:G1 + NT], g2, axis=AX.X))
            vo(lambda e: e.tensor_tensor(
                bank(6, 0, NT * E).rearrange("p (c e) -> p c e", e=E), g2,
                bc(stat[:, G1:G1 + NT], E), ALU.is_equal))
            tok_d = vo(lambda e: e.tensor_sub(
                stat[:, G1:G1 + NT], stat[:, G1:G1 + NT], stat[:, G0:G0 + NT]))
            tok_exp = pl.op("scalar", lambda e: e.activation(
                stat[:, G1:G1 + NT], stat[:, G1:G1 + NT], AF.Exp), sem=s_sc,
                waits=[tok_d])
            vo(lambda e: e.tensor_scalar(
                stat[:, G0:G0 + NT], stat[:, G1:G1 + NT], 1.0, None, ALU.add),
                waits=[tok_exp])
            vo(lambda e: e.reciprocal(stat[:, G0:G0 + NT], stat[:, G0:G0 + NT]))
            vo(lambda e: e.tensor_scalar(
                stat[:, G0:G0 + NT], stat[:, G0:G0 + NT], 0.5, None, ALU.mult))
            vo(lambda e: e.tensor_scalar(
                stat[:, G1:G1 + NT], stat[:, G0:G0 + NT], -1.0, 0.5, ALU.mult, ALU.add))
            vo(lambda e: e.tensor_tensor(
                comb[:, :].rearrange("p (c e) -> p c e", e=E),
                comb[:, :].rearrange("p (c e) -> p c e", e=E),
                bc(stat[:, G0:G0 + NT], E), ALU.mult))
            vo(lambda e: e.tensor_tensor(
                bank(6, 0, NT * E).rearrange("p (c e) -> p c e", e=E),
                bank(6, 0, NT * E).rearrange("p (c e) -> p c e", e=E),
                bc(stat[:, G1:G1 + NT], E), ALU.mult))
            tok_cb = vo(lambda e: e.tensor_add(comb[:, :], comb[:, :], bank(6, 0, NT * E)))
            pl.free_bank(6, tok_cb)
            pl.free_bank(7, tok_cb)

            pl.wait("tensor", tok_cb[0], tok_cb[1])
            for b in range(4):
                pl.use_bank("tensor", b)
            for c in range(NT):
                tok_t = pl.op("tensor", lambda e, c=c: e.transpose(
                    ps[0:8, c * 128:(c + 1) * 128],
                    comb[:, c * E:(c + 1) * E],
                    ident[:, :]), sem=s_pe)
            tok_cT = pl.op("vector", lambda e: e.tensor_copy(
                combT[0:8, :], ps[0:8, 0:T]), sem=s_ve, waits=[tok_t])
            for b in range(4):
                pl.free_bank(b, tok_cT)

            pl.wait("tensor", tok_cT[0], tok_cT[1])
            for tt in range(NT):
                for dc in range(2):
                    b = 6 + (tt * 2 + dc) % 2
                    pl.use_bank("tensor", b)
                    tok_m = pl.op("tensor", lambda e, tt=tt, dc=dc, b=b: e.matmul(
                        bank(b), combT[0:8, tt * 128:(tt + 1) * 128],
                        b2bf[0:8, dc * 512:(dc + 1) * 512], start=True, stop=True),
                        sem=s_pe)
                    xap = x_sb[:, tt * D + dc * 512: tt * D + dc * 512 + 512]
                    tok_a = pl.op("vector", lambda e, b=b, xap=xap: e.tensor_add(
                        xap, xap, bank(b)), sem=s_pe if False else s_ve, waits=[tok_m])
                    pl.free_bank(b, tok_a)

            st_gate["tok_cT"] = tok_cT

        v_w1 = [None] * E
        v_w2 = [None] * E
        tok_m1_done = [None] * E
        tok_m2_done = [None] * E
        relu_tok = {}
        x_tile_tok = [None] * NT
        for ee in range(E):

            w = [v_wcast1[ee]]
            if ee >= 1:
                w.append(tok_m1_done[ee - 1])
            else:
                w.extend(region_toks)
            for dt in range(ND):
                tok = pl.op("sync", lambda e, ee=ee, dt=dt: e.dma_start(
                    w1t[:, dt * HF:(dt + 1) * HF],
                    AP(W1s, (ee * ND + dt) * HF * 128, [[128, HF], [1, 128]]),
                    transpose=True), sem=s_w1l, dma=True, waits=w)
                w = []
            v_w1[ee] = tok[1]
            w = [v_wcast2[ee]]
            if ee >= 1:
                w.append(tok_m2_done[ee - 1])
            else:
                w.extend(region_toks2 if region_toks2 is not None else region_toks)
            for ht in range(NHT):
                tok = pl.op("sync", lambda e, ee=ee, ht=ht: e.dma_start(
                    w2t[:, ht * D:(ht + 1) * D],
                    AP(W2s, (ee * NHT + ht) * D * 128, [[128, D], [1, 128]]),
                    transpose=True), sem=s_w2l, dma=True, waits=w)
                w = []
            v_w2[ee] = tok[1]

            for tc in range(NTC):
                pl.wait("tensor", s_w1l, v_w1[ee])
                for cc in range(4 * tc, 4 * tc + 4):
                    pl.wait_toks("tensor", tok_xnT[cc])
                for ht in range(NHT):
                    b = ht % 4
                    pl.use_bank("tensor", b)
                    for dt in range(ND):
                        tok_m = pl.op("tensor", lambda e, ht=ht, tc=tc, dt=dt, b=b: e.matmul(
                            bank(b),
                            w1t[:, dt * HF + ht * 128: dt * HF + (ht + 1) * 128],
                            xnT[:, dt * T + tc * TC: dt * T + (tc + 1) * TC],
                            start=(dt == 0), stop=(dt == ND - 1)), sem=s_pe)
                    bcol = b1sb[:, ee * NHT + ht: ee * NHT + ht + 1]
                    oap = hT[:, ht * TC:(ht + 1) * TC]
                    if ht % 2 == 0:
                        tok_r = pl.op("scalar", lambda e, oap=oap, b=b, bcol=bcol:
                                      e.activation(oap, bank(b), AF.Relu, bias=bcol),
                                      sem=s_sc, waits=[tok_m])
                    else:
                        tok_r = pl.op("vector", lambda e, oap=oap, b=b, bcol=bcol:
                                      e.tensor_scalar(oap, bank(b), bcol, 0.0,
                                                      ALU.add, ALU.max),
                                      sem=s_ve, waits=[tok_m])
                    relu_tok[(tc, ht)] = tok_r
                    pl.free_bank(b, tok_r)
                tok_m1_done[ee] = tok_m
                if ee == 0 and tc == 0:
                    emit_gate()
                if call1 and tc == 0:
                    if ee >= 1 and ee + 2 < E:
                        emit_cast(ee + 2, pace_waits=[tok_m])
                    if ee == 2 and cfg.get("attn"):
                        emit_qkv_casts(pace_waits=[tok_m])

                pl.wait("tensor", s_w2l, v_w2[ee])
                for msub in range(TC // 128):
                    tt = tc * (TC // 128) + msub
                    for dc in range(2):
                        b = 4 + (msub * 2 + dc) % 2
                        pl.use_bank("tensor", b)
                        for ht in range(NHT):
                            waits = [relu_tok[(tc, ht)]] if msub == 0 and dc == 0 else []
                            tok_m = pl.op("tensor", lambda e, ht=ht, msub=msub, dc=dc, b=b, tc=tc: e.matmul(
                                bank(b),
                                hT[:, ht * TC + msub * 128: ht * TC + (msub + 1) * 128],
                                w2t[:, ht * D + dc * 512: ht * D + (dc + 1) * 512],
                                start=(ht == 0), stop=(ht == NHT - 1)), sem=s_pe,
                                waits=waits)
                        xap = x_sb[:, tt * D + dc * 512: tt * D + dc * 512 + 512]
                        tok_y = pl.op("vector", lambda e, b=b, xap=xap, col=tt * E + ee:
                                      e.scalar_tensor_tensor(
                                          xap, bank(b), comb[:, col:col + 1], xap,
                                          ALU.mult, ALU.add), sem=s_ve, waits=[tok_m])
                        pl.free_bank(b, tok_y)
                        if ee == E - 1:
                            x_tile_tok[tt] = tok_y
                tok_m2_done[ee] = tok_m

        return dict(x_done=x_tile_tok[NT - 1], x_tiles=x_tile_tok,
                    m1_last=tok_m1_done[E - 1], m2_last=tok_m2_done[E - 1])


    def attn(tok_xnT, tok_ffn1, tok_lnsq):
        SHIFTC = 112

        tok_cs = None
        for i, tgt in ((2, cosu), (3, sinu)):
            tok_cs = pl.op("gpsimd", lambda e, i=i, tgt=tgt: e.dma_start(
                tgt[:, :], AP(coss, i * 128 * T, [[T, 128], [1, T]])),
                sem=s_cs, dma=True,
                waits=[tok_ffn1["m1_last"], (s_caq, v_qkvcast)] if i == 2 else [])
        pl.op("vector", lambda e: e.memset(stat[:, SHIFTC:SHIFTC + 1], EXPSHIFT), sem=s_ve)
        tok_shift = pl.op("vector", lambda e: e.memset(stat[:, 113:114], 0.125), sem=s_ve)

        tok_vones = pl.op("vector", lambda e: e.memset(
            AP(vh, 64, [[NT * 520, 128], [520, NT], [65, 8], [1, 1]]), 1.0),
            sem=s_ve, waits=[tok_ffn1["m2_last"]])

        def v_half(half, extra_waits):
            toks = []
            for grp in range(2):
                tok_l = pl.op("sync", lambda e, half=half, grp=grp: e.dma_start(
                    wsl[:, grp * 2048:(grp + 1) * 2048],
                    AP(qkvs, 2048 + half * 512 + grp * 256,
                       [[3 * D, 128], [128 * 3 * D, ND], [1, 256]]),
                    transpose=False), sem=s_vsl, dma=True,
                    waits=(extra_waits if grp == 0 else []))
            v_ld = tok_l[1]
            last_ev = None
            for tt in range(NT):
                pl.wait_toks("tensor", tok_xnT[tt])
                for grp in range(2):
                    b = 6 + grp
                    pl.use_bank("tensor", b)
                    for dt in range(ND):
                        tok_m = pl.op("tensor", lambda e, tt=tt, grp=grp, dt=dt, b=b: e.matmul(
                            bank(b, 0, 256),
                            xnT[:, dt * T + tt * 128: dt * T + (tt + 1) * 128],
                            wsl[:, grp * 2048 + dt * 256: grp * 2048 + (dt + 1) * 256],
                            start=(dt == 0), stop=(dt == ND - 1)), sem=s_pe,
                            waits=[(s_vsl, v_ld)] if (tt == 0 and dt == 0) else [])
                    oap = AP(vh, tt * 520 + grp * 4 * 65,
                             [[NT * 520, 128], [65, 4], [1, 64]])
                    iap = bank(b, 0, 256).rearrange("p (h c) -> p h c", c=64)
                    last_ev = pl.op("vector", lambda e, oap=oap, iap=iap: e.tensor_copy(
                        oap, iap), sem=s_ve, waits=[tok_m])
                    pl.free_bank(b, last_ev)
            return last_ev

        tok_vA = v_half(0, [tok_ffn1["m2_last"], (s_caq, v_qkvcast), tok_vones, tok_lnsq])

        avq_tok = [[None] * 4, [None] * 4]
        ev_prev = [[None] * 4, [None] * 4]
        rl_pend = [None, None]
        norm_pend = [None, None]
        tok_qk_done = [None] * 9
        tok_rope_done = [None] * 9
        tok_sc_done = [None] * 8
        tok_norm = None
        tok_vB = [None]

        def emit_chunk_qk(hc, extra_w=()):
            par = hc % 2
            w = list(extra_w)
            if hc >= 1:
                w.append(tok_qk_done[hc - 1])
            if hc == 0:
                w.append(tok_vA)
            if hc == 5 and tok_vB[0] is not None:
                w.append(tok_vB[0])
            tok_l = None
            for qd in (0, 2):
                base_c = hc * 128 if qd == 0 else D + hc * 128
                tok_l = pl.op("sync", lambda e, qd=qd, base_c=base_c: e.dma_start(
                    wsl[:, qd * 1024:(qd + 1) * 1024],
                    AP(qkvs, base_c, [[3 * D, 128], [128 * 3 * D, ND], [1, 128]])),
                    sem=s_wsl[par], dma=True, waits=(w if qd == 0 else []))
            for qsrc, qdst in ((0, 1), (2, 3)):
                base_c = hc * 128 if qsrc == 0 else D + hc * 128
                for half in range(2):
                    for sub in range(2):
                        oap = AP(wsl, qdst * 1024 + half * 32 + sub * 64,
                                 [[4 * ND * 128, 128], [128, ND], [1, 32]])
                        iap = AP(qkvs, base_c + sub * 64 + (32 if half == 0 else 0),
                                 [[3 * D, 128], [128 * 3 * D, ND], [1, 32]])
                        tok_l = pl.op("sync", lambda e, oap=oap, iap=iap: e.dma_start(
                            oap, iap), sem=s_wsl[par], dma=True)
            v_wsl = tok_l[1]
            tok_mq = None
            rope_waits = []
            if hc >= 2:
                rope_waits.append(tok_sc_done[hc - 2])
            last_rope = None
            for part_i, (cost, sint, dest) in enumerate((
                    (cosu, sinu, qrope[par]), (cosu, sinu, krope[par]))):
                for qq in range(4):
                    ba, bb = ((6, 7), (0, 1), (2, 3))[(part_i * 4 + qq) % 3]
                    pl.use_bank("tensor", ba)
                    pl.use_bank("tensor", bb)
                    for cc in range(4 * qq, 4 * qq + 4):
                        pl.wait_toks("tensor", tok_xnT[cc])
                    for dt in range(ND):
                        wts = []
                        if part_i == 0 and qq == 0 and dt == 0:
                            wts = [(s_wsl[par], v_wsl), (s_cs, tok_cs[1])]
                            wts.extend(rope_waits)
                        tok_mq = pl.op("tensor", lambda e, qq=qq, dt=dt, pi=part_i, ba=ba: e.matmul(
                            bank(ba),
                            wsl[:, (2 * pi) * 1024 + dt * 128:(2 * pi) * 1024 + (dt + 1) * 128],
                            xnT[:, dt * T + qq * 512: dt * T + (qq + 1) * 512],
                            start=(dt == 0), stop=(dt == ND - 1)), sem=s_pe, waits=wts)
                    for dt in range(ND):
                        tok_mq = pl.op("tensor", lambda e, qq=qq, dt=dt, pi=part_i, bb=bb: e.matmul(
                            bank(bb),
                            wsl[:, (2 * pi + 1) * 1024 + dt * 128:(2 * pi + 1) * 1024 + (dt + 1) * 128],
                            xnT[:, dt * T + qq * 512: dt * T + (qq + 1) * 512],
                            start=(dt == 0), stop=(dt == ND - 1)), sem=s_pe)
                    wts = [tok_mq] + (rope_waits if part_i == 0 and qq == 0 else [])
                    tma, tmb = tmpah[hc], tmpbh[hc]
                    tok_a = pl.op("vector", lambda e, qq=qq, cost=cost, tma=tma, ba=ba: e.tensor_tensor(
                        tma[:, :], bank(ba), cost[:, qq * 512:(qq + 1) * 512],
                        ALU.mult), sem=s_ve, waits=wts + ([(s_ve, last_rope[1])] if last_rope else []))
                    tok_b = pl.op("vector", lambda e, qq=qq, sint=sint, tmb=tmb, bb=bb: e.tensor_tensor(
                        tmb[:, :], bank(bb), sint[:, qq * 512:(qq + 1) * 512],
                        ALU.mult), sem=s_ve)
                    pl.free_bank(ba, tok_a)
                    pl.free_bank(bb, tok_b)
                    pl.wait("vector", s_ve, tok_b[1])
                    last_rope = pl.op("vector", lambda e, qq=qq, dest=dest, tma=tma, tmb=tmb: e.tensor_add(
                        dest[:, qq * 512:(qq + 1) * 512], tma[:, :], tmb[:, :]),
                        sem=s_ve)
            tok_qk_done[hc] = tok_mq
            tok_rope_done[hc] = last_rope

        def scores_chunk(hc):
            nonlocal tok_norm
            par = hc % 2
            def avq(hp, j, g, tok_exp):
                abo = hp * 4096
                bo = (4 if hp == 0 else 6) + (j % 2)
                hh = (2 * hc + hp) % 8
                if g == 0:
                    pl.use_bank("tensor", bo)
                tok = None
                for tki in range(4):
                    tk = 4 * g + tki
                    wts = [tok_exp] if tki == 0 else []
                    if tki == 0 and g == 0 and j == 0 and hp == 0 and hc >= 4:
                        wts.append(tok_vB[0])
                    tok = pl.op("tensor", lambda e, tk=tk, hh=hh, bo=bo, abo=abo, g=g, j=j: e.matmul(
                        bank(bo, 0, 256, 0, 65),
                        vh[:, tk * 520 + hh * 65: tk * 520 + hh * 65 + 65],
                        atAB[:, abo + tk * 256: abo + (tk + 1) * 256],
                        start=(tk == 0), stop=(tk == 15)), sem=s_pe, waits=wts)
                avq_tok[hp][g] = tok
                if g == 3:
                    ro = 512 * hp + 256 * (j % 2)
                    tok_rl = pl.op("vector", lambda e, bo=bo, ro=ro: e.reciprocal(
                        rlbf[0:1, ro:ro + 256], bank(bo, 0, 256, 64, 1)), sem=s_ve,
                        waits=[tok])
                    pl.wait("vector", s_ve, tok_rl[1])
                    rl_pend[hp] = (j, tok_rl)

            def do_norm(hp, j, tok_rl):
                nonlocal tok_norm
                bo = (4 if hp == 0 else 6) + (j % 2)
                ro = 512 * hp + 256 * (j % 2)
                tok_rb = pl.op("tensor", lambda e, bo=bo, ro=ro: e.matmul(
                    bank(bo, 256, 256, 0, 64), ones[0:1, 0:64],
                    rlbf[0:1, ro:ro + 256],
                    start=True, stop=True), sem=s_pe, waits=[tok_rl, tok_shift])
                rbo = 256 * hp
                tok_rbe = pl.op("vector", lambda e, bo=bo, rbo=rbo: e.tensor_copy(
                    rbsb[:, rbo:rbo + 256], bank(bo, 256, 256, 0, 64)), sem=s_ve,
                    waits=[tok_rb])
                pl.wait("vector", s_ve, tok_rbe[1])
                tok_norm = pl.op("vector", lambda e, hp=hp, hc=hc, j=j, bo=bo, rbo=rbo: e.tensor_mul(
                    otall[64 * hp:64 * hp + 64,
                          hc * T + j * 256: hc * T + j * 256 + 256],
                    bank(bo, 0, 256, 0, 64), rbsb[:, rbo:rbo + 256]), sem=s_ve)
                pl.free_bank(bo, tok_norm)

            from collections import deque
            avail = [deque(), deque()]
            slot_now = 0
            for tqc in range(8):
                evs = [[], []]
                for g in range(4):
                    slot_now += 1
                    pl.use_bank("tensor", 0)
                    tok_s = None
                    tok_sh = [None, None]
                    first = True
                    for tki in range(4):
                        tk = 4 * g + tki
                        for hp in range(2):
                            if tki == 2:
                                pl.use_bank("tensor", 1 if hp == 0 else 3)
                            base = (0 if hp == 0 else 2) * BK
                            wts = []
                            if g == 0 and tki == 0 and tqc == 0 and hp == 0:
                                wts = [tok_rope_done[hc]]
                            tok_s = pl.op("tensor", lambda e, tk=tk, tki=tki, base=base, tqc=tqc, hp=hp, par=par: e.matmul(
                                ps[:, base + tki * 256: base + tki * 256 + 256],
                                krope[par][64 * hp:64 * hp + 64, tk * 128:(tk + 1) * 128],
                                qrope[par][64 * hp:64 * hp + 64, tqc * 256:(tqc + 1) * 256],
                                start=True, stop=True), sem=s_pe, waits=wts)
                            tok_sh[tki // 2] = tok_s
                            if first:
                                pl.use_bank("tensor", 2)
                                first = False
                    for hp in range(2):
                        if avail[hp] and slot_now - avail[hp][0][0] >= 2:
                            sp, jq, qq_, te = avail[hp].popleft()
                            avq(hp, jq, qq_, te)
                    ew = [tok_shift]
                    for hp in range(2):
                        if avq_tok[hp][g] is not None:
                            ew.append(avq_tok[hp][g])
                    tok_e = None
                    for half in range(2):
                        srcap = AP(ps, half * 512, [[4096, 128], [1024, 2], [1, 512]])
                        dstap = AP(atAB, g * 1024 + half * 512,
                                   [[32 * 256, 128], [16 * 256, 2], [1, 512]])
                        tok_e = pl.op("scalar", lambda e, srcap=srcap, dstap=dstap: e.activation(
                            dstap, srcap, AF.Exp,
                            bias=stat[:, SHIFTC:SHIFTC + 1], scale=0.125), sem=s_sc,
                            waits=ew + [tok_sh[half]])
                        pl.free_bank(half, tok_e)
                        pl.free_bank(2 + half, tok_e)
                    for hp in range(2):
                        avail[hp].append((slot_now, tqc, g, tok_e))
                    if g == 0 and norm_pend[0] is not None:
                        for hp in range(2):
                            jn, tok_rl = norm_pend[hp]
                            do_norm(hp, jn, tok_rl)
                        norm_pend[0] = norm_pend[1] = None
                    if g == 3:
                        for hp in range(2):
                            if rl_pend[hp] is not None:
                                norm_pend[hp] = rl_pend[hp]
                                rl_pend[hp] = None
            while avail[0] or avail[1]:
                for hp in range(2):
                    if avail[hp]:
                        sp, jq, qq_, te = avail[hp].popleft()
                        avq(hp, jq, qq_, te)
                for hp in range(2):
                    if norm_pend[hp] is not None:
                        jn, tok_rl = norm_pend[hp]
                        do_norm(hp, jn, tok_rl)
                        norm_pend[hp] = None
                    if rl_pend[hp] is not None:
                        norm_pend[hp] = rl_pend[hp]
                        rl_pend[hp] = None
            for hp in range(2):
                if norm_pend[hp] is not None:
                    jn, tok_rl = norm_pend[hp]
                    do_norm(hp, jn, tok_rl)
                    norm_pend[hp] = None
                if rl_pend[hp] is not None:
                    jn, tok_rl = rl_pend[hp]
                    do_norm(hp, jn, tok_rl)
                    rl_pend[hp] = None

        emit_chunk_qk(0)
        for hc in range(8):
            if hc + 1 < 8:
                emit_chunk_qk(hc + 1)
            scores_chunk(hc)
            tok_sc_done[hc] = (s_pe, s_pe.val)
            if hc == 3:
                tok_vB[0] = v_half(1, [tok_qk_done[4]])

        tok_bpj = pl.op("gpsimd", lambda e: e.dma_start(
            bprojbf[:, :], AP(b_proj, 0, [[D, 1], [1, D]])), sem=s_cs, dma=True,
            waits=[tok_sc_done[7]])
        tok_wpj = pl.op("sync", lambda e: e.dma_start(
            wpj[:, :], AP(wprojs, 0, [[D, 128], [128 * D, ND], [1, D]])),
            sem=s_wpj, dma=True,
            waits=[tok_sc_done[7], (s_caq, v_qkvcast)])
        last = None
        x_proj_tok = [None] * NT
        for tt in range(NT):
            for dc in range(2):
                b = (tt * 2 + dc) % 2
                pl.use_bank("tensor", b)
                for ct in range(ND):
                    wts = []
                    if tt == 0 and dc == 0 and ct == 0:
                        wts = [tok_norm, (s_wpj, tok_wpj[1]), tok_bpj]
                    tok_m = pl.op("tensor", lambda e, tt=tt, dc=dc, ct=ct, b=b: e.matmul(
                        bank(b), otall[:, ct * T + tt * 128: ct * T + (tt + 1) * 128],
                        wpj[:, ct * D + dc * 512: ct * D + (dc + 1) * 512],
                        start=(ct == 0), stop=False), sem=s_pe, waits=wts)
                tok_m = pl.op("tensor", lambda e, dc=dc, b=b: e.matmul(
                    bank(b), ones[0:1, 0:128], bprojbf[0:1, dc * 512:(dc + 1) * 512],
                    start=False, stop=True), sem=s_pe)
                xap = x_sb[:, tt * D + dc * 512: tt * D + dc * 512 + 512]
                last = pl.op("vector", lambda e, b=b, xap=xap: e.tensor_add(
                    xap, xap, bank(b)), sem=s_ve, waits=[tok_m])
                pl.free_bank(b, last)
                x_proj_tok[tt] = last
        return dict(x_done=last, x_tiles=x_proj_tok,
                    proj_done=(s_pe, s_pe.val),
                    rope_last=tok_rope_done[7], sc_last=tok_sc_done[7],
                    qk_last=tok_qk_done[7], vB=tok_vB[0])

    tok_xnT, _sq1 = layernorm(tok_x_tiles, [])
    r1 = ffn(tok_xnT, call1=True) if cfg.get("ffn1") else None
    if cfg.get("attn"):
        xr = r1["x_tiles"] if r1 else [tok_x]
        xnfree = [r1["m1_last"]] if r1 else []
        tok_xnT2, sq2 = layernorm(xr, xnfree)
        if r1 is None:
            emit_cast(0)
            emit_cast(1)
            r1 = dict(m1_last=(s_pe, 0), m2_last=(s_pe, 0))
        ra = attn(tok_xnT2, r1, sq2)
        if cfg.get("ffn2"):
            for ee in range(2, E):
                emit_cast(ee)
            tok_xnT3, _sq3 = layernorm(ra["x_tiles"], [ra["qk_last"], ra["vB"]])
            r2 = ffn(tok_xnT3, region_toks=[ra["proj_done"]],
                     region_toks2=[ra["proj_done"]])

    x_fin = r2["x_tiles"] if cfg.get("ffn2") else (
        [None] * NT if not cfg.get("ffn1") else r1["x_tiles"])
    for tt in range(NT):
        w = [x_fin[tt]] if x_fin[tt] is not None else [
            (s_ve, s_ve.val), (s_sc, s_sc.val), (s_pe, s_pe.val)]
        pl.op("gpsimd", lambda e, tt=tt: e.dma_start(
            AP(out, tt * 128 * D, [[D, 128], [1, D]]),
            AP(x_sb, tt * D, [[NT * D, 128], [1, D]])), sem=s_st, dma=True,
            waits=w)
    pl.op("gpsimd", lambda e: e.nop(), waits=[(s_st, s_st.val)])

    pl.finish()
    return nc


def _rope_tables_host():
    try:
        import jax
        import jax.numpy as jnp
        cpu = [d for d in jax.devices("cpu")]
        with jax.default_device(cpu[0]):
            inv = 1.0 / 10000.0 ** (jnp.arange(0, DH, 2, dtype=jnp.float32) / DH)
            t = jnp.arange(T, dtype=jnp.float32)
            freqs = t[:, None] * inv[None, :]
            cos = np.asarray(jnp.cos(freqs), dtype=np.float32)
            sin = np.asarray(jnp.sin(freqs), dtype=np.float32)
            return cos, sin
    except Exception:
        expo = np.arange(0, DH, 2, dtype=np.float32) / np.float32(DH)
        inv = np.float32(1.0) / np.power(np.float32(10000.0), expo)
        t = np.arange(T, dtype=np.float32)
        ang = t[:, None] * inv[None, :]
        return np.cos(ang), np.sin(ang)


def make_consts():
    cos, sin = _rope_tables_host()
    p = np.arange(128)
    cosf = cos[:, p % 32].T
    sinf = sin[:, p % 32].T
    sgn = np.where((p % 64) < 32, -1.0, 1.0).astype(np.float32)[:, None]
    coss = np.stack([cosf / 8.0, (sinf * sgn) / 8.0, cosf, sinf * sgn])
    ident = np.eye(128, dtype=np.float32)
    onesm = np.ones((128, 128), dtype=ml_dtypes.bfloat16)
    return coss.astype(np.float32), ident, onesm


_NC_CACHE = {}


def _get_nc():
    if "nc" not in _NC_CACHE:
        _NC_CACHE["nc"] = build({"ffn1": True, "attn": True, "ffn2": True})
    return _NC_CACHE["nc"]


def kernel(L0_attn_x, L0_attn_w_qkv, L0_attn_w_proj, L0_attn_b_proj,
           L0_ffn_W1, L0_ffn_b1, L0_ffn_W2, L0_ffn_b2, L0_ffn_Wg, L0_ffn_bg):
    from concourse.bass_utils import run_bass_kernel_spmd
    x = np.ascontiguousarray(np.asarray(L0_attn_x, dtype=np.float32))
    coss, ident, onesm = make_consts()
    shared = dict(
        w_qkv=np.ascontiguousarray(np.asarray(L0_attn_w_qkv, dtype=np.float32)),
        w_proj=np.ascontiguousarray(np.asarray(L0_attn_w_proj, dtype=np.float32)),
        b_proj=np.ascontiguousarray(np.asarray(L0_attn_b_proj, dtype=np.float32)),
        W1=np.ascontiguousarray(np.asarray(L0_ffn_W1, dtype=np.float32)),
        b1=np.ascontiguousarray(np.asarray(L0_ffn_b1, dtype=np.float32)),
        W2=np.ascontiguousarray(np.asarray(L0_ffn_W2, dtype=np.float32)),
        b2=np.ascontiguousarray(np.asarray(L0_ffn_b2, dtype=np.float32)),
        Wg=np.ascontiguousarray(np.asarray(L0_ffn_Wg, dtype=np.float32)),
        bg=np.ascontiguousarray(np.asarray(L0_ffn_bg, dtype=np.float32)),
        coss=coss, ident_in=ident, ones_in=onesm,
    )
    n_cores = 8
    in_maps = [dict(shared, x_in=np.ascontiguousarray(x[i])) for i in range(n_cores)]
    nc = _get_nc()
    res = run_bass_kernel_spmd(nc, in_maps, core_ids=list(range(n_cores)))
    return np.stack([np.asarray(r["out"], dtype=np.float32) for r in res.results])


# version 15
# speedup vs baseline: 1.2003x; 1.0397x over previous
import numpy as np
import ml_dtypes
import concourse.bass as bass
import concourse.mybir as mybir

F32 = mybir.dt.float32
BF16 = mybir.dt.bfloat16
AF = mybir.ActivationFunctionType
ALU = mybir.AluOpType
AX = mybir.AxisListType
AP = bass.AP

T, D, HF, E, NH, DH = 2048, 1024, 2048, 8, 16, 64
NT, ND, NHT = T // 128, D // 128, HF // 128
TC = 512
NTC = T // TC
EPS = 1e-5
EXPSHIFT = -20.0
BK = 512


def bc(ap, n):
    return AP(ap.tensor, ap.offset, [list(d) for d in ap.ap] + [[0, n]])


class SemC:
    __slots__ = ("h", "val")

    def __init__(self, h):
        self.h = h
        self.val = 0


class Plan:
    ENGS = ("tensor", "vector", "scalar", "sync", "gpsimd")

    def __init__(self, nc):
        self.nc = nc
        self.ops = {e: [] for e in self.ENGS}
        self.waited = {e: {} for e in self.ENGS}
        self.bank_free = {}

    def sem(self, name):
        return SemC(self.nc.alloc_semaphore(name))

    def wait(self, eng, s, v):
        if s is None or v is None or v <= 0:
            return
        if self.waited[eng].get(s.h.num, -1) >= v:
            return
        self.waited[eng][s.h.num] = v
        self.ops[eng].append(lambda e, h=s.h, v=v: e.wait_ge(h, v))

    def op(self, eng, build, sem=None, waits=(), dma=False):
        for w in waits:
            if w is None:
                continue
            self.wait(eng, w[0], w[1])
        if sem is not None:
            inc = 16 if dma else 1
            sem.val += inc
            self.ops[eng].append(lambda e, b=build, h=sem.h, i=inc: b(e).then_inc(h, i))
            return (sem, sem.val)
        self.ops[eng].append(build)
        return None

    def wait_toks(self, eng, toks):
        if toks is None:
            return
        if isinstance(toks, tuple):
            toks = [toks]
        for t in toks:
            if t is not None:
                self.wait(eng, t[0], t[1])

    def use_bank(self, eng, b):
        w = self.bank_free.get(b)
        if w is not None:
            self.wait(eng, w[0], w[1])

    def free_bank(self, b, tok):
        self.bank_free[b] = tok

    def finish(self):
        with self.nc.allow_low_precision("bf16 attn l-reciprocal (within tolerance)"), \
             self.nc.Block() as block:
            for eng in self.ENGS:
                def runner(e, eng=eng):
                    for f in self.ops[eng]:
                        f(e)
                getattr(block, eng)(runner)


def build(cfg):
    nc = bass.Bass("TRN2", target_bir_lowering=False, debug=False)

    x_in = nc.dram_tensor("x_in", [T, D], F32, kind="ExternalInput")
    w_qkv = nc.dram_tensor("w_qkv", [D, 3 * D], F32, kind="ExternalInput")
    w_proj = nc.dram_tensor("w_proj", [D, D], F32, kind="ExternalInput")
    b_proj = nc.dram_tensor("b_proj", [D], F32, kind="ExternalInput")
    W1 = nc.dram_tensor("W1", [E, HF, D], F32, kind="ExternalInput")
    b1 = nc.dram_tensor("b1", [E, HF], F32, kind="ExternalInput")
    W2 = nc.dram_tensor("W2", [E, D, HF], F32, kind="ExternalInput")
    b2 = nc.dram_tensor("b2", [E, D], F32, kind="ExternalInput")
    Wg = nc.dram_tensor("Wg", [E, D], F32, kind="ExternalInput")
    bg = nc.dram_tensor("bg", [E], F32, kind="ExternalInput")
    coss = nc.dram_tensor("coss", [4, 128, T], F32, kind="ExternalInput")
    ident_in = nc.dram_tensor("ident_in", [128, 128], F32, kind="ExternalInput")
    ones_in = nc.dram_tensor("ones_in", [128, 128], BF16, kind="ExternalInput")
    out = nc.dram_tensor("out", [T, D], F32, kind="ExternalOutput")

    W1s = nc.dram_tensor("W1s", [E, ND, HF, 128], BF16, kind="Internal")
    W2s = nc.dram_tensor("W2s", [E, NHT, D, 128], BF16, kind="Internal")
    qkvs = nc.dram_tensor("qkvs", [D, 3 * D], BF16, kind="Internal")
    wprojs = nc.dram_tensor("wprojs", [D, D], BF16, kind="Internal")

    AR = 212736
    arena = nc.alloc_sbuf_tensor("arena", [128, AR], mybir.dt.uint8)
    abase = nc.lookup_mloc(arena).addr

    def sb(name, shape, dtype, off):
        nbytes = int(np.prod(shape[1:])) * mybir.dt.size(dtype)
        assert off + nbytes <= AR, f"{name}: {off}+{nbytes} > {AR}"
        return nc.alloc_sbuf_tensor_at(name, shape, dtype, offset=abase + off)

    x_sb = sb("x_sb", [128, NT * D], F32, 0)
    M = 65536
    def psb(name, shape, dtype):
        nonlocal M
        t = sb(name, shape, dtype, M)
        nbytes = int(np.prod(shape[1:])) * mybir.dt.size(dtype)
        M += (nbytes + 31) // 32 * 32
        return t
    b1sb = psb("b1sb", [128, E * NHT], F32)
    comb = psb("comb", [128, NT * E], F32)
    b2bf = psb("b2bf", [8, D], BF16)
    bgbf = psb("bgbf", [1, E], BF16)
    wgtsb = psb("wgtsb", [128, ND * E], BF16)
    ident = psb("ident", [128, 128], F32)
    ones = psb("ones", [128, 128], BF16)
    stat = psb("stat", [128, 8 * NT], F32)
    rlbf = psb("rlbf", [1, 1024], BF16)
    rbsb = psb("rbsb", [64, 512], F32)
    assert M <= 75776, M
    xnT = sb("xnT", [128, ND * T], BF16, 75776)
    U = 108544
    w1t = sb("w1t", [128, ND * HF], BF16, U)
    w2t = sb("w2t", [128, NHT * D], BF16, U + 32768)
    hT = sb("hT", [128, NHT * TC], BF16, U + 65536)
    combT = sb("combT", [8, T], BF16, U + 81920)
    tmpah = [sb(f"tmpa{h}", [128, 512], F32, U + 57600 + h * 4096) for h in range(8)]
    tmpbh = [sb(f"tmpb{h}", [128, 512], F32, U + 59648 + h * 4096) for h in range(8)]
    cosu = sb("cosu", [128, T], BF16, U + 0)
    sinu = sb("sinu", [128, T], BF16, U + 4096)
    atAB = sb("atAB", [128, 32 * 256], BF16, U + 8192)
    qrope = [sb(f"qrope{i}", [128, T], BF16, U + 24576 + 8192 * i) for i in range(2)]
    krope = [sb(f"krope{i}", [128, T], BF16, U + 28672 + 8192 * i) for i in range(2)]
    vh = sb("vh", [128, NT * 520], BF16, U + 40960)
    bprojbf = sb("bprojbf", [1, D], BF16, U + 8192)
    rlbf2 = None
    otall = sb("otall", [128, 8 * T], BF16, U + 57600)
    wsl = sb("wsl", [128, 4 * ND * 128], BF16, U + 90368)
    xnbuf = [sb("xnbuf0", [128, D], F32, U + 81920),
             sb("xnbuf1", [128, D], F32, U + 98560)]
    junk2 = [sb("junk0", [128, D], BF16, U + 92160),
             sb("junk1", [128, D], BF16, U + 96256)]
    wpj = sb("wpj", [128, ND * D], BF16, U + 24576)
    wgf = sb("wgf", [8, D], F32, U + 86016)

    ps = nc.alloc_psum_tensor("ps", [128, 8 * BK], F32)

    def bank(b, lo=0, n=BK, p0=0, pn=128):
        return ps[p0:p0 + pn, b * BK + lo: b * BK + lo + n]

    nc.const_aps.register(F32, 0.125, stat[0:128, 113:114])
    pl = Plan(nc)
    s_x4 = [pl.sem(f"s_x{i}") for i in range(4)]
    s_casm = pl.sem("s_casm")
    s_ca1 = [pl.sem(f"s_ca1_{i}") for i in range(E)]
    s_ca2 = [pl.sem(f"s_ca2_{i}") for i in range(E)]
    s_caq = pl.sem("s_caq")
    s_sml = pl.sem("s_sml")
    s_w1l = pl.sem("s_w1l")
    s_w2l = pl.sem("s_w2l")
    s_xp = [[pl.sem(f"s_xp{q}{p}") for p in range(2)] for q in range(2)]
    s_pe = pl.sem("s_pe")
    s_sc = pl.sem("s_sc")
    s_ve = pl.sem("s_ve")
    s_st = pl.sem("s_st")
    s_cs = pl.sem("s_cs")
    s_vsl = pl.sem("s_vsl")
    s_wsl = [pl.sem("s_wsl0"), pl.sem("s_wsl1")]
    s_wpj = pl.sem("s_wpj")

    tok_x4 = []
    for g in range(4):
        tok_x = pl.op("gpsimd", lambda e, g=g: e.dma_start(
            AP(x_sb, g * 4 * D, [[NT * D, 128], [D, 4], [1, D]]),
            AP(x_in, g * 4 * 128 * D, [[D, 128], [128 * D, 4], [1, D]])),
            sem=s_x4[g], dma=True)
        tok_x4.append(tok_x)
    tok_x = tok_x4[-1]
    tok_x_tiles = [tok_x4[c // 4] for c in range(NT)]

    def slow(e, dst, src):
        return e.dma_start(dst, src, allow_slow_non_contiguous=True)
    tok_b1 = pl.op("sync", lambda e: slow(e,
        AP(b1sb, 0, [[E * NHT, 128], [NHT, E], [1, NHT]]),
        AP(b1, 0, [[1, 128], [HF, E], [128, NHT]])), sem=s_sml, dma=True)
    pl.op("sync", lambda e: e.dma_start(ident[:, :], ident_in.ap()), sem=s_sml, dma=True)
    pl.op("sync", lambda e: e.dma_start(ones[:, :], ones_in.ap()), sem=s_sml, dma=True)
    tok_small = pl.op("sync", lambda e: e.dma_start(wgf[:, :], Wg.ap()), sem=s_sml, dma=True)
    v_small = tok_small[1]

    pl.op("gpsimd", lambda e: e.dma_start(b2bf[:, :], b2.ap()), sem=s_casm, dma=True)
    tok_sc2 = pl.op("gpsimd", lambda e: e.dma_start(
        bgbf[:, :], AP(bg, 0, [[E, 1], [1, E]])), sem=s_casm, dma=True)
    v_smallcast = tok_sc2[1]

    v_wcast1 = [None] * E
    v_wcast2 = [None] * E

    def emit_cast(ee, pace_waits=()):
        tok = None
        for dt in range(ND):
            tok = pl.op("gpsimd", lambda e, ee=ee, dt=dt: e.dma_start(
                AP(W1s, (ee * ND + dt) * HF * 128, [[128, HF], [1, 128]]),
                AP(W1, ee * HF * D + dt * 128, [[D, HF], [1, 128]])),
                sem=s_ca1[ee], dma=True,
                waits=(pace_waits if dt == 0 else []))
        v_wcast1[ee] = tok
        for ht in range(NHT):
            tok = pl.op("gpsimd", lambda e, ee=ee, ht=ht: e.dma_start(
                AP(W2s, (ee * NHT + ht) * D * 128, [[128, D], [1, 128]]),
                AP(W2, ee * D * HF + ht * 128, [[HF, D], [1, 128]])),
                sem=s_ca2[ee], dma=True)
        v_wcast2[ee] = tok

    emit_cast(0)
    emit_cast(1)
    emit_cast(2)

    v_qkvcast = None

    def emit_qkv_casts(pace_waits=()):
        nonlocal v_qkvcast
        pl.op("gpsimd", lambda e: e.dma_start(
            AP(qkvs, 0, [[3 * D, D], [1, 3 * D]]),
            AP(w_qkv, 0, [[3 * D, D], [1, 3 * D]])), sem=s_caq, dma=True,
            waits=pace_waits)
        tok = pl.op("gpsimd", lambda e: e.dma_start(
            AP(wprojs, 0, [[D, D], [1, D]]),
            AP(w_proj, 0, [[D, D], [1, D]])), sem=s_caq, dma=True)
        v_qkvcast = tok[1]
    if not cfg.get("ffn1"):
        if cfg.get("attn"):
            emit_qkv_casts()

    for dt in range(ND):
        pl.use_bank("tensor", 6)
        tokt = pl.op("tensor", lambda e, dt=dt: e.transpose(
            bank(6, 0, 8), wgf[0:8, dt * 128:(dt + 1) * 128],
            ident[0:8, 0:8]),
            sem=s_pe, waits=[(s_sml, v_small)])
        toke = pl.op("vector", lambda e, dt=dt: e.tensor_copy(
            wgtsb[:, dt * E:(dt + 1) * E], bank(6, 0, 8)), sem=s_ve,
            waits=[tokt])
        pl.free_bank(6, toke)
    v_wgt = s_ve.val

    def layernorm(x_tiles, xnT_free):
        SUM, SSQ, MM, RS, NMRS = 0, 16, 32, 48, 64
        if len(x_tiles) != NT:
            x_tiles = [x_tiles[-1] if x_tiles else None] * NT
        for w in xnT_free:
            if w is not None:
                pl.wait("sync", w[0], w[1])
        sq_toks = []
        v_tr = [None] * NT
        v_trpe = [None] * NT
        for g in range(4):
            cs = range(4 * g, 4 * g + 4)
            for c in cs:
                w = [] if len(sq_toks) < 2 else [sq_toks[-2]]
                if x_tiles[c] is not None:
                    w.append(x_tiles[c])
                tok_sq = pl.op("scalar", lambda e, c=c: e.activation(
                    junk2[c % 2][:, :], x_sb[:, c * D:(c + 1) * D], AF.Square,
                    accum_out=stat[:, SSQ + c:SSQ + c + 1]), sem=s_sc, waits=w)
                tok_sq = pl.op("scalar", lambda e, c=c: e.activation(
                    junk2[c % 2][:, :], x_sb[:, c * D:(c + 1) * D], AF.Copy,
                    accum_out=stat[:, SUM + c:SUM + c + 1]), sem=s_sc,
                    waits=[tok_sq])
                sq_toks.append(tok_sq)
            o = 4 * g

            def vo(b, waits=()):
                tok = pl.op("vector", b, sem=s_ve, waits=waits)
                pl.wait("vector", s_ve, tok[1])
                return tok
            vo(lambda e, o=o: e.tensor_scalar(
                stat[:, MM + o:MM + o + 4], stat[:, SUM + o:SUM + o + 4],
                1.0 / D, None, ALU.mult), waits=[sq_toks[-1]])
            vo(lambda e, o=o: e.scalar_tensor_tensor(
                stat[:, RS + o:RS + o + 4], stat[:, MM + o:MM + o + 4], -1.0,
                stat[:, MM + o:MM + o + 4], ALU.mult, ALU.mult))
            vo(lambda e, o=o: e.scalar_tensor_tensor(
                stat[:, RS + o:RS + o + 4], stat[:, SSQ + o:SSQ + o + 4], 1.0 / D,
                stat[:, RS + o:RS + o + 4], ALU.mult, ALU.add),
                waits=[sq_toks[o + 3]])
            vo(lambda e, o=o: e.tensor_scalar(
                stat[:, RS + o:RS + o + 4], stat[:, RS + o:RS + o + 4], EPS,
                None, ALU.add))
            tok_rcp = vo(lambda e, o=o: e.reciprocal(
                stat[:, RS + o:RS + o + 4], stat[:, RS + o:RS + o + 4]))
            tok_rs = pl.op("scalar", lambda e, o=o: e.activation(
                stat[:, RS + o:RS + o + 4], stat[:, RS + o:RS + o + 4], AF.Sqrt),
                sem=s_sc, waits=[tok_rcp])
            tok_nmrs = vo(lambda e, o=o: e.scalar_tensor_tensor(
                stat[:, NMRS + o:NMRS + o + 4], stat[:, MM + o:MM + o + 4], -1.0,
                stat[:, RS + o:RS + o + 4], ALU.mult, ALU.mult), waits=[tok_rs])
            for c in cs:
                par = c % 2
                waits = [tok_nmrs]
                if c >= 2:
                    waits.append(v_trpe[c - 2])
                tok_app = pl.op("vector", lambda e, c=c: e.tensor_scalar(
                    xnbuf[c % 2][:, :], x_sb[:, c * D:(c + 1) * D],
                    stat[:, RS + c:RS + c + 1], stat[:, NMRS + c:NMRS + c + 1],
                    ALU.mult, ALU.add), sem=s_ve, waits=waits)
                b0_ = 4 if c % 2 == 0 else 6
                pl.use_bank("tensor", b0_)
                pl.use_bank("tensor", b0_ + 1)
                for dt in range(ND):
                    tok_tp = pl.op("tensor", lambda e, c=c, dt=dt, b0_=b0_: e.transpose(
                        ps[:, b0_ * BK + dt * 128: b0_ * BK + (dt + 1) * 128],
                        xnbuf[c % 2][:, dt * 128:(dt + 1) * 128],
                        ident[:, :]), sem=s_pe,
                        waits=[tok_app] if dt == 0 else [])
                v_trpe[c] = tok_tp
                oap = AP(xnT, c * 128, [[ND * T, 128], [T, ND], [1, 128]])
                if c % 4 == 0:
                    tok_ev = pl.op("scalar", lambda e, oap=oap, b0_=b0_: e.activation(
                        oap, ps[:, b0_ * BK: b0_ * BK + 1024], AF.Copy),
                        sem=s_sc, waits=[tok_tp])
                else:
                    tok_ev = pl.op("vector", lambda e, oap=oap, b0_=b0_: e.tensor_copy(
                        oap, ps[:, b0_ * BK: b0_ * BK + 1024]), sem=s_ve, waits=[tok_tp])
                pl.free_bank(b0_, tok_ev)
                pl.free_bank(b0_ + 1, tok_ev)
                v_tr[c] = [tok_ev]
        return v_tr, sq_toks[-1]

    def ffn(tok_xnT, region_toks=(), region_toks2=None, call1=False):
        st_gate = {}

        def emit_gate():
            pl.wait("tensor", s_ve, v_wgt)
            pl.wait("tensor", s_casm, v_smallcast)
            pl.use_bank("tensor", 6)
            pl.use_bank("tensor", 7)
            for c in range(NT):
                pl.wait_toks("tensor", tok_xnT[c])
                for dt in range(ND):
                    pl.op("tensor", lambda e, c=c, dt=dt: e.matmul(
                        bank(6, c * E, E), xnT[:, dt * T + c * 128: dt * T + (c + 1) * 128],
                        wgtsb[:, dt * E:(dt + 1) * E], start=(dt == 0), stop=False), sem=s_pe)
                tok_g = pl.op("tensor", lambda e, c=c: e.matmul(
                    bank(6, c * E, E), ones[0:1, 0:128], bgbf[0:1, :],
                    start=False, stop=True), sem=s_pe)

            G0, G1 = 80, 96
            gv = bank(6, 0, NT * E).rearrange("p (c e) -> p c e", e=E)
            g2 = bank(7, 0, NT * E).rearrange("p (c e) -> p c e", e=E)

            def vo(b, waits=()):
                tok = pl.op("vector", b, sem=s_ve, waits=waits)
                pl.wait("vector", s_ve, tok[1])
                return tok

            vo(lambda e: e.reduce_max(stat[:, G0:G0 + NT], gv, axis=AX.X), waits=[tok_g])
            vo(lambda e: e.tensor_tensor(
                comb[:, :].rearrange("p (c e) -> p c e", e=E), gv,
                bc(stat[:, G0:G0 + NT], E), ALU.is_equal))
            vo(lambda e: e.scalar_tensor_tensor(
                bank(7, 0, NT * E), comb[:, :], -1e30, bank(6, 0, NT * E),
                ALU.mult, ALU.add))
            vo(lambda e: e.reduce_max(stat[:, G1:G1 + NT], g2, axis=AX.X))
            vo(lambda e: e.tensor_tensor(
                bank(6, 0, NT * E).rearrange("p (c e) -> p c e", e=E), g2,
                bc(stat[:, G1:G1 + NT], E), ALU.is_equal))
            tok_d = vo(lambda e: e.tensor_sub(
                stat[:, G1:G1 + NT], stat[:, G1:G1 + NT], stat[:, G0:G0 + NT]))
            tok_exp = pl.op("scalar", lambda e: e.activation(
                stat[:, G1:G1 + NT], stat[:, G1:G1 + NT], AF.Exp), sem=s_sc,
                waits=[tok_d])
            vo(lambda e: e.tensor_scalar(
                stat[:, G0:G0 + NT], stat[:, G1:G1 + NT], 1.0, None, ALU.add),
                waits=[tok_exp])
            vo(lambda e: e.reciprocal(stat[:, G0:G0 + NT], stat[:, G0:G0 + NT]))
            vo(lambda e: e.tensor_scalar(
                stat[:, G0:G0 + NT], stat[:, G0:G0 + NT], 0.5, None, ALU.mult))
            vo(lambda e: e.tensor_scalar(
                stat[:, G1:G1 + NT], stat[:, G0:G0 + NT], -1.0, 0.5, ALU.mult, ALU.add))
            vo(lambda e: e.tensor_tensor(
                comb[:, :].rearrange("p (c e) -> p c e", e=E),
                comb[:, :].rearrange("p (c e) -> p c e", e=E),
                bc(stat[:, G0:G0 + NT], E), ALU.mult))
            vo(lambda e: e.tensor_tensor(
                bank(6, 0, NT * E).rearrange("p (c e) -> p c e", e=E),
                bank(6, 0, NT * E).rearrange("p (c e) -> p c e", e=E),
                bc(stat[:, G1:G1 + NT], E), ALU.mult))
            tok_cb = vo(lambda e: e.tensor_add(comb[:, :], comb[:, :], bank(6, 0, NT * E)))
            pl.free_bank(6, tok_cb)
            pl.free_bank(7, tok_cb)

            pl.wait("tensor", tok_cb[0], tok_cb[1])
            for b in range(4):
                pl.use_bank("tensor", b)
            for c in range(NT):
                tok_t = pl.op("tensor", lambda e, c=c: e.transpose(
                    ps[0:8, c * 128:(c + 1) * 128],
                    comb[:, c * E:(c + 1) * E],
                    ident[:, :]), sem=s_pe)
            tok_cT = pl.op("vector", lambda e: e.tensor_copy(
                combT[0:8, :], ps[0:8, 0:T]), sem=s_ve, waits=[tok_t])
            for b in range(4):
                pl.free_bank(b, tok_cT)

            pl.wait("tensor", tok_cT[0], tok_cT[1])
            for tt in range(NT):
                for dc in range(2):
                    b = 6 + (tt * 2 + dc) % 2
                    pl.use_bank("tensor", b)
                    tok_m = pl.op("tensor", lambda e, tt=tt, dc=dc, b=b: e.matmul(
                        bank(b), combT[0:8, tt * 128:(tt + 1) * 128],
                        b2bf[0:8, dc * 512:(dc + 1) * 512], start=True, stop=True),
                        sem=s_pe)
                    xap = x_sb[:, tt * D + dc * 512: tt * D + dc * 512 + 512]
                    tok_a = pl.op("vector", lambda e, b=b, xap=xap: e.tensor_add(
                        xap, xap, bank(b)), sem=s_pe if False else s_ve, waits=[tok_m])
                    pl.free_bank(b, tok_a)

            st_gate["tok_cT"] = tok_cT

        v_w1 = [None] * E
        v_w2 = [None] * E
        tok_m1_done = [None] * E
        tok_m2_done = [None] * E
        relu_tok = {}
        x_tile_tok = [None] * NT
        for ee in range(E):

            w = [v_wcast1[ee]]
            if ee >= 1:
                w.append(tok_m1_done[ee - 1])
            else:
                w.extend(region_toks)
            for dt in range(ND):
                tok = pl.op("sync", lambda e, ee=ee, dt=dt: e.dma_start(
                    w1t[:, dt * HF:(dt + 1) * HF],
                    AP(W1s, (ee * ND + dt) * HF * 128, [[128, HF], [1, 128]]),
                    transpose=True), sem=s_w1l, dma=True, waits=w)
                w = []
            v_w1[ee] = tok[1]
            w = [v_wcast2[ee]]
            if ee >= 1:
                w.append(tok_m2_done[ee - 1])
            else:
                w.extend(region_toks2 if region_toks2 is not None else region_toks)
            for ht in range(NHT):
                tok = pl.op("sync", lambda e, ee=ee, ht=ht: e.dma_start(
                    w2t[:, ht * D:(ht + 1) * D],
                    AP(W2s, (ee * NHT + ht) * D * 128, [[128, D], [1, 128]]),
                    transpose=True), sem=s_w2l, dma=True, waits=w)
                w = []
            v_w2[ee] = tok[1]

            for tc in range(NTC):
                pl.wait("tensor", s_w1l, v_w1[ee])
                for cc in range(4 * tc, 4 * tc + 4):
                    pl.wait_toks("tensor", tok_xnT[cc])
                for ht in range(NHT):
                    b = ht % 4
                    pl.use_bank("tensor", b)
                    for dt in range(ND):
                        tok_m = pl.op("tensor", lambda e, ht=ht, tc=tc, dt=dt, b=b: e.matmul(
                            bank(b),
                            w1t[:, dt * HF + ht * 128: dt * HF + (ht + 1) * 128],
                            xnT[:, dt * T + tc * TC: dt * T + (tc + 1) * TC],
                            start=(dt == 0), stop=(dt == ND - 1)), sem=s_pe)
                    bcol = b1sb[:, ee * NHT + ht: ee * NHT + ht + 1]
                    oap = hT[:, ht * TC:(ht + 1) * TC]
                    if ht % 2 == 0:
                        tok_r = pl.op("scalar", lambda e, oap=oap, b=b, bcol=bcol:
                                      e.activation(oap, bank(b), AF.Relu, bias=bcol),
                                      sem=s_sc, waits=[tok_m])
                    else:
                        tok_r = pl.op("vector", lambda e, oap=oap, b=b, bcol=bcol:
                                      e.tensor_scalar(oap, bank(b), bcol, 0.0,
                                                      ALU.add, ALU.max),
                                      sem=s_ve, waits=[tok_m])
                    relu_tok[(tc, ht)] = tok_r
                    pl.free_bank(b, tok_r)
                tok_m1_done[ee] = tok_m
                if ee == 0 and tc == 0:
                    emit_gate()
                if call1 and tc == 0:
                    if ee >= 1 and ee + 2 < E:
                        emit_cast(ee + 2, pace_waits=[tok_m])
                    if ee == 2 and cfg.get("attn"):
                        emit_qkv_casts(pace_waits=[tok_m])

                pl.wait("tensor", s_w2l, v_w2[ee])
                for msub in range(TC // 128):
                    tt = tc * (TC // 128) + msub
                    for dc in range(2):
                        b = 4 + (msub * 2 + dc) % 2
                        pl.use_bank("tensor", b)
                        for ht in range(NHT):
                            waits = [relu_tok[(tc, ht)]] if msub == 0 and dc == 0 else []
                            tok_m = pl.op("tensor", lambda e, ht=ht, msub=msub, dc=dc, b=b, tc=tc: e.matmul(
                                bank(b),
                                hT[:, ht * TC + msub * 128: ht * TC + (msub + 1) * 128],
                                w2t[:, ht * D + dc * 512: ht * D + (dc + 1) * 512],
                                start=(ht == 0), stop=(ht == NHT - 1)), sem=s_pe,
                                waits=waits)
                        xap = x_sb[:, tt * D + dc * 512: tt * D + dc * 512 + 512]
                        tok_y = pl.op("vector", lambda e, b=b, xap=xap, col=tt * E + ee:
                                      e.scalar_tensor_tensor(
                                          xap, bank(b), comb[:, col:col + 1], xap,
                                          ALU.mult, ALU.add), sem=s_ve, waits=[tok_m])
                        pl.free_bank(b, tok_y)
                        if ee == E - 1:
                            x_tile_tok[tt] = tok_y
                tok_m2_done[ee] = tok_m

        return dict(x_done=x_tile_tok[NT - 1], x_tiles=x_tile_tok,
                    m1_last=tok_m1_done[E - 1], m2_last=tok_m2_done[E - 1])


    def attn(tok_xnT, tok_ffn1, tok_lnsq):
        SHIFTC = 112

        tok_cs = None
        for i, tgt in ((2, cosu), (3, sinu)):
            tok_cs = pl.op("gpsimd", lambda e, i=i, tgt=tgt: e.dma_start(
                tgt[:, :], AP(coss, i * 128 * T, [[T, 128], [1, T]])),
                sem=s_cs, dma=True,
                waits=[tok_ffn1["m1_last"], (s_caq, v_qkvcast)] if i == 2 else [])
        pl.op("vector", lambda e: e.memset(stat[:, SHIFTC:SHIFTC + 1], EXPSHIFT), sem=s_ve)
        tok_shift = pl.op("vector", lambda e: e.memset(stat[:, 113:114], 0.125), sem=s_ve)

        tok_vones = pl.op("vector", lambda e: e.memset(
            AP(vh, 64, [[NT * 520, 128], [520, NT], [65, 8], [1, 1]]), 1.0),
            sem=s_ve, waits=[tok_ffn1["m2_last"]])

        def v_half(half, extra_waits):
            toks = []
            for grp in range(2):
                tok_l = pl.op("sync", lambda e, half=half, grp=grp: e.dma_start(
                    wsl[:, grp * 2048:(grp + 1) * 2048],
                    AP(qkvs, 2048 + half * 512 + grp * 256,
                       [[3 * D, 128], [128 * 3 * D, ND], [1, 256]]),
                    transpose=False), sem=s_vsl, dma=True,
                    waits=(extra_waits if grp == 0 else []))
            v_ld = tok_l[1]
            last_ev = None
            for tt in range(NT):
                pl.wait_toks("tensor", tok_xnT[tt])
                for grp in range(2):
                    b = 6 + grp
                    pl.use_bank("tensor", b)
                    for dt in range(ND):
                        tok_m = pl.op("tensor", lambda e, tt=tt, grp=grp, dt=dt, b=b: e.matmul(
                            bank(b, 0, 256),
                            xnT[:, dt * T + tt * 128: dt * T + (tt + 1) * 128],
                            wsl[:, grp * 2048 + dt * 256: grp * 2048 + (dt + 1) * 256],
                            start=(dt == 0), stop=(dt == ND - 1)), sem=s_pe,
                            waits=[(s_vsl, v_ld)] if (tt == 0 and dt == 0) else [])
                    oap = AP(vh, tt * 520 + grp * 4 * 65,
                             [[NT * 520, 128], [65, 4], [1, 64]])
                    iap = bank(b, 0, 256).rearrange("p (h c) -> p h c", c=64)
                    last_ev = pl.op("vector", lambda e, oap=oap, iap=iap: e.tensor_copy(
                        oap, iap), sem=s_ve, waits=[tok_m])
                    pl.free_bank(b, last_ev)
            return last_ev

        tok_vA = v_half(0, [tok_ffn1["m2_last"], (s_caq, v_qkvcast), tok_vones, tok_lnsq])

        avq_tok = [[None] * 4, [None] * 4]
        ev_prev = [[None] * 4, [None] * 4]
        rl_pend = [None, None]
        norm_pend = [None, None]
        tok_qk_done = [None] * 9
        tok_rope_done = [None] * 9
        tok_sc_done = [None] * 8
        tok_norm = None
        tok_vB = [None]

        def emit_chunk_qk(hc, extra_w=()):
            par = hc % 2
            w = list(extra_w)
            if hc >= 1:
                w.append(tok_qk_done[hc - 1])
            if hc == 0:
                w.append(tok_vA)
            if hc == 5 and tok_vB[0] is not None:
                w.append(tok_vB[0])
            tok_l = None
            for qd in (0, 2):
                base_c = hc * 128 if qd == 0 else D + hc * 128
                tok_l = pl.op("sync", lambda e, qd=qd, base_c=base_c: e.dma_start(
                    wsl[:, qd * 1024:(qd + 1) * 1024],
                    AP(qkvs, base_c, [[3 * D, 128], [128 * 3 * D, ND], [1, 128]])),
                    sem=s_wsl[par], dma=True, waits=(w if qd == 0 else []))
            for qsrc, qdst in ((0, 1), (2, 3)):
                base_c = hc * 128 if qsrc == 0 else D + hc * 128
                for half in range(2):
                    for sub in range(2):
                        oap = AP(wsl, qdst * 1024 + half * 32 + sub * 64,
                                 [[4 * ND * 128, 128], [128, ND], [1, 32]])
                        iap = AP(qkvs, base_c + sub * 64 + (32 if half == 0 else 0),
                                 [[3 * D, 128], [128 * 3 * D, ND], [1, 32]])
                        tok_l = pl.op("sync", lambda e, oap=oap, iap=iap: e.dma_start(
                            oap, iap), sem=s_wsl[par], dma=True)
            v_wsl = tok_l[1]
            tok_mq = None
            rope_waits = []
            if hc >= 2:
                rope_waits.append(tok_sc_done[hc - 2])
            last_rope = None
            for part_i, (cost, sint, dest) in enumerate((
                    (cosu, sinu, qrope[par]), (cosu, sinu, krope[par]))):
                for qq in range(4):
                    ba, bb = ((6, 7), (0, 1), (2, 3))[(part_i * 4 + qq) % 3]
                    pl.use_bank("tensor", ba)
                    pl.use_bank("tensor", bb)
                    for cc in range(4 * qq, 4 * qq + 4):
                        pl.wait_toks("tensor", tok_xnT[cc])
                    for dt in range(ND):
                        wts = []
                        if part_i == 0 and qq == 0 and dt == 0:
                            wts = [(s_wsl[par], v_wsl), (s_cs, tok_cs[1])]
                            wts.extend(rope_waits)
                        tok_mq = pl.op("tensor", lambda e, qq=qq, dt=dt, pi=part_i, ba=ba: e.matmul(
                            bank(ba),
                            wsl[:, (2 * pi) * 1024 + dt * 128:(2 * pi) * 1024 + (dt + 1) * 128],
                            xnT[:, dt * T + qq * 512: dt * T + (qq + 1) * 512],
                            start=(dt == 0), stop=(dt == ND - 1)), sem=s_pe, waits=wts)
                    for dt in range(ND):
                        tok_mq = pl.op("tensor", lambda e, qq=qq, dt=dt, pi=part_i, bb=bb: e.matmul(
                            bank(bb),
                            wsl[:, (2 * pi + 1) * 1024 + dt * 128:(2 * pi + 1) * 1024 + (dt + 1) * 128],
                            xnT[:, dt * T + qq * 512: dt * T + (qq + 1) * 512],
                            start=(dt == 0), stop=(dt == ND - 1)), sem=s_pe)
                    wts = [tok_mq] + (rope_waits if part_i == 0 and qq == 0 else [])
                    tma, tmb = tmpah[hc], tmpbh[hc]
                    tok_a = pl.op("vector", lambda e, qq=qq, cost=cost, tma=tma, ba=ba: e.tensor_tensor(
                        tma[:, :], bank(ba), cost[:, qq * 512:(qq + 1) * 512],
                        ALU.mult), sem=s_ve, waits=wts + ([(s_ve, last_rope[1])] if last_rope else []))
                    tok_b = pl.op("vector", lambda e, qq=qq, sint=sint, tmb=tmb, bb=bb: e.tensor_tensor(
                        tmb[:, :], bank(bb), sint[:, qq * 512:(qq + 1) * 512],
                        ALU.mult), sem=s_ve)
                    pl.free_bank(ba, tok_a)
                    pl.free_bank(bb, tok_b)
                    pl.wait("vector", s_ve, tok_b[1])
                    last_rope = pl.op("vector", lambda e, qq=qq, dest=dest, tma=tma, tmb=tmb: e.tensor_add(
                        dest[:, qq * 512:(qq + 1) * 512], tma[:, :], tmb[:, :]),
                        sem=s_ve)
            tok_qk_done[hc] = tok_mq
            tok_rope_done[hc] = last_rope

        def scores_chunk(hc):
            nonlocal tok_norm
            par = hc % 2
            def avq(hp, j, g, tok_exp):
                abo = hp * 4096
                bo = (4 if hp == 0 else 6) + (j % 2)
                hh = (2 * hc + hp) % 8
                if g == 0:
                    pl.use_bank("tensor", bo)
                tok = None
                for tki in range(4):
                    tk = 4 * g + tki
                    wts = [tok_exp] if tki == 0 else []
                    if tki == 0 and g == 0 and j == 0 and hp == 0 and hc >= 4:
                        wts.append(tok_vB[0])
                    tok = pl.op("tensor", lambda e, tk=tk, hh=hh, bo=bo, abo=abo, g=g, j=j: e.matmul(
                        bank(bo, 0, 256, 0, 65),
                        vh[:, tk * 520 + hh * 65: tk * 520 + hh * 65 + 65],
                        atAB[:, abo + tk * 256: abo + (tk + 1) * 256],
                        start=(tk == 0), stop=(tk == 15)), sem=s_pe, waits=wts)
                avq_tok[hp][g] = tok
                if g == 3:
                    ro = 512 * hp + 256 * (j % 2)
                    tok_rl = pl.op("vector", lambda e, bo=bo, ro=ro: e.reciprocal(
                        rlbf[0:1, ro:ro + 256], bank(bo, 0, 256, 64, 1)), sem=s_ve,
                        waits=[tok])
                    pl.wait("vector", s_ve, tok_rl[1])
                    rl_pend[hp] = (j, tok_rl)

            def do_norm(hp, j, tok_rl):
                nonlocal tok_norm
                bo = (4 if hp == 0 else 6) + (j % 2)
                ro = 512 * hp + 256 * (j % 2)
                tok_rb = pl.op("tensor", lambda e, bo=bo, ro=ro: e.matmul(
                    bank(bo, 256, 256, 0, 64), ones[0:1, 0:64],
                    rlbf[0:1, ro:ro + 256],
                    start=True, stop=True), sem=s_pe, waits=[tok_rl, tok_shift])
                rbo = 256 * hp
                tok_rbe = pl.op("vector", lambda e, bo=bo, rbo=rbo: e.tensor_copy(
                    rbsb[:, rbo:rbo + 256], bank(bo, 256, 256, 0, 64)), sem=s_ve,
                    waits=[tok_rb])
                pl.wait("vector", s_ve, tok_rbe[1])
                tok_norm = pl.op("vector", lambda e, hp=hp, hc=hc, j=j, bo=bo, rbo=rbo: e.tensor_mul(
                    otall[64 * hp:64 * hp + 64,
                          hc * T + j * 256: hc * T + j * 256 + 256],
                    bank(bo, 0, 256, 0, 64), rbsb[:, rbo:rbo + 256]), sem=s_ve)
                pl.free_bank(bo, tok_norm)

            from collections import deque
            avail = [deque(), deque()]
            slot_now = 0
            for tqc in range(8):
                evs = [[], []]
                for g in range(4):
                    slot_now += 1
                    pl.use_bank("tensor", 0)
                    tok_s = None
                    tok_sh = [None, None]
                    first = True
                    for tki in range(4):
                        tk = 4 * g + tki
                        for hp in range(2):
                            if tki == 2:
                                pl.use_bank("tensor", 1 if hp == 0 else 3)
                            base = (0 if hp == 0 else 2) * BK
                            wts = []
                            if g == 0 and tki == 0 and tqc == 0 and hp == 0:
                                wts = [tok_rope_done[hc]]
                            tok_s = pl.op("tensor", lambda e, tk=tk, tki=tki, base=base, tqc=tqc, hp=hp, par=par: e.matmul(
                                ps[:, base + tki * 256: base + tki * 256 + 256],
                                krope[par][64 * hp:64 * hp + 64, tk * 128:(tk + 1) * 128],
                                qrope[par][64 * hp:64 * hp + 64, tqc * 256:(tqc + 1) * 256],
                                start=True, stop=True), sem=s_pe, waits=wts)
                            tok_sh[tki // 2] = tok_s
                            if first:
                                pl.use_bank("tensor", 2)
                                first = False
                    for hp in range(2):
                        if avail[hp] and slot_now - avail[hp][0][0] >= 2:
                            sp, jq, qq_, te = avail[hp].popleft()
                            avq(hp, jq, qq_, te)
                    ew = [tok_shift]
                    for hp in range(2):
                        if avq_tok[hp][g] is not None:
                            ew.append(avq_tok[hp][g])
                    tok_e = None
                    for half in range(2):
                        srcap = AP(ps, half * 512, [[4096, 128], [1024, 2], [1, 512]])
                        dstap = AP(atAB, g * 1024 + half * 512,
                                   [[32 * 256, 128], [16 * 256, 2], [1, 512]])
                        tok_e = pl.op("scalar", lambda e, srcap=srcap, dstap=dstap: e.activation(
                            dstap, srcap, AF.Exp,
                            bias=stat[:, SHIFTC:SHIFTC + 1], scale=0.125), sem=s_sc,
                            waits=ew + [tok_sh[half]])
                        pl.free_bank(half, tok_e)
                        pl.free_bank(2 + half, tok_e)
                    for hp in range(2):
                        avail[hp].append((slot_now, tqc, g, tok_e))
                    if g == 0 and norm_pend[0] is not None:
                        for hp in range(2):
                            jn, tok_rl = norm_pend[hp]
                            do_norm(hp, jn, tok_rl)
                        norm_pend[0] = norm_pend[1] = None
                    if g == 3:
                        for hp in range(2):
                            if rl_pend[hp] is not None:
                                norm_pend[hp] = rl_pend[hp]
                                rl_pend[hp] = None
            while avail[0] or avail[1]:
                for hp in range(2):
                    if avail[hp]:
                        sp, jq, qq_, te = avail[hp].popleft()
                        avq(hp, jq, qq_, te)
                for hp in range(2):
                    if norm_pend[hp] is not None:
                        jn, tok_rl = norm_pend[hp]
                        do_norm(hp, jn, tok_rl)
                        norm_pend[hp] = None
                    if rl_pend[hp] is not None:
                        norm_pend[hp] = rl_pend[hp]
                        rl_pend[hp] = None
            for hp in range(2):
                if norm_pend[hp] is not None:
                    jn, tok_rl = norm_pend[hp]
                    do_norm(hp, jn, tok_rl)
                    norm_pend[hp] = None
                if rl_pend[hp] is not None:
                    jn, tok_rl = rl_pend[hp]
                    do_norm(hp, jn, tok_rl)
                    rl_pend[hp] = None

        emit_chunk_qk(0)
        for hc in range(8):
            if hc + 1 < 8:
                emit_chunk_qk(hc + 1)
            scores_chunk(hc)
            tok_sc_done[hc] = (s_pe, s_pe.val)
            if hc == 3:
                tok_vB[0] = v_half(1, [tok_qk_done[4]])

        tok_bpj = pl.op("gpsimd", lambda e: e.dma_start(
            bprojbf[:, :], AP(b_proj, 0, [[D, 1], [1, D]])), sem=s_cs, dma=True,
            waits=[tok_sc_done[7]])
        tok_wpj = pl.op("sync", lambda e: e.dma_start(
            wpj[:, :], AP(wprojs, 0, [[D, 128], [128 * D, ND], [1, D]])),
            sem=s_wpj, dma=True,
            waits=[tok_sc_done[7], (s_caq, v_qkvcast)])
        last = None
        x_proj_tok = [None] * NT
        for tt in range(NT):
            for dc in range(2):
                b = (tt * 2 + dc) % 2
                pl.use_bank("tensor", b)
                for ct in range(ND):
                    wts = []
                    if tt == 0 and dc == 0 and ct == 0:
                        wts = [tok_norm, (s_wpj, tok_wpj[1]), tok_bpj]
                    tok_m = pl.op("tensor", lambda e, tt=tt, dc=dc, ct=ct, b=b: e.matmul(
                        bank(b), otall[:, ct * T + tt * 128: ct * T + (tt + 1) * 128],
                        wpj[:, ct * D + dc * 512: ct * D + (dc + 1) * 512],
                        start=(ct == 0), stop=False), sem=s_pe, waits=wts)
                tok_m = pl.op("tensor", lambda e, dc=dc, b=b: e.matmul(
                    bank(b), ones[0:1, 0:128], bprojbf[0:1, dc * 512:(dc + 1) * 512],
                    start=False, stop=True), sem=s_pe)
                xap = x_sb[:, tt * D + dc * 512: tt * D + dc * 512 + 512]
                last = pl.op("vector", lambda e, b=b, xap=xap: e.tensor_add(
                    xap, xap, bank(b)), sem=s_ve, waits=[tok_m])
                pl.free_bank(b, last)
                x_proj_tok[tt] = last
        return dict(x_done=last, x_tiles=x_proj_tok,
                    proj_done=(s_pe, s_pe.val),
                    rope_last=tok_rope_done[7], sc_last=tok_sc_done[7],
                    qk_last=tok_qk_done[7], vB=tok_vB[0])

    tok_xnT, _sq1 = layernorm(tok_x_tiles, [])
    r1 = ffn(tok_xnT, call1=True) if cfg.get("ffn1") else None
    if cfg.get("attn"):
        xr = r1["x_tiles"] if r1 else [tok_x]
        xnfree = [r1["m1_last"]] if r1 else []
        tok_xnT2, sq2 = layernorm(xr, xnfree)
        if r1 is None:
            emit_cast(0)
            emit_cast(1)
            r1 = dict(m1_last=(s_pe, 0), m2_last=(s_pe, 0))
        ra = attn(tok_xnT2, r1, sq2)
        if cfg.get("ffn2"):
            for ee in range(2, E):
                emit_cast(ee)
            tok_xnT3, _sq3 = layernorm(ra["x_tiles"], [ra["qk_last"], ra["vB"]])
            r2 = ffn(tok_xnT3, region_toks=[ra["proj_done"]],
                     region_toks2=[ra["proj_done"]])

    x_fin = r2["x_tiles"] if cfg.get("ffn2") else (
        [None] * NT if not cfg.get("ffn1") else r1["x_tiles"])
    for tt in range(NT):
        w = [x_fin[tt]] if x_fin[tt] is not None else [
            (s_ve, s_ve.val), (s_sc, s_sc.val), (s_pe, s_pe.val)]
        pl.op("gpsimd", lambda e, tt=tt: e.dma_start(
            AP(out, tt * 128 * D, [[D, 128], [1, D]]),
            AP(x_sb, tt * D, [[NT * D, 128], [1, D]])), sem=s_st, dma=True,
            waits=w)
    pl.op("gpsimd", lambda e: e.nop(), waits=[(s_st, s_st.val)])

    pl.finish()
    return nc


def _rope_tables_host():
    try:
        import jax
        import jax.numpy as jnp
        cpu = [d for d in jax.devices("cpu")]
        with jax.default_device(cpu[0]):
            inv = 1.0 / 10000.0 ** (jnp.arange(0, DH, 2, dtype=jnp.float32) / DH)
            t = jnp.arange(T, dtype=jnp.float32)
            freqs = t[:, None] * inv[None, :]
            cos = np.asarray(jnp.cos(freqs), dtype=np.float32)
            sin = np.asarray(jnp.sin(freqs), dtype=np.float32)
            return cos, sin
    except Exception:
        expo = np.arange(0, DH, 2, dtype=np.float32) / np.float32(DH)
        inv = np.float32(1.0) / np.power(np.float32(10000.0), expo)
        t = np.arange(T, dtype=np.float32)
        ang = t[:, None] * inv[None, :]
        return np.cos(ang), np.sin(ang)


def make_consts():
    cos, sin = _rope_tables_host()
    p = np.arange(128)
    cosf = cos[:, p % 32].T
    sinf = sin[:, p % 32].T
    sgn = np.where((p % 64) < 32, -1.0, 1.0).astype(np.float32)[:, None]
    coss = np.stack([cosf / 8.0, (sinf * sgn) / 8.0, cosf, sinf * sgn])
    ident = np.eye(128, dtype=np.float32)
    onesm = np.ones((128, 128), dtype=ml_dtypes.bfloat16)
    return coss.astype(np.float32), ident, onesm


_NC_CACHE = {}


def _get_nc():
    if "nc" not in _NC_CACHE:
        _NC_CACHE["nc"] = build({"ffn1": True, "attn": True, "ffn2": True})
    return _NC_CACHE["nc"]


def kernel(L0_attn_x, L0_attn_w_qkv, L0_attn_w_proj, L0_attn_b_proj,
           L0_ffn_W1, L0_ffn_b1, L0_ffn_W2, L0_ffn_b2, L0_ffn_Wg, L0_ffn_bg):
    from concourse.bass_utils import run_bass_kernel_spmd
    x = np.ascontiguousarray(np.asarray(L0_attn_x, dtype=np.float32))
    coss, ident, onesm = make_consts()
    shared = dict(
        w_qkv=np.ascontiguousarray(np.asarray(L0_attn_w_qkv, dtype=np.float32)),
        w_proj=np.ascontiguousarray(np.asarray(L0_attn_w_proj, dtype=np.float32)),
        b_proj=np.ascontiguousarray(np.asarray(L0_attn_b_proj, dtype=np.float32)),
        W1=np.ascontiguousarray(np.asarray(L0_ffn_W1, dtype=np.float32)),
        b1=np.ascontiguousarray(np.asarray(L0_ffn_b1, dtype=np.float32)),
        W2=np.ascontiguousarray(np.asarray(L0_ffn_W2, dtype=np.float32)),
        b2=np.ascontiguousarray(np.asarray(L0_ffn_b2, dtype=np.float32)),
        Wg=np.ascontiguousarray(np.asarray(L0_ffn_Wg, dtype=np.float32)),
        bg=np.ascontiguousarray(np.asarray(L0_ffn_bg, dtype=np.float32)),
        coss=coss, ident_in=ident, ones_in=onesm,
    )
    n_cores = 8
    in_maps = [dict(shared, x_in=np.ascontiguousarray(x[i])) for i in range(n_cores)]
    nc = _get_nc()
    res = run_bass_kernel_spmd(nc, in_maps, core_ids=list(range(n_cores)))
    return np.stack([np.asarray(r["out"], dtype=np.float32) for r in res.results])
